# version 10
# baseline (speedup 1.0000x reference)
import math
from contextlib import ExitStack

import numpy as np
import ml_dtypes

import concourse.bass as bass
import concourse.mybir as mybir
from concourse.bass_utils import run_bass_kernel_spmd

F32 = mybir.dt.float32
BF16 = mybir.dt.bfloat16
AF = mybir.ActivationFunctionType
ALU = mybir.AluOpType
AX = mybir.AxisListType

T = 4096
D = 1024
NCORE = 8
I_FF0 = 2816
I_FF1 = 4096
NH = 16
HD = 64
WIN = 512
CH = 128
NCH = T // CH
LN_EPS = 1e-5
GN_EPS = 1e-5
NEG = -1.0e30
BF = ml_dtypes.bfloat16

ACT_CONSTS = [0.0, 1e-5, -4.0, -6.0, 1.0, -1.0, 0.5]

OG_SIZES = [512, 512, 512, 512, 512, 256]
OG_OFF = [0, 512, 1024, 1536, 2048, 2560]
N_IC0 = I_FF0 // 128


def gammas_np():
    lo = math.log(1.0 - 0.999)
    hi = math.log(1.0 - 0.9)
    return (1.0 - np.exp(np.linspace(lo, hi, NH, dtype=np.float32))).astype(np.float64)


def host_consts():
    g = gammas_np()
    c = {}
    c["c_ident"] = np.eye(128, dtype=BF)
    idx = np.arange(CH)
    dist = idx[None, :] - idx[:, None]
    dt = np.where(dist >= 0, np.power(g[:, None, None], -(idx[:, None] + 1.0)[None]), 0.0)
    c["c_dt"] = dt.astype(np.float32)
    lam = np.zeros((8, 128, CH))
    mu = np.zeros((8, CH, 128))
    glc = np.zeros((128, 8))
    for oc in range(8):
        for p in range(128):
            lam[oc, p, :] = np.power(g[2 * oc + p // 64], idx + 1)
            glc[p, oc] = np.power(g[2 * oc + p // 64], CH)
        for fcol in range(128):
            mu[oc, :, fcol] = np.power(g[2 * oc + fcol // 64], CH - 1 - idx)
    c["c_lam"] = lam.astype(np.float32)
    c["c_mu"] = mu.astype(np.float32)
    c["c_glc"] = glc.astype(np.float32)
    qi = np.arange(128)[:, None]
    kj = np.arange(640)[None, :]
    c["c_wmask"] = np.where((kj > qi) & (kj <= qi + WIN), 0.0, NEG).astype(np.float32)
    jj = np.arange(128)[None, :]
    c["c_mdiag"] = np.where(jj <= qi, 0.0, NEG).astype(np.float32)
    c["c_mold"] = np.where(jj > qi, 0.0, NEG).astype(np.float32)
    c["c_ones1"] = np.ones((1, 128), dtype=np.float32)
    c["c_b2"] = np.stack([(np.arange(128) < 64), (np.arange(128) >= 64)]).astype(np.float32)
    c["c_ones2"] = np.ascontiguousarray(c["c_b2"].T)
    return c


def flat_og_pmajor(w):
    O, F = w.shape
    wt = w.T.reshape(F // 128, 128, O)
    blocks = []
    for off, sz in zip(OG_OFF, OG_SIZES):
        blocks.append(np.ascontiguousarray(wt[:, :, off:off + sz].transpose(1, 0, 2)).ravel())
    return np.concatenate(blocks)


def flat_pmajor_icwide(w):
    O, F = w.shape
    wt = w.T.reshape(F // 128, 128, O).transpose(1, 0, 2)
    return np.ascontiguousarray(wt).ravel()


def tiles_oc_fc(w):
    O, F = w.shape
    wt = w.T.reshape(F // 128, 128, O // 128, 128)
    return np.ascontiguousarray(wt.transpose(2, 0, 1, 3)).ravel()


def flat_pmajor_fcwide(w):
    O, F = w.shape
    return np.ascontiguousarray(w.T.reshape(F // 128, 128, O)).ravel()


def flat_oc_pmajor_wide(w):
    return flat_pmajor_fcwide(w)


def tiles_lhsT_slabbed(w, ocps):
    O, F = w.shape
    arr = w.T.reshape(F // 128, 128, O // 128, 128).transpose(2, 0, 1, 3)
    arr = arr.reshape(O // 128 // ocps, ocps, F // 128, 128, 128).transpose(0, 3, 1, 2, 4)
    return np.ascontiguousarray(arr).ravel()


class Sem:
    def __init__(self, nc, es, name):
        self.h = es.enter_context(nc.semaphore(name))
        self.n = 0

    def inc(self, instr, k=1):
        instr.then_inc(self.h, k)
        self.n += k
        return self.n


class Eng:

    def __init__(self, raw, sem):
        self._raw = raw
        self._sem = sem

    def __getattr__(self, name):
        f = getattr(self._raw, name)

        def call(*a, **k):
            self._raw.wait_ge(self._sem.h, self._sem.n)
            ins = f(*a, **k)
            self._sem.inc(ins)
            return ins

        return call

    def wait_ge(self, h, n):
        self._raw.wait_ge(h, n)


class KB:
    def __init__(self):
        self.nc = bass.Bass("TRN2", target_bir_lowering=False)
        self.es = ExitStack()
        for name in ["sx", "sxb", "sw", "swb", "sw2", "sst", "sstb", "spe", "sdv", "sac", "spo", "scv", "swg"]:
            setattr(self, name, Sem(self.nc, self.es, name))
        self.phase_no = 0
        self._uid = 0
        self.v = Eng(self.nc.vector, self.sdv)
        self.s = Eng(self.nc.scalar, self.sac)
        self.g = Eng(self.nc.gpsimd, self.spo)

    def sb(self, es, name, shape, dtype):
        return es.enter_context(self.nc.sbuf_tensor(name, shape, dtype))

    def psum(self, es, name, shape, dtype):
        return es.enter_context(self.nc.psum_tensor(name, shape, dtype))

    def uid(self):
        self._uid += 1
        return str(self._uid)

    def barrier(self):
        nc = self.nc
        self.sdv.inc(nc.vector.memset(self._barr[:, 0:1], 0.0))
        self.sac.inc(nc.scalar.memzero(self._barr[:, 1:2]))
        nc.tensor.wait_ge(self.spe.h, self.spe.n)
        self.spe.inc(nc.tensor.nop())
        self.spo.inc(nc.gpsimd.memset(self._barr[:, 2:3], 0.0))
        nc.sync.wait_ge(self.sdv.h, self.sdv.n)
        nc.sync.wait_ge(self.sac.h, self.sac.n)
        nc.sync.wait_ge(self.spe.h, self.spe.n)
        nc.sync.wait_ge(self.spo.h, self.spo.n)
        nc.sync.wait_ge(self.sst.h, self.sst.n)
        nc.sync.wait_ge(self.sstb.h, self.sstb.n)
        nc.sync.wait_ge(self.sx.h, self.sx.n)
        nc.sync.wait_ge(self.sw.h, self.sw.n)
        nc.sync.wait_ge(self.sw2.h, self.sw2.n)
        nc.sync.wait_ge(self.swg.h, self.swg.n)
        self.phase_no += 1
        self.scv.inc(self.nc.sync.nop())
        for eng in [nc.vector, nc.scalar, nc.tensor, nc.gpsimd]:
            eng.wait_ge(self.scv.h, self.scv.n)


class QuantScratch:
    def __init__(self, kb, es, name, F):
        self.F = F
        self.bm = kb.sb(es, name + "_bm", [128, F // 16], F32)
        self.r = kb.sb(es, name + "_r", [128, F // 16], F32)
        self.xn = kb.sb(es, name + "_xn", [128, F], F32)
        self.absn = kb.sb(es, name + "_absn", [128, F], F32)
        self.sg = kb.sb(es, name + "_sg", [128, F], BF16)
        self.u = kb.sb(es, name + "_u", [128, F], BF16)
        self.r1 = kb.sb(es, name + "_r1", [128, F], BF16)
        self.r2 = kb.sb(es, name + "_r2", [128, F], BF16)


def emit_quant(kb, q, in3, out3, nj, sz, eng="v"):
    v, s = kb.v, kb.s
    e = kb.v if eng == "v" else kb.g
    esem = kb.sdv if eng == "v" else kb.spo
    F = nj * sz
    nb = F // 16
    r4 = lambda ap: ap.rearrange("p j (g k) -> p j g k", k=16)
    bm3 = q.bm[:, 0:nb].rearrange("p (j g) -> p j g", j=nj)
    fl = lambda buf: buf[:, 0:F]
    v.tensor_reduce(out=q.bm[:, 0:nb], in_=r4(in3), axis=AX.X, op=ALU.max,
                    apply_absolute_value=True)
    v.tensor_scalar(out=q.bm[:, 0:nb], in0=q.bm[:, 0:nb], scalar1=1.0 / 6.0, scalar2=1e-12,
                    op0=ALU.mult, op1=ALU.max)
    v.reciprocal(out=q.r[:, 0:nb], in_=q.bm[:, 0:nb])
    if eng != "v":
        e.wait_ge(kb.sdv.h, kb.sdv.n)
    e.tensor_tensor(out=r4(q.xn[:, 0:F].rearrange("p (j z) -> p j z", j=nj)), in0=r4(in3),
                    in1=q.r[:, 0:nb].rearrange("p (j g) -> p j g", j=nj).unsqueeze(3)
                    .broadcast_to([128, nj, sz // 16, 16]),
                    op=ALU.mult)
    s.wait_ge(esem.h, esem.n)
    s.activation(out=fl(q.absn), in_=fl(q.xn), func=AF.Abs)
    s.activation(out=fl(q.sg), in_=fl(q.xn), func=AF.Sign)
    e.wait_ge(kb.sac.h, kb.sac.n)
    if eng == "v":
        e.tensor_scalar(out=fl(q.u), in0=fl(q.absn), scalar1=0.25, scalar2=None, op0=ALU.is_gt)
        for th in (0.75, 1.25, 1.75, 2.5, 3.5, 5.0):
            e.scalar_tensor_tensor(out=fl(q.u), in0=fl(q.absn), scalar=th, in1=fl(q.u),
                                   op0=ALU.is_gt, op1=ALU.add)
    else:
        e.tensor_scalar(out=fl(q.u), in0=fl(q.absn), scalar1=0.25, scalar2=None, op0=ALU.is_gt)
        for th in (0.75, 1.25, 1.75, 2.5, 3.5, 5.0):
            e.tensor_scalar(out=fl(q.r1), in0=fl(q.absn), scalar1=th, scalar2=None, op0=ALU.is_gt)
            e.tensor_tensor(out=fl(q.u), in0=fl(q.u), in1=fl(q.r1), op=ALU.add)
    s.wait_ge(esem.h, esem.n)
    s.activation(out=fl(q.r1), in_=fl(q.u), func=AF.Relu, bias=-4.0, scale=1.0)
    s.activation(out=fl(q.r2), in_=fl(q.u), func=AF.Relu, bias=-6.0, scale=1.0)
    e.wait_ge(kb.sac.h, kb.sac.n)
    if eng == "v":
        e.scalar_tensor_tensor(out=fl(q.u), in0=fl(q.u), scalar=0.5, in1=fl(q.r2),
                               op0=ALU.mult, op1=ALU.add)
        e.scalar_tensor_tensor(out=fl(q.u), in0=fl(q.r1), scalar=0.5, in1=fl(q.u),
                               op0=ALU.mult, op1=ALU.add)
    else:
        e.tensor_scalar(out=fl(q.u), in0=fl(q.u), scalar1=0.5, scalar2=None, op0=ALU.mult)
        e.tensor_scalar(out=fl(q.r1), in0=fl(q.r1), scalar1=0.5, scalar2=None, op0=ALU.mult)
        e.tensor_tensor(out=fl(q.u), in0=fl(q.u), in1=fl(q.r2), op=ALU.add)
        e.tensor_tensor(out=fl(q.u), in0=fl(q.u), in1=fl(q.r1), op=ALU.add)
    e.tensor_tensor(out=fl(q.u), in0=fl(q.u), in1=fl(q.sg), op=ALU.mult)
    e.tensor_tensor(out=r4(out3), in0=r4(q.u[:, 0:F].rearrange("p (j z) -> p j z", j=nj)),
                    in1=q.bm[:, 0:nb].rearrange("p (j g) -> p j g", j=nj).unsqueeze(3)
                    .broadcast_to([128, nj, sz // 16, 16]),
                    op=ALU.mult)
    return esem.n


def emit_quant_front(kb, q, in3, nj, sz):
    v, s = kb.v, kb.s
    F = nj * sz
    nb = F // 16
    r4 = lambda ap: ap.rearrange("p j (g k) -> p j g k", k=16)
    v.tensor_reduce(out=q.bm[:, 0:nb], in_=r4(in3), axis=AX.X, op=ALU.max,
                    apply_absolute_value=True)
    v.tensor_scalar(out=q.bm[:, 0:nb], in0=q.bm[:, 0:nb], scalar1=1.0 / 6.0, scalar2=1e-12,
                    op0=ALU.mult, op1=ALU.max)
    v.reciprocal(out=q.r[:, 0:nb], in_=q.bm[:, 0:nb])
    v.tensor_tensor(out=r4(q.xn[:, 0:F].rearrange("p (j z) -> p j z", j=nj)), in0=r4(in3),
                    in1=q.r[:, 0:nb].rearrange("p (j g) -> p j g", j=nj).unsqueeze(3)
                    .broadcast_to([128, nj, sz // 16, 16]),
                    op=ALU.mult)
    s.wait_ge(kb.sdv.h, kb.sdv.n)
    s.activation(out=q.absn[:, 0:F], in_=q.xn[:, 0:F], func=AF.Abs)
    s.activation(out=q.sg[:, 0:F], in_=q.xn[:, 0:F], func=AF.Sign)
    q.front_sac = kb.sac.n


def emit_quant_back(kb, q, out3, nj, sz):
    v = kb.v
    F = nj * sz
    nb = F // 16
    r4 = lambda ap: ap.rearrange("p j (g k) -> p j g k", k=16)
    fl = lambda buf: buf[:, 0:F]
    v.wait_ge(kb.sac.h, q.front_sac)
    v.tensor_scalar(out=fl(q.u), in0=fl(q.absn), scalar1=0.25, scalar2=None, op0=ALU.is_gt)
    for th in (0.75, 1.25, 1.75, 2.5, 3.5, 5.0):
        v.scalar_tensor_tensor(out=fl(q.u), in0=fl(q.absn), scalar=th, in1=fl(q.u),
                               op0=ALU.is_gt, op1=ALU.add)
    v.tensor_scalar(out=fl(q.r1), in0=fl(q.u), scalar1=-4.0, scalar2=0.0,
                    op0=ALU.add, op1=ALU.max)
    v.tensor_scalar(out=fl(q.r2), in0=fl(q.u), scalar1=-6.0, scalar2=0.0,
                    op0=ALU.add, op1=ALU.max)
    v.scalar_tensor_tensor(out=fl(q.u), in0=fl(q.u), scalar=0.5, in1=fl(q.r2),
                           op0=ALU.mult, op1=ALU.add)
    v.scalar_tensor_tensor(out=fl(q.u), in0=fl(q.r1), scalar=0.5, in1=fl(q.u),
                           op0=ALU.mult, op1=ALU.add)
    v.tensor_tensor(out=fl(q.u), in0=fl(q.u), in1=fl(q.sg), op=ALU.mult)
    v.tensor_tensor(out=r4(out3), in0=r4(q.u[:, 0:F].rearrange("p (j z) -> p j z", j=nj)),
                    in1=q.bm[:, 0:nb].rearrange("p (j g) -> p j g", j=nj).unsqueeze(3)
                    .broadcast_to([128, nj, sz // 16, 16]),
                    op=ALU.mult)
    return kb.sdv.n


class LNScratch:
    def __init__(self, kb, es, name, nj):
        self.st6 = kb.sb(es, name + "_st6", [128, nj, 12], F32)
        self.mv = kb.sb(es, name + "_mv", [128, nj, 2], F32)
        self.rstd = kb.sb(es, name + "_rstd", [128, nj], F32)
        self.bneg = kb.sb(es, name + "_bneg", [128, nj], F32)
        self.sq = kb.sb(es, name + "_sq", [128, nj], F32)


def emit_ln_stats(kb, ln, xres_ap, nj):
    v, s = kb.v, kb.s
    for j in range(nj):
        v.bn_stats(out=ln.st6[:, j, 0:6], in_=xres_ap[:, j, 0:512])
        v.bn_stats(out=ln.st6[:, j, 6:12], in_=xres_ap[:, j, 512:1024])
    for j in range(nj):
        v.bn_aggr(out=ln.mv[:, j, 0:2], in_=ln.st6[:, j, 0:12])
    s.wait_ge(kb.sdv.h, kb.sdv.n)
    s.activation(out=ln.sq[:, 0:nj], in_=ln.mv[:, :, 1], func=AF.Sqrt, bias=LN_EPS, scale=1.0)
    v.wait_ge(kb.sac.h, kb.sac.n)
    v.reciprocal(out=ln.rstd[:, 0:nj], in_=ln.sq[:, 0:nj])
    v.scalar_tensor_tensor(out=ln.bneg[:, 0:nj], in0=ln.mv[:, :, 0], scalar=-1.0,
                           in1=ln.rstd[:, 0:nj], op0=ALU.mult, op1=ALU.mult)
    return kb.sdv.n


def emit_l0ffn(kb, src, dst, s_wg, s_wu, s_wd, ident):
    nc = kb.nc
    v, s, g = kb.v, kb.s, kb.g
    te, sy = nc.tensor, nc.sync
    es = ExitStack()
    pfx = "ff0" + kb.uid() + "_"
    sb = lambda n, sh, dt: kb.sb(es, pfx + n, sh, dt)
    psal = lambda n, sh, dt: kb.psum(es, pfx + n, sh, dt)

    NJ, TT = 4, 512
    NTILE = T // TT

    xres = sb("xres", [128, 2, NJ, D], F32)
    xln = sb("xln", [128, 2, D], F32)
    xq = sb("xq", [128, NJ, D], BF16)
    xqT = sb("xqT", [128, 8, TT], BF16)
    hslab = sb("hslab", [128, 2, 512], F32)
    sa = sb("sa", [128, 2, 512], F32)
    hqp = sb("hqp", [128, 2, I_FF0], BF16)
    hqT = sb("hqT", [128, N_IC0, 256], BF16)
    wgbuf = sb("wgbuf", [128, 2, 8, 512], BF16)
    wubuf = sb("wubuf", [128, 2, 8, 512], BF16)
    wd = sb("wd", [128, N_IC0, D], BF16)
    qA = QuantScratch(kb, es, pfx + "qA", 1024)
    qB = QuantScratch(kb, es, pfx + "qB", 1024)
    qs = [qA, qB]
    ln = LNScratch(kb, es, pfx + "ln", NJ)

    pgu = psal("pgu", [128, 4, 512], F32)
    pd = psal("pd", [128, 2, 512], F32)
    ptr = [psal("ptrA", [128, 1024], BF16), psal("ptrB", [128, 1024], BF16)]

    sy.wait_ge(kb.sw2.h, kb.sw2.n)
    kb.sw2.inc(sy.dma_start(out=wd[:], in_=s_wd.rearrange("(p ic o) -> p ic o", p=128, ic=N_IC0)), 16)
    wd_ready = kb.sw2.n

    xsem = [kb.sx, kb.sxb]
    wsem = [kb.sw, kb.swb]
    stsem = [kb.sst, kb.sstb]
    slab_free = {0: 0, 1: 0}
    slab_ready = {}
    slab_count = [0]
    ptr_free = {0: None, 1: None}
    gu_free_g = {0: 0, 2: 0}
    gu_free_u = {1: 0, 3: 0}
    pd_free = {0: 0, 1: 0}
    store_done = []
    x_ready = {}
    pending = []

    def flush_pending():
        for (bb_, j_, oh_, np2) in pending:
            v.wait_ge(kb.spe.h, np2)
            v.scalar_tensor_tensor(out=xres[:, bb_, j_, oh_ * 512:(oh_ + 1) * 512],
                                   in0=pd[:, oh_, :], scalar=0.5,
                                   in1=xres[:, bb_, j_, oh_ * 512:(oh_ + 1) * 512],
                                   op0=ALU.mult, op1=ALU.add)
            pd_free[oh_] = kb.sdv.n
        pending.clear()

    def slab_ap(flat, og):
        off = 128 * 8 * OG_OFF[og]
        szz = OG_SIZES[og]
        return flat[off: off + 128 * 8 * szz].rearrange("(p fc o) -> p fc o", p=128, fc=8)

    def emit_slab_loads(key):
        ti_, pair_, og_ = key
        pr = slab_count[0] % 2
        slab_count[0] += 1
        sy.wait_ge(kb.spe.h, slab_free[pr])
        szz = OG_SIZES[og_]
        wsem[pr].inc(sy.dma_start(out=wgbuf[:, pr, :, 0:szz], in_=slab_ap(s_wg, og_)), 16)
        wsem[pr].inc(sy.dma_start(out=wubuf[:, pr, :, 0:szz], in_=slab_ap(s_wu, og_)), 16)
        slab_ready[key] = (wsem[pr], wsem[pr].n, pr)

    def slab_key_seq():
        for ti_ in range(NTILE):
            for pair_ in range(2):
                for og_ in range(6):
                    yield (ti_, pair_, og_)

    slab_seq = list(slab_key_seq())
    slab_pos = [0]

    def pump_slabs(upto):
        while slab_pos[0] < min(upto, len(slab_seq)):
            emit_slab_loads(slab_seq[slab_pos[0]])
            slab_pos[0] += 1

    def emit_x_load(ti):
        b = ti % 2
        if ti >= 2:
            sy.wait_ge(store_done[ti - 2][0].h, store_done[ti - 2][1])
        ap = src[ti * TT:(ti + 1) * TT, :].rearrange("(j p) f -> p j f", p=128)
        xsem[b].inc(sy.dma_start(out=xres[:, b, :, :], in_=ap), 16)
        x_ready[ti] = (xsem[b], xsem[b].n)

    emit_x_load(0)
    pump_slabs(2)

    def emit_store(tprev):
        flush_pending()
        nc.scalar.wait_ge(kb.sdv.h, kb.sdv.n)
        out_ap = dst[tprev * TT:(tprev + 1) * TT, :].rearrange("(j p) f -> p j f", p=128)
        stsem[tprev % 2].inc(nc.scalar.dma_start(out=out_ap, in_=xres[:, tprev % 2, :, :]), 16)
        store_done.append((stsem[tprev % 2], stsem[tprev % 2].n))

    for ti in range(NTILE):
        b = ti % 2
        v.wait_ge(x_ready[ti][0].h, x_ready[ti][1])
        s.wait_ge(x_ready[ti][0].h, x_ready[ti][1])
        emit_ln_stats(kb, ln, xres[:, b], NJ)
        def xq_transposes(j, xq_dv):
            for grp in range(2):
                pb = grp
                if ptr_free[pb] is not None:
                    te.wait_ge(ptr_free[pb][0].h, ptr_free[pb][1])
                te.wait_ge(kb.sdv.h, xq_dv)
                for k in range(4):
                    fc = grp * 4 + k
                    if k == 3:
                        te.wait_ge(kb.spe.h, kb.spe.n)
                    last = te.transpose(out=ptr[pb][:, k * 128:(k + 1) * 128],
                                        in_=xq[:, j, fc * 128:(fc + 1) * 128], identity=ident)
                np_ = kb.spe.inc(last)
                s.wait_ge(kb.spe.h, np_)
                s.copy(out=xqT[:, grp * 4:(grp + 1) * 4, j * 128:(j + 1) * 128],
                       in_=ptr[pb][:, 0:512])
                ptr_free[pb] = (kb.sac, kb.sac.n)

        for j in range(NJ):
            s.wait_ge(kb.sdv.h, kb.sdv.n)
            s.activation(out=xln[:, j % 2, :], in_=xres[:, b, j, :], func=AF.Identity,
                         scale=ln.rstd[:, j:j + 1], bias=ln.bneg[:, j:j + 1])
            v.wait_ge(kb.sac.h, kb.sac.n)
            emit_quant_front(kb, qs[j % 2], xln[:, j % 2, :].unsqueeze(1), 1, D)
            if j > 0:
                dv = emit_quant_back(kb, qs[(j - 1) % 2], xq[:, j - 1, :].unsqueeze(1), 1, D)
                xq_transposes(j - 1, dv)
        dv = emit_quant_back(kb, qs[(NJ - 1) % 2], xq[:, NJ - 1, :].unsqueeze(1), 1, D)
        xq_transposes(NJ - 1, dv)
        xqT_ready = kb.sac.n
        te.wait_ge(kb.sac.h, xqT_ready)
        if ti > 0:
            emit_store(ti - 1)
        if ti + 1 < NTILE:
            emit_x_load(ti + 1)

        for pair in range(2):
            back_pend = None
            for og in range(6):
                sz = OG_SIZES[og]
                key = (ti, pair, og)
                sr = slab_ready[key]
                for jj in range(2):
                    j = 2 * pair + jj
                    gslot = jj * 2
                    uslot = gslot + 1
                    te.wait_ge(sr[0].h, sr[1])
                    te.wait_ge(kb.sac.h, gu_free_g[gslot])
                    te.wait_ge(kb.sdv.h, gu_free_u[uslot])
                    for fc in range(8):
                        te.matmul(pgu[:, gslot, 0:sz], lhsT=xqT[:, fc, j * 128:(j + 1) * 128],
                                  rhs=wgbuf[:, sr[2], fc, 0:sz], start=(fc == 0), stop=(fc == 7))
                    for fc in range(8):
                        if fc == 7:
                            te.wait_ge(kb.spe.h, kb.spe.n)
                        last = te.matmul(pgu[:, uslot, 0:sz],
                                         lhsT=xqT[:, fc, j * 128:(j + 1) * 128],
                                         rhs=wubuf[:, sr[2], fc, 0:sz],
                                         start=(fc == 0), stop=(fc == 7))
                    np_ = kb.spe.inc(last)
                    if jj == 1:
                        slab_free[sr[2]] = np_
                        pump_slabs(slab_pos[0] + 1)
                    s.wait_ge(kb.spe.h, np_)
                    s.activation(out=sa[:, jj, 0:sz], in_=pgu[:, gslot, 0:sz], func=AF.Silu)
                    gu_free_g[gslot] = kb.sac.n
                    v.wait_ge(kb.sac.h, kb.sac.n)
                    v.tensor_tensor(out=hslab[:, jj, 0:sz], in0=sa[:, jj, 0:sz],
                                    in1=pgu[:, uslot, 0:sz], op=ALU.mult)
                    gu_free_u[uslot] = kb.sdv.n
                emit_quant_front(kb, qs[og % 2], hslab[:, 0:2, 0:sz], 2, sz)
                if back_pend is not None:
                    emit_quant_back(kb, qs[back_pend[0] % 2],
                                    hqp[:, 0:2, OG_OFF[back_pend[0]]:OG_OFF[back_pend[0]] + back_pend[1]],
                                    2, back_pend[1])
                back_pend = (og, sz)
            dv = emit_quant_back(kb, qs[back_pend[0] % 2],
                                 hqp[:, 0:2, OG_OFF[back_pend[0]]:OG_OFF[back_pend[0]] + back_pend[1]],
                                 2, back_pend[1])
            hq_done = (kb.sdv, dv)
            for jj in range(2):
                j = 2 * pair + jj
                for grp in range(6):
                    pb = grp % 2
                    n_in = min(4, N_IC0 - grp * 4)
                    if ptr_free[pb] is not None:
                        te.wait_ge(ptr_free[pb][0].h, ptr_free[pb][1])
                    if grp == 0 and jj == 0:
                        te.wait_ge(hq_done[0].h, hq_done[1])
                    for k in range(n_in):
                        ic = grp * 4 + k
                        if k == n_in - 1:
                            te.wait_ge(kb.spe.h, kb.spe.n)
                        last = te.transpose(out=ptr[pb][:, k * 128:(k + 1) * 128],
                                            in_=hqp[:, jj, ic * 128:(ic + 1) * 128],
                                            identity=ident)
                    np_ = kb.spe.inc(last)
                    s.wait_ge(kb.spe.h, np_)
                    s.copy(out=hqT[:, grp * 4:grp * 4 + n_in, jj * 128:(jj + 1) * 128],
                           in_=ptr[pb][:, 0:n_in * 128])
                    ptr_free[pb] = (kb.sac, kb.sac.n)
                hqT_ready = kb.sac.n
                te.wait_ge(kb.sac.h, hqT_ready)
                if ti == 0 and pair == 0 and jj == 0:
                    te.wait_ge(kb.sw2.h, wd_ready)
                for oh in range(2):
                    if any(p_[2] == oh for p_ in pending):
                        flush_pending()
                    te.wait_ge(kb.sdv.h, pd_free[oh])
                    for ic in range(N_IC0):
                        if ic == N_IC0 - 1:
                            te.wait_ge(kb.spe.h, kb.spe.n)
                        last = te.matmul(pd[:, oh, :], lhsT=hqT[:, ic, jj * 128:(jj + 1) * 128],
                                         rhs=wd[:, ic, oh * 512:(oh + 1) * 512],
                                         start=(ic == 0), stop=(ic == N_IC0 - 1))
                    np_ = kb.spe.inc(last)
                    pending.append((b, j, oh, np_))

    emit_store(NTILE - 1)

    kb.barrier()
    es.close()



def emit_l0attn(kb, src, dst, wqf, gnwf, gnbf, c_dtc, c_lamc, c_muc,
                c_glcc, c_b2, c_ones2, ident, identf):
    nc = kb.nc
    v, s = kb.v, kb.s
    te, sy, g_ = nc.tensor, nc.sync, nc.gpsimd
    es = ExitStack()
    pfx = "at0" + kb.uid() + "_"
    sb = lambda n, sh, dt: kb.sb(es, pfx + n, sh, dt)
    psal = lambda n, sh, dt: kb.psum(es, pfx + n, sh, dt)

    SC = 512
    NSC = T // SC
    NJ = 4

    xres = sb("xres", [128, NJ, D], F32)
    xhat = sb("xhat", [128, D], BF16)
    xhT = sb("xhT", [128, 8, SC], BF16)
    qf = sb("qf", [128, 8, SC], BF16)
    kf = sb("kf", [128, 8, SC], BF16)
    vt = sb("vt", [128, NJ, D], BF16)
    ktl = sb("ktl", [128, NJ, 8, 128], BF16)
    sdtAll = sb("sdtAll", [128, NH, 128], BF16)
    yf = sb("yf", [128, 8, T], BF16)
    S_ = sb("S", [128, 8, HD], F32)
    S_bf = sb("Sbf", [128, 8, HD], BF16)
    wq_sb = sb("wq", [128, 64, 128], BF16)
    wk_sb = sb("wk", [128, 64, 128], BF16)
    wv_sb = sb("wv", [128, 8, D], BF16)
    dtc = sb("dtc", [128, NH, 128], F32)
    lamc = sb("lamc", [128, 8, 128], F32)
    muc = sb("muc", [128, 8, 128], F32)
    glc = sb("glc", [128, 8], F32)
    gnw = sb("gnw", [128, 8], F32)
    gnb = sb("gnb", [128, 8], F32)
    b2c = sb("b2c", [2, 128], F32)
    ones2 = sb("ones2", [128, 2], F32)
    st6c = sb("st6c", [128, 8, 48], F32)
    mvv = sb("mvv", [128, 8, 2], F32)
    ey2 = sb("ey2", [128, 8], F32)
    m2 = sb("m2", [128, 8], F32)
    sr8 = sb("sr8", [8, 4], F32)
    rtT = sb("rtT", [2, 8], F32)
    mnT = sb("mnT", [2, 8], F32)
    aG = sb("aG", [128, 8], F32)
    bG = sb("bG", [128, 8], F32)
    gmn = sb("gmn", [128, 8], F32)
    ln = LNScratch(kb, es, pfx + "ln", NJ)

    pbig = psal("pbig", [128, 6, 512], F32)
    ptr = [psal("ptrA", [128, 1024], BF16), psal("ptrB", [128, 1024], BF16)]

    sy.wait_ge(kb.sw2.h, kb.sw2.n)
    sy.wait_ge(kb.swg.h, wqf["marker"])
    kb.sw2.inc(sy.dma_start(out=wq_sb[:], in_=wqf["wq"].rearrange("(a p o) -> p a o", p=128, a=64)), 16)
    kb.sw2.inc(sy.dma_start(out=wk_sb[:], in_=wqf["wk"].rearrange("(a p o) -> p a o", p=128, a=64)), 16)
    kb.sw2.inc(sy.dma_start(out=wv_sb[:], in_=wqf["wv"].rearrange("(fc p o) -> p fc o", p=128, fc=8)), 16)
    kb.sw2.inc(sy.dma_start(out=dtc[:], in_=c_dtc[:, :, :]), 16)
    kb.sw2.inc(sy.dma_start(out=lamc[:], in_=c_lamc[:, :, :]), 16)
    kb.sw2.inc(sy.dma_start(out=muc[:], in_=c_muc[:, :, :]), 16)
    kb.sw2.inc(sy.dma_start(out=glc[:], in_=c_glcc[:, :]), 16)
    kb.sw2.inc(sy.dma_start(out=gnw[:], in_=gnwf[:, :]), 16)
    kb.sw2.inc(sy.dma_start(out=gnb[:], in_=gnbf[:, :]), 16)
    kb.sw2.inc(sy.dma_start(out=b2c[:], in_=c_b2[:, :]), 16)
    kb.sw2.inc(sy.dma_start(out=ones2[:], in_=c_ones2[:, :]), 16)
    w_ready = (0, kb.sw2.n)
    xsem = [kb.sx, kb.sxb]

    v.memset(S_[:], 0.0)
    v.memset(S_bf[:], 0.0)
    sbf_done = kb.sdv.n

    ptr_free = {0: None, 1: None}
    psS_free = {0: 0, 1: 0}
    psO_free = {0: 0, 1: 0, 2: 0, 3: 0}
    psDS_free = {4: 0, 5: 0}
    qkv_free = 0
    xres_free = 0
    store_done = 0

    te.wait_ge(kb.sw2.h, w_ready[1])

    def emit_x_load(idx):
        sy.wait_ge(kb.sac.h, xres_free)
        sy.wait_ge(kb.sst.h, store_done)
        ap = src[idx * SC:(idx + 1) * SC, :].rearrange("(j p) f -> p j f", p=128)
        xsem[idx % 2].inc(sy.dma_start(out=xres[:, :, :], in_=ap), 16)
        return (xsem[idx % 2], xsem[idx % 2].n)

    x_ready = emit_x_load(0)

    for sc in range(NSC):
        v.wait_ge(x_ready[0].h, x_ready[1])
        s.wait_ge(x_ready[0].h, x_ready[1])
        emit_ln_stats(kb, ln, xres[:, :, :], NJ)
        for j in range(NJ):
            s.wait_ge(kb.sdv.h, kb.sdv.n)
            s.wait_ge(kb.spe.h, kb.spe.n)
            s.activation(out=xhat[:], in_=xres[:, j, :], func=AF.Identity,
                         scale=ln.rstd[:, j:j + 1], bias=ln.bneg[:, j:j + 1])
            if j == NJ - 1:
                xres_free = kb.sac.n
            for grp in range(2):
                pb = grp
                if ptr_free[pb] is not None:
                    te.wait_ge(ptr_free[pb][0].h, ptr_free[pb][1])
                te.wait_ge(kb.sac.h, kb.sac.n)
                for k in range(4):
                    fc = grp * 4 + k
                    if k == 3:
                        te.wait_ge(kb.spe.h, kb.spe.n)
                    last = te.transpose(out=ptr[pb][:, k * 128:(k + 1) * 128],
                                        in_=xhat[:, fc * 128:(fc + 1) * 128], identity=ident)
                np_ = kb.spe.inc(last)
                s.wait_ge(kb.spe.h, np_)
                s.copy(out=xhT[:, grp * 4:(grp + 1) * 4, j * 128:(j + 1) * 128],
                       in_=ptr[pb][:, 0:512])
                ptr_free[pb] = (kb.sac, kb.sac.n)
        xhT_ready = kb.sac.n
        te.wait_ge(kb.sac.h, xhT_ready)
        te.wait_ge(kb.sdv.h, kb.sdv.n)
        te.wait_ge(kb.sac.h, kb.sac.n)
        qk_slot_free = {0: 0, 1: 0}
        for wsel, (wtile, out_sb) in enumerate([(wq_sb, qf), (wk_sb, kf)]):
            for oc in range(8):
                slot = (2 * wsel + oc) % 2
                te.wait_ge(kb.sdv.h, qk_slot_free[slot])
                for fc in range(8):
                    if fc == 7:
                        te.wait_ge(kb.spe.h, kb.spe.n)
                    last = te.matmul(pbig[:, slot, :], lhsT=wtile[:, oc * 8 + fc, :],
                                     rhs=xhT[:, fc, :], start=(fc == 0), stop=(fc == 7))
                np_ = kb.spe.inc(last)
                v.wait_ge(kb.spe.h, np_)
                if wsel == 0:
                    v.tensor_tensor(out=qf[:, oc, :].rearrange("p (c t) -> p c t", t=128),
                                    in0=pbig[:, slot, :].rearrange("p (c t) -> p c t", t=128),
                                    in1=lamc[:, oc, :].unsqueeze(1).broadcast_to([128, NJ, 128]),
                                    op=ALU.mult)
                else:
                    v.tensor_copy(out=kf[:, oc, :], in_=pbig[:, slot, :])
                qk_slot_free[slot] = kb.sdv.n
        v_slot_free = {2: 0, 3: 0}
        for j in range(NJ):
            for oh in range(2):
                slot = 2 + (2 * j + oh) % 2
                te.wait_ge(kb.sdv.h, v_slot_free[slot])
                for fc in range(8):
                    if fc == 7:
                        te.wait_ge(kb.spe.h, kb.spe.n)
                    last = te.matmul(pbig[:, slot, :], lhsT=xhT[:, fc, j * 128:(j + 1) * 128],
                                     rhs=wv_sb[:, fc, oh * 512:(oh + 1) * 512],
                                     start=(fc == 0), stop=(fc == 7))
                np_ = kb.spe.inc(last)
                v.wait_ge(kb.spe.h, np_)
                v.tensor_copy(out=vt[:, j, oh * 512:(oh + 1) * 512], in_=pbig[:, slot, :])
                v_slot_free[slot] = kb.sdv.n
        for j in range(NJ):
            for grp in range(2):
                pb = grp
                if ptr_free[pb] is not None:
                    te.wait_ge(ptr_free[pb][0].h, ptr_free[pb][1])
                te.wait_ge(kb.sdv.h, kb.sdv.n)
                for k in range(4):
                    oc = grp * 4 + k
                    if k == 3:
                        te.wait_ge(kb.spe.h, kb.spe.n)
                    last = te.transpose(out=ptr[pb][:, k * 128:(k + 1) * 128],
                                        in_=kf[:, oc, j * 128:(j + 1) * 128], identity=ident)
                np_ = kb.spe.inc(last)
                v.wait_ge(kb.spe.h, np_)
                v.tensor_tensor(out=ktl[:, j, grp * 4:(grp + 1) * 4, :],
                                in0=ptr[pb][:, 0:512].rearrange("p (a q) -> p a q", q=128),
                                in1=muc[:, grp * 4:(grp + 1) * 4, :], op=ALU.mult)
                ptr_free[pb] = (kb.sdv, kb.sdv.n)

        if sc + 1 < NSC:
            x_ready = emit_x_load(sc + 1)

        for cl in range(NJ):
            te.wait_ge(kb.sdv.h, kb.sdv.n)
            te.wait_ge(kb.sac.h, kb.sac.n)
            for h in range(NH):
                if h == NH - 1:
                    te.wait_ge(kb.spe.h, kb.spe.n)
                hc, par = h // 2, h % 2
                hr = 64 * par
                last = te.matmul(pbig[0:128, h % 4, (h // 4) * 128:(h // 4) * 128 + 128],
                                 lhsT=kf[hr:hr + 64, hc, cl * 128:(cl + 1) * 128],
                                 rhs=qf[hr:hr + 64, hc, cl * 128:(cl + 1) * 128],
                                 start=True, stop=True)
            nsc_ = kb.spe.inc(last)
            v.wait_ge(kb.spe.h, nsc_)
            v.tensor_tensor(out=sdtAll[:, :, :].rearrange("p (q b) i -> p q b i", b=4),
                            in0=pbig[:, 0:4, :].rearrange("p b (q i) -> p q b i", i=128),
                            in1=dtc[:, :, :].rearrange("p (q b) i -> p q b i", b=4),
                            op=ALU.mult)
            sdt_dv = kb.sdv.n
            te.wait_ge(kb.sdv.h, sdt_dv)
            for h in range(NH):
                hc, par = h // 2, h % 2
                hr = 64 * par
                m = h // 2
                ob = (0 if m < 4 else 1) + 2 * par
                ocol = (m % 4) * 128
                if h in (0, 1, 8, 9):
                    te.wait_ge(kb.sac.h, psO_free[ob])
                if h < 2:
                    te.wait_ge(kb.sdv.h, psDS_free[4 + par])
                te.matmul(pbig[0:64, ob, ocol:ocol + 128], lhsT=vt[:, cl, 64 * h:64 * h + 64],
                          rhs=sdtAll[:, h, :], start=True, stop=False)
                te.wait_ge(kb.spe.h, kb.spe.n)
                last = te.matmul(pbig[0:64, ob, ocol:ocol + 128], lhsT=S_bf[hr:hr + 64, hc, :],
                                 rhs=qf[hr:hr + 64, hc, cl * 128:(cl + 1) * 128],
                                 start=False, stop=True)
                no_ = kb.spe.inc(last)
                te.wait_ge(kb.spe.h, kb.spe.n)
                last = te.matmul(pbig[0:64, 4 + par, m * 64:(m + 1) * 64],
                                 lhsT=ktl[:, cl, hc, hr:hr + 64],
                                 rhs=vt[:, cl, 64 * h:64 * h + 64], start=True, stop=True)
                nds_ = kb.spe.inc(last)
                if h in (6, 7, 14, 15):
                    s.wait_ge(kb.spe.h, no_)
                    hcb = 0 if h in (6, 7) else 4
                    s.copy(out=yf[hr:hr + 64, hcb:hcb + 4,
                                  sc * SC + cl * 128: sc * SC + (cl + 1) * 128],
                           in_=pbig[0:64, ob, 0:512].rearrange("p (a i) -> p a i", i=128))
                    psO_free[ob] = kb.sac.n
                if h >= 14:
                    v.wait_ge(kb.spe.h, nds_)
                    v.tensor_tensor(out=S_[hr:hr + 64, :, :], in0=S_[hr:hr + 64, :, :],
                                    in1=glc[hr:hr + 64, :].unsqueeze(2)
                                    .broadcast_to([64, 8, HD]), op=ALU.mult)
                    v.tensor_tensor(out=S_[hr:hr + 64, :, :], in0=S_[hr:hr + 64, :, :],
                                    in1=pbig[0:64, 4 + par, :].rearrange("p (a d) -> p a d", d=64),
                                    op=ALU.add)
                    v.tensor_copy(out=S_bf[hr:hr + 64, :, :], in_=S_[hr:hr + 64, :, :])
                    psDS_free[4 + par] = kb.sdv.n
        qkv_free = kb.spe.n

    v.wait_ge(kb.sac.h, kb.sac.n)
    for oc in range(8):
        for k in range(8):
            v.bn_stats(out=st6c[:, oc, k * 6:(k + 1) * 6], in_=yf[:, oc, k * 512:(k + 1) * 512])
    for oc in range(8):
        v.bn_aggr(out=mvv[:, oc, 0:2], in_=st6c[:, oc, 0:48])
    v.tensor_tensor(out=m2[:], in0=mvv[:, :, 0], in1=mvv[:, :, 0], op=ALU.mult)
    v.tensor_tensor(out=ey2[:], in0=mvv[:, :, 1], in1=m2[:], op=ALU.add)
    te.wait_ge(kb.sdv.h, kb.sdv.n)
    te.wait_ge(kb.spe.h, kb.spe.n)
    last = te.matmul(pbig[0:8, 0, 0:2], lhsT=mvv[:, :, 0], rhs=ones2[:], start=True, stop=True)
    te.wait_ge(kb.spe.h, kb.spe.n)
    last2 = te.matmul(pbig[0:8, 1, 0:2], lhsT=ey2[:], rhs=ones2[:], start=True, stop=True)
    nst = kb.spe.inc(last2)
    v.wait_ge(kb.spe.h, nst)
    v.tensor_scalar(out=sr8[0:8, 0:2], in0=pbig[0:8, 0, 0:2], scalar1=1.0 / 64, scalar2=None,
                    op0=ALU.mult)
    v.tensor_scalar(out=sr8[0:8, 2:4], in0=pbig[0:8, 1, 0:2], scalar1=1.0 / 64, scalar2=None,
                    op0=ALU.mult)
    v.tensor_tensor(out=ey2[0:8, 0:2], in0=sr8[0:8, 0:2], in1=sr8[0:8, 0:2], op=ALU.mult)
    v.tensor_tensor(out=ey2[0:8, 2:4], in0=sr8[0:8, 2:4], in1=ey2[0:8, 0:2], op=ALU.subtract)
    s.wait_ge(kb.sdv.h, kb.sdv.n)
    s.activation(out=m2[0:8, 0:2], in_=ey2[0:8, 2:4], func=AF.Sqrt, bias=GN_EPS, scale=1.0)
    v.wait_ge(kb.sac.h, kb.sac.n)
    v.reciprocal(out=m2[0:8, 2:4], in_=m2[0:8, 0:2])
    v.tensor_copy(out=m2[0:8, 4:6], in_=sr8[0:8, 0:2])
    te.wait_ge(kb.sdv.h, kb.sdv.n)
    te.wait_ge(kb.spe.h, kb.spe.n)
    last = te.transpose(out=pbig[0:2, 2, 0:8], in_=m2[0:8, 2:4], identity=identf[0:8, 0:8])
    te.wait_ge(kb.spe.h, kb.spe.n)
    last = te.transpose(out=pbig[0:2, 3, 0:8], in_=m2[0:8, 4:6], identity=identf[0:8, 0:8])
    ntr = kb.spe.inc(last)
    v.wait_ge(kb.spe.h, ntr)
    v.tensor_copy(out=rtT[0:2, 0:8], in_=pbig[0:2, 2, 0:8])
    v.tensor_copy(out=mnT[0:2, 0:8], in_=pbig[0:2, 3, 0:8])
    te.wait_ge(kb.sdv.h, kb.sdv.n)
    te.wait_ge(kb.spe.h, kb.spe.n)
    last = te.matmul(pbig[0:128, 4, 0:8], lhsT=b2c[:, :], rhs=rtT[0:2, 0:8], start=True, stop=True)
    te.wait_ge(kb.spe.h, kb.spe.n)
    last = te.matmul(pbig[0:128, 5, 0:8], lhsT=b2c[:, :], rhs=mnT[0:2, 0:8], start=True, stop=True)
    ngb = kb.spe.inc(last)
    v.wait_ge(kb.spe.h, ngb)
    v.tensor_tensor(out=aG[:], in0=pbig[:, 4, 0:8], in1=gnw[:], op=ALU.mult)
    v.tensor_copy(out=gmn[:], in_=pbig[:, 5, 0:8])
    v.scalar_tensor_tensor(out=bG[:], in0=gmn[:], scalar=-1.0, in1=aG[:], op0=ALU.mult, op1=ALU.mult)
    v.tensor_tensor(out=bG[:], in0=bG[:], in1=gnb[:], op=ALU.add)
    gn_ready_dv = kb.sdv.n

    sy.wait_ge(kb.spe.h, kb.spe.n)
    sy.wait_ge(kb.sw2.h, kb.sw2.n)
    kb.sw2.inc(sy.dma_start(out=wv_sb[:], in_=wqf["wo"].rearrange("(oc p o) -> p oc o", p=128, oc=8)), 16)
    wo_ready = kb.sw2.n

    s.wait_ge(kb.sdv.h, gn_ready_dv)
    te.wait_ge(kb.sw2.h, wo_ready)
    for sc in range(NSC):
        x_ready = emit_x_load(sc)
        for oc in range(8):
            s.activation(out=xhT[:, oc, :], in_=yf[:, oc, sc * SC:(sc + 1) * SC],
                         func=AF.Identity, scale=aG[:, oc:oc + 1], bias=bG[:, oc:oc + 1])
        yh_ready = kb.sac.n
        v.wait_ge(x_ready[0].h, x_ready[1])
        te.wait_ge(kb.sac.h, yh_ready)
        for j in range(NJ):
            for oh in range(2):
                slot = (2 * j + oh) % 4
                te.wait_ge(kb.sdv.h, psS_free.get(('c', slot), 0))
                for oc in range(8):
                    if oc == 7:
                        te.wait_ge(kb.spe.h, kb.spe.n)
                    last = te.matmul(pbig[:, slot, :], lhsT=xhT[:, oc, j * 128:(j + 1) * 128],
                                     rhs=wv_sb[:, oc, oh * 512:(oh + 1) * 512],
                                     start=(oc == 0), stop=(oc == 7))
                np_ = kb.spe.inc(last)
                v.wait_ge(kb.spe.h, np_)
                v.tensor_tensor(out=xres[:, j, oh * 512:(oh + 1) * 512],
                                in0=pbig[:, slot, :], in1=xres[:, j, oh * 512:(oh + 1) * 512],
                                op=ALU.add)
                psS_free[('c', slot)] = kb.sdv.n
        nc.scalar.wait_ge(kb.sdv.h, kb.sdv.n)
        out_ap = dst[sc * SC:(sc + 1) * SC, :].rearrange("(j p) f -> p j f", p=128)
        kb.sst.inc(nc.scalar.dma_start(out=out_ap, in_=xres[:, :, :]), 16)
        store_done = kb.sst.n
        xres_free = kb.sac.n if False else xres_free
    kb.barrier()
    es.close()




def emit_l1ffn(kb, src, dst, w, s1f, s2f, sh1f, sh2f, s3f, ident, identf):
    nc = kb.nc
    v, s = kb.v, kb.s
    te, sy = nc.tensor, nc.sync
    es = ExitStack()
    pfx = "ff1" + kb.uid() + "_"
    sb = lambda n, sh, dt: kb.sb(es, pfx + n, sh, dt)
    psal = lambda n, sh, dt: kb.psum(es, pfx + n, sh, dt)

    NJ, TT = 4, 512
    NTILE = T // TT
    NIC = 32
    SLABS_PER_TILE = 44
    S1O, S2O, S3O, S4O = 0, 8, 24, 40

    xres = sb("xres", [128, 2, NJ, D], F32)
    xhat = sb("xhat", [128, D], BF16)
    xhT = sb("xhT", [128, 8, TT], BF16)
    hbuf = sb("h", [128, NIC, TT], BF16)
    gbuf = sb("g", [128, NIC, TT], BF16)
    warena = sb("wa", [128, 2, 8192], BF16)
    of = sb("of", [128, 2, TT], F32)
    s1 = sb("s1", [128, 32], F32)
    s2 = sb("s2", [128, 32], F32)
    sh1 = sb("sh1", [128, 32], F32)
    sh2 = sb("sh2", [128, 32], F32)
    s3h = sb("s3h", [128, 8], F32)
    ln = LNScratch(kb, es, pfx + "ln", NJ)

    pmm = psal("pmm", [128, 4, 512], F32)
    pot = psal("pot", [128, 2, 512], F32)
    ptr = [psal("ptrA", [128, 1024], BF16), psal("ptrB", [128, 1024], BF16)]

    sy.wait_ge(kb.sw2.h, kb.sw2.n)
    for dstt, srcf in [(s1, s1f), (s2, s2f), (sh1, sh1f), (sh2, sh2f)]:
        kb.sw2.inc(sy.dma_start(out=dstt[:], in_=srcf[:, :]), 16)
    kb.sw2.inc(sy.dma_start(out=s3h[:], in_=s3f[:, :]), 16)
    sc_ready = kb.sw2.n
    v.wait_ge(kb.sw2.h, sc_ready)
    v.tensor_scalar(out=s3h[:], in0=s3h[:], scalar1=0.5, scalar2=None, op0=ALU.mult)
    s3_dv = kb.sdv.n

    xsem = [kb.sx, kb.sxb]
    stsem = [kb.sst, kb.sstb]
    wsem = [kb.sw, kb.swb]
    store_done = []
    ptr_free = {0: None, 1: None}
    mm_free = {0: None, 1: None, 2: None, 3: None}
    pot_free = {0: 0, 1: 0}
    of_free = {0: None, 1: None}
    slab_free = {0: 0, 1: 0}
    slab_ready = {}

    def emit_slab(pos):
        ti, rel = divmod(pos, SLABS_PER_TILE)
        pr = pos % 2
        sy.wait_ge(kb.spe.h, slab_free[pr])
        if rel < S2O:
            sidx = rel
            npp = 4096
            off = 128 * npp * sidx
            wsem[pr].inc(sy.dma_start(
                out=warena[:, pr, 0:npp],
                in_=w["w1"][off:off + 128 * npp].rearrange("(p a) -> p a", p=128)), 16)
            wsem[pr].inc(sy.dma_start(
                out=warena[:, pr, npp:2 * npp],
                in_=w["w2"][off:off + 128 * npp].rearrange("(p a) -> p a", p=128)), 16)
        else:
            if rel < S3O:
                key, sidx = "h1", rel - S2O
            elif rel < S4O:
                key, sidx = "h2", rel - S3O
            else:
                key, sidx = "w3", rel - S4O
            npp = 8192
            off = 128 * npp * sidx
            wsem[pr].inc(sy.dma_start(
                out=warena[:, pr, 0:npp],
                in_=w[key][off:off + 128 * npp].rearrange("(p a) -> p a", p=128)), 16)
        slab_ready[pos] = (wsem[pr], wsem[pr].n)

    slab_pos = 0

    def pump_slabs(upto):
        nonlocal slab_pos
        while slab_pos < min(upto, NTILE * SLABS_PER_TILE):
            emit_slab(slab_pos)
            slab_pos += 1

    x_ready = {}

    def emit_x_load(ti):
        b = ti % 2
        if ti >= 2:
            sy.wait_ge(store_done[ti - 2][0].h, store_done[ti - 2][1])
        ap = src[ti * TT:(ti + 1) * TT, :].rearrange("(j p) f -> p j f", p=128)
        xsem[b].inc(sy.dma_start(out=xres[:, b, :, :], in_=ap), 16)
        x_ready[ti] = (xsem[b], xsem[b].n)

    sy.wait_ge(kb.swg.h, w["marker"])
    emit_x_load(0)
    pump_slabs(2)

    for ti in range(NTILE):
        b = ti % 2
        base = ti * SLABS_PER_TILE
        if ti + 1 < NTILE:
            emit_x_load(ti + 1)
        v.wait_ge(x_ready[ti][0].h, x_ready[ti][1])
        s.wait_ge(x_ready[ti][0].h, x_ready[ti][1])
        emit_ln_stats(kb, ln, xres[:, b], NJ)
        for j in range(NJ):
            s.wait_ge(kb.sdv.h, kb.sdv.n)
            s.wait_ge(kb.spe.h, kb.spe.n)
            s.activation(out=xhat[:], in_=xres[:, b, j, :], func=AF.Identity,
                         scale=ln.rstd[:, j:j + 1], bias=ln.bneg[:, j:j + 1])
            for grp in range(2):
                pb = grp
                if ptr_free[pb] is not None:
                    te.wait_ge(ptr_free[pb][0].h, ptr_free[pb][1])
                te.wait_ge(kb.sac.h, kb.sac.n)
                for k in range(4):
                    fc = grp * 4 + k
                    if k == 3:
                        te.wait_ge(kb.spe.h, kb.spe.n)
                    last = te.transpose(out=ptr[pb][:, k * 128:(k + 1) * 128],
                                        in_=xhat[:, fc * 128:(fc + 1) * 128], identity=ident)
                np_ = kb.spe.inc(last)
                s.wait_ge(kb.spe.h, np_)
                s.copy(out=xhT[:, grp * 4:(grp + 1) * 4, j * 128:(j + 1) * 128],
                       in_=ptr[pb][:, 0:512])
                ptr_free[pb] = (kb.sac, kb.sac.n)
        xhT_ready = kb.sac.n
        te.wait_ge(kb.sac.h, xhT_ready)
        if ti == 0:
            v.wait_ge(kb.sw2.h, sc_ready)
            s.wait_ge(kb.sw2.h, sc_ready)

        for oc in range(NIC):
            sidx = oc // 4
            pos = base + S1O + sidx
            pr = pos % 2
            sr = slab_ready[pos]
            oin = oc % 4
            for half in range(2):
                slot = (2 * oc + half) % 4
                te.wait_ge(sr[0].h, sr[1])
                if mm_free[slot] is not None:
                    te.wait_ge(mm_free[slot][0].h, mm_free[slot][1])
                for fc in range(8):
                    if fc == 7:
                        te.wait_ge(kb.spe.h, kb.spe.n)
                    last = te.matmul(
                        pmm[:, slot, :],
                        lhsT=warena[:, pr, half * 4096 + (oin * 8 + fc) * 128:
                                    half * 4096 + (oin * 8 + fc) * 128 + 128],
                        rhs=xhT[:, fc, :], start=(fc == 0), stop=(fc == 7))
                np_ = kb.spe.inc(last)
                par = oc % 2
                if half == 0:
                    s.wait_ge(kb.spe.h, np_)
                    if of_free[par] is not None:
                        s.wait_ge(of_free[par][0].h, of_free[par][1])
                    s.activation(out=of[:, par, :], in_=pmm[:, slot, :], func=AF.Silu,
                                 scale=s1[:, oc:oc + 1])
                    mm_free[slot] = (kb.sac, kb.sac.n)
                else:
                    v.wait_ge(kb.spe.h, np_)
                    v.wait_ge(kb.sac.h, kb.sac.n)
                    v.scalar_tensor_tensor(out=hbuf[:, oc, :], in0=pmm[:, slot, :],
                                           scalar=s2[:, oc:oc + 1], in1=of[:, par, :],
                                           op0=ALU.mult, op1=ALU.mult)
                    mm_free[slot] = (kb.sdv, kb.sdv.n)
                    of_free[par] = (kb.sdv, kb.sdv.n)
            if oin == 3:
                slab_free[pr] = kb.spe.n
                pump_slabs(pos + 3)
        h_ready_dv = kb.sdv.n

        stage_out_sac = 0
        for stage, soff, inbuf, outbuf, ssc in [(2, S2O, hbuf, gbuf, sh1),
                                                (3, S3O, gbuf, hbuf, sh2)]:
            for oc in range(NIC):
                sidx = oc // 2
                pos = base + soff + sidx
                pr = pos % 2
                sr = slab_ready[pos]
                oin = oc % 2
                slot = oc % 4
                te.wait_ge(sr[0].h, sr[1])
                if oc == 0:
                    if stage == 2:
                        te.wait_ge(kb.sdv.h, h_ready_dv)
                    else:
                        te.wait_ge(kb.sac.h, stage_out_sac)
                if mm_free[slot] is not None:
                    te.wait_ge(mm_free[slot][0].h, mm_free[slot][1])
                for ic in range(NIC):
                    if ic == NIC - 1:
                        te.wait_ge(kb.spe.h, kb.spe.n)
                    last = te.matmul(pmm[:, slot, :],
                                     lhsT=warena[:, pr, (oin * 32 + ic) * 128:
                                                 (oin * 32 + ic) * 128 + 128],
                                     rhs=inbuf[:, ic, :], start=(ic == 0), stop=(ic == NIC - 1))
                np_ = kb.spe.inc(last)
                s.wait_ge(kb.spe.h, np_)
                s.activation(out=outbuf[:, oc, :], in_=pmm[:, slot, :], func=AF.Gelu,
                             scale=ssc[:, oc:oc + 1])
                mm_free[slot] = (kb.sac, kb.sac.n)
                if oin == 1:
                    slab_free[pr] = kb.spe.n
                    pump_slabs(pos + 3)
            stage_out_sac = kb.sac.n

        for oc3 in range(8):
            sidx = oc3 // 2
            pos = base + S4O + sidx
            pr = pos % 2
            sr = slab_ready[pos]
            oin = oc3 % 2
            slot = oc3 % 4
            te.wait_ge(sr[0].h, sr[1])
            if oc3 == 0:
                te.wait_ge(kb.sac.h, stage_out_sac)
            if mm_free[slot] is not None:
                te.wait_ge(mm_free[slot][0].h, mm_free[slot][1])
            for ic in range(NIC):
                if ic == NIC - 1:
                    te.wait_ge(kb.spe.h, kb.spe.n)
                last = te.matmul(pmm[:, slot, :],
                                 lhsT=warena[:, pr, (oin * 32 + ic) * 128:
                                             (oin * 32 + ic) * 128 + 128],
                                 rhs=hbuf[:, ic, :], start=(ic == 0), stop=(ic == NIC - 1))
            np_ = kb.spe.inc(last)
            par = oc3 % 2
            s.wait_ge(kb.spe.h, np_)
            if ti == 0 and oc3 == 0:
                s.wait_ge(kb.sdv.h, s3_dv)
            if of_free[par] is not None:
                s.wait_ge(of_free[par][0].h, of_free[par][1])
            s.activation(out=of[:, par, :], in_=pmm[:, slot, :], func=AF.Identity,
                         scale=s3h[:, oc3:oc3 + 1])
            mm_free[slot] = (kb.sac, kb.sac.n)
            ofr = kb.sac.n
            for tj in range(NJ):
                pb2 = tj % 2
                te.wait_ge(kb.sdv.h, pot_free[pb2])
                te.wait_ge(kb.sac.h, ofr)
                te.wait_ge(kb.spe.h, kb.spe.n)
                last = te.transpose(out=pot[:, pb2, 0:128],
                                    in_=of[:, par, tj * 128:(tj + 1) * 128], identity=identf)
                np2 = kb.spe.inc(last)
                v.wait_ge(kb.spe.h, np2)
                v.tensor_tensor(out=xres[:, b, tj, oc3 * 128:(oc3 + 1) * 128],
                                in0=pot[:, pb2, 0:128],
                                in1=xres[:, b, tj, oc3 * 128:(oc3 + 1) * 128], op=ALU.add)
                pot_free[pb2] = kb.sdv.n
            of_free[par] = (kb.spe, kb.spe.n)
            if oin == 1:
                slab_free[pr] = kb.spe.n
                pump_slabs(pos + 3)

        nc.scalar.wait_ge(kb.sdv.h, kb.sdv.n)
        out_ap = dst[ti * TT:(ti + 1) * TT, :].rearrange("(j p) f -> p j f", p=128)
        stsem[b].inc(nc.scalar.dma_start(out=out_ap, in_=xres[:, b, :, :]), 16)
        store_done.append((stsem[b], stsem[b].n))

    kb.barrier()
    es.close()




def emit_l1attn(kb, src, dst, w, bof, c_mdiag, c_mold, c_ones1, ident, identf):
    nc = kb.nc
    v, s, g = kb.v, kb.s, kb.g
    te, sy = nc.tensor, nc.sync
    es = ExitStack()
    pfx = "at1" + kb.uid() + "_"
    sb = lambda n, sh, dt: kb.sb(es, pfx + n, sh, dt)
    psal = lambda n, sh, dt: kb.psum(es, pfx + n, sh, dt)

    NR = 6
    xres = sb("xres", [128, 2, D], F32)
    xhat = sb("xhat", [128, D], BF16)
    xhT = sb("xhT", [128, 2, 8, 128], BF16)
    qf = sb("qf", [128, 2, 8, 128], BF16)
    kfr = sb("kfr", [128, 8, NR, 128], BF16)
    vtr = sb("vtr", [128, NR, D], BF16)
    pbf = sb("pbf", [128, NH, 640], BF16)
    pnT = sb("pnT", [128, 5, 128], BF16)
    yhT = sb("yhT", [128, 8, 128], BF16)
    mxh = sb("mxh", [128, NH], F32)
    smh = sb("smh", [128, NH], F32)
    rsh = sb("rsh", [128, NH], F32)
    wqk = sb("wqk", [128, 128, 128], BF16)
    wv_sb = sb("wv", [128, 8, D], BF16)
    wou = sb("wou", [128, 8, D], BF16)
    md = sb("md", [128, 128], F32)
    mo = sb("mo", [128, 128], F32)
    bor = sb("bor", [1, D], F32)
    onesr = sb("onesr", [1, 128], F32)
    bb = sb("bb", [128, D], F32)
    ln = LNScratch(kb, es, pfx + "ln", 1)

    pA = psal("pA", [128, 2, 1024], F32)
    pB = psal("pB", [128, 2, 512], F32)
    ptr = [psal("ptrA", [128, 1024], BF16), psal("ptrB", [128, 1024], BF16)]

    sy.wait_ge(kb.sw2.h, kb.sw2.n)
    sy.wait_ge(kb.swg.h, w["marker"])
    kb.sw2.inc(sy.dma_start(out=wqk[:], in_=w["qk"].rearrange("(a p o) -> p a o", p=128, a=128)), 16)
    kb.sw2.inc(sy.dma_start(out=wv_sb[:], in_=w["v"].rearrange("(fc p o) -> p fc o", p=128, fc=8)), 16)
    kb.sw2.inc(sy.dma_start(out=wou[:], in_=w["wo"].rearrange("(oc p o) -> p oc o", p=128, oc=8)), 16)
    kb.sw2.inc(sy.dma_start(out=md[:], in_=c_mdiag[:, :]), 16)
    kb.sw2.inc(sy.dma_start(out=mo[:], in_=c_mold[:, :]), 16)
    kb.sw2.inc(sy.dma_start(out=bor[:], in_=bof[:, :]), 16)
    kb.sw2.inc(sy.dma_start(out=onesr[:], in_=c_ones1[:, :]), 16)
    w_ready = kb.sw2.n
    te.wait_ge(kb.sw2.h, w_ready)
    for oh in range(2):
        te.wait_ge(kb.spe.h, kb.spe.n)
        last = te.matmul(pB[:, oh, :], lhsT=onesr[0:1, :], rhs=bor[0:1, oh * 512:(oh + 1) * 512],
                         start=True, stop=True)
    nbb = kb.spe.inc(last)
    v.wait_ge(kb.spe.h, nbb)
    v.tensor_copy(out=bb[:, 0:512], in_=pB[:, 0, :])
    v.tensor_copy(out=bb[:, 512:1024], in_=pB[:, 1, :])
    bb_dv = kb.sdv.n

    xsem = [kb.sx, kb.sxb]
    stsem = [kb.sst, kb.sstb]
    store_done = []
    x_ready = {}
    ptr_free = {0: None, 1: None}
    pa_free = {0: 0, 1: 0}
    pb_free = {0: (kb.sdv, bb_dv), 1: (kb.sdv, bb_dv)}
    kv_free = {}
    a_done = {}
    pbf_war = 0

    def emit_x_load(c):
        b = c % 2
        if c >= 2:
            sy.wait_ge(store_done[c - 2][0].h, store_done[c - 2][1])
        xsem[b].inc(sy.dma_start(out=xres[:, b, :],
                                 in_=src[c * 128:(c + 1) * 128, :]), 16)
        x_ready[c] = (xsem[b], xsem[b].n)

    def emit_proj(c):
        b = c % 2
        sl = c % NR
        v.wait_ge(x_ready[c][0].h, x_ready[c][1])
        s.wait_ge(x_ready[c][0].h, x_ready[c][1])
        emit_ln_stats(kb, ln, xres[:, b, :].unsqueeze(1), 1)
        s.wait_ge(kb.sdv.h, kb.sdv.n)
        s.wait_ge(kb.spe.h, kb.spe.n)
        s.activation(out=xhat[:], in_=xres[:, b, :], func=AF.Identity,
                     scale=ln.rstd[:, 0:1], bias=ln.bneg[:, 0:1])
        for grp in range(2):
            pb_ = grp
            if ptr_free[pb_] is not None:
                te.wait_ge(ptr_free[pb_][0].h, ptr_free[pb_][1])
            te.wait_ge(kb.sac.h, kb.sac.n)
            for k in range(4):
                fc = grp * 4 + k
                if k == 3:
                    te.wait_ge(kb.spe.h, kb.spe.n)
                last = te.transpose(out=ptr[pb_][:, k * 128:(k + 1) * 128],
                                    in_=xhat[:, fc * 128:(fc + 1) * 128], identity=ident)
            np_ = kb.spe.inc(last)
            s.wait_ge(kb.spe.h, np_)
            s.copy(out=xhT[:, b, grp * 4:(grp + 1) * 4, :], in_=ptr[pb_][:, 0:512])
            ptr_free[pb_] = (kb.sac, kb.sac.n)
        te.wait_ge(kb.sac.h, kb.sac.n)
        for oc in range(16):
            slot = oc % 2
            if pb_free[slot] is not None:
                te.wait_ge(pb_free[slot][0].h, pb_free[slot][1])
            if oc >= 8 and kv_free.get(sl) is not None:
                te.wait_ge(kv_free[sl][0].h, kv_free[sl][1])
            for fc in range(8):
                if fc == 7:
                    te.wait_ge(kb.spe.h, kb.spe.n)
                last = te.matmul(pB[:, slot, 0:128], lhsT=wqk[:, oc * 8 + fc, :],
                                 rhs=xhT[:, b, fc, :], start=(fc == 0), stop=(fc == 7))
            np_ = kb.spe.inc(last)
            v.wait_ge(kb.spe.h, np_)
            if oc < 8:
                v.tensor_scalar(out=qf[:, b, oc, :], in0=pB[:, slot, 0:128], scalar1=0.125,
                                scalar2=None, op0=ALU.mult)
            else:
                v.tensor_copy(out=kfr[:, oc - 8, sl, :], in_=pB[:, slot, 0:128])
            pb_free[slot] = (kb.sdv, kb.sdv.n)
        for oh in range(2):
            slot = oh
            if pb_free[slot] is not None:
                te.wait_ge(pb_free[slot][0].h, pb_free[slot][1])
            if kv_free.get(sl) is not None:
                te.wait_ge(kv_free[sl][0].h, kv_free[sl][1])
            for fc in range(8):
                if fc == 7:
                    te.wait_ge(kb.spe.h, kb.spe.n)
                last = te.matmul(pB[:, slot, :], lhsT=xhT[:, b, fc, :],
                                 rhs=wv_sb[:, fc, oh * 512:(oh + 1) * 512],
                                 start=(fc == 0), stop=(fc == 7))
            np_ = kb.spe.inc(last)
            v.wait_ge(kb.spe.h, np_)
            v.tensor_copy(out=vtr[:, sl, oh * 512:(oh + 1) * 512], in_=pB[:, slot, :])
            pb_free[slot] = (kb.sdv, kb.sdv.n)
        a_done[c] = kb.sdv.n

    emit_x_load(0)
    emit_x_load(1)
    emit_proj(0)

    for c in range(NCH):
        b = c % 2
        nsl = min(c, 4) + 1
        kcs = list(range(c - nsl + 1, c + 1))
        used = nsl * 128
        moff = (4 - c) * 128 if c < 4 else 0
        merged = []
        gi = 0
        i = 0
        while i < len(kcs):
            sl_ = kcs[i] % NR
            cnt = 1
            while (i + cnt < len(kcs) and (kcs[i + cnt] % NR) == sl_ + cnt and
                   (gi + cnt) % 4 != 0):
                cnt += 1
            merged.append((sl_, gi, cnt))
            gi += cnt
            i += cnt
        for h in range(NH):
            hc, par = h // 2, h % 2
            hr = 64 * par
            te.wait_ge(kb.sac.h, pa_free[par] if h >= 2 else max(pa_free[par], 0))
            if h < 2:
                te.wait_ge(kb.sdv.h, a_done[c])
            for ri, (sl_, gi0, cnt) in enumerate(merged):
                if ri == len(merged) - 1:
                    te.wait_ge(kb.spe.h, kb.spe.n)
                last = te.matmul(pA[:, par, gi0 * 128:(gi0 + cnt) * 128],
                                 lhsT=qf[hr:hr + 64, b, hc, :],
                                 rhs=kfr[hr:hr + 64, hc, sl_:sl_ + cnt, :],
                                 start=True, stop=True)
            nsc_ = kb.spe.inc(last)
            v.wait_ge(kb.spe.h, nsc_)
            v.tensor_tensor(out=pA[:, par, (nsl - 1) * 128:nsl * 128],
                            in0=pA[:, par, (nsl - 1) * 128:nsl * 128], in1=md[:, :],
                            op=ALU.add)
            if c >= 4:
                v.tensor_tensor(out=pA[:, par, 0:128], in0=pA[:, par, 0:128], in1=mo[:, :],
                                op=ALU.add)
            v.tensor_reduce(out=mxh[:, h:h + 1], in_=pA[:, par, 0:used], axis=AX.X, op=ALU.max)
            v.tensor_scalar(out=mxh[:, h:h + 1], in0=mxh[:, h:h + 1], scalar1=-1.0,
                            scalar2=None, op0=ALU.mult)
            s.wait_ge(kb.sdv.h, kb.sdv.n)
            if h < 2:
                s.wait_ge(kb.spe.h, pbf_war)
            s.activation(out=pbf[:, h, 0:used], in_=pA[:, par, 0:used], func=AF.Exp,
                         bias=mxh[:, h:h + 1], scale=1.0, accum_out=smh[:, h:h + 1])
            pa_free[par] = kb.sac.n
        exp_done = kb.sac.n
        v.wait_ge(kb.sac.h, exp_done)
        v.reciprocal(out=rsh[:, 0:NH], in_=smh[:, 0:NH])
        v.tensor_tensor(out=pbf[:, :, 0:used], in0=pbf[:, :, 0:used],
                        in1=rsh[:, 0:NH].unsqueeze(2).broadcast_to([128, NH, used]),
                        op=ALU.mult)
        pn_done = ("dv", kb.sdv.n)
        if c + 1 < NCH:
            emit_proj(c + 1)
        te.wait_ge(kb.sdv.h, pn_done[1])
        for h in range(NH):
            hc, par = h // 2, h % 2
            hr = 64 * par
            pb_t = par
            if ptr_free[pb_t] is not None:
                te.wait_ge(ptr_free[pb_t][0].h, ptr_free[pb_t][1])
            for gi in range(nsl):
                if gi == nsl - 1:
                    te.wait_ge(kb.spe.h, kb.spe.n)
                last = te.transpose(out=ptr[pb_t][:, gi * 128:(gi + 1) * 128],
                                    in_=pbf[:, h, gi * 128:(gi + 1) * 128], identity=ident)
            ntp = kb.spe.inc(last)
            v.wait_ge(kb.spe.h, ntp)
            v.tensor_copy(out=pnT[:, 0:nsl, :], in_=ptr[pb_t][:, 0:used])
            ptr_free[pb_t] = (kb.sdv, kb.sdv.n)
            pnt_dv = kb.sdv.n
            slot = par
            if pb_free[slot] is not None:
                te.wait_ge(pb_free[slot][0].h, pb_free[slot][1])
            te.wait_ge(kb.sdv.h, pnt_dv)
            for gi, kc in enumerate(kcs):
                if gi == nsl - 1:
                    te.wait_ge(kb.spe.h, kb.spe.n)
                last = te.matmul(pB[0:64, slot, 0:128], lhsT=vtr[:, kc % NR, 64 * h:64 * h + 64],
                                 rhs=pnT[:, gi, :], start=(gi == 0), stop=(gi == nsl - 1))
            nctx = kb.spe.inc(last)
            s.wait_ge(kb.spe.h, nctx)
            s.copy(out=yhT[hr:hr + 64, hc, :], in_=pB[0:64, slot, 0:128])
            pb_free[slot] = (kb.sac, kb.sac.n)
        pbf_war = kb.spe.n
        kv_free[(c - 4) % NR if c >= 4 else (c + NR - 4) % NR] = (kb.spe, kb.spe.n)
        yh_sac = kb.sac.n
        te.wait_ge(kb.sac.h, yh_sac)
        if c == 0:
            v.wait_ge(kb.sdv.h, bb_dv)
        for oh in range(2):
            slot = oh
            if pb_free[slot] is not None:
                te.wait_ge(pb_free[slot][0].h, pb_free[slot][1])
            for oc in range(8):
                if oc == 7:
                    te.wait_ge(kb.spe.h, kb.spe.n)
                last = te.matmul(pB[:, slot, :], lhsT=yhT[:, oc, :],
                                 rhs=wou[:, oc, oh * 512:(oh + 1) * 512],
                                 start=(oc == 0), stop=(oc == 7))
            np_ = kb.spe.inc(last)
            v.wait_ge(kb.spe.h, np_)
            v.tensor_tensor(out=xres[:, b, oh * 512:(oh + 1) * 512], in0=pB[:, slot, :],
                            in1=xres[:, b, oh * 512:(oh + 1) * 512], op=ALU.add)
            v.tensor_tensor(out=xres[:, b, oh * 512:(oh + 1) * 512],
                            in0=xres[:, b, oh * 512:(oh + 1) * 512],
                            in1=bb[:, oh * 512:(oh + 1) * 512], op=ALU.add)
            pb_free[slot] = (kb.sdv, kb.sdv.n)
        nc.scalar.wait_ge(kb.sdv.h, kb.sdv.n)
        stsem[b].inc(nc.scalar.dma_start(out=dst[c * 128:(c + 1) * 128, :],
                                         in_=xres[:, b, :]), 16)
        store_done.append((stsem[b], stsem[b].n))
        if c + 2 < NCH:
            emit_x_load(c + 2)

    kb.barrier()
    es.close()



def emit_quant_ffn0_weights(kb, wg_nat, wu_nat, wd_nat, s_wg, s_wu, s_wd, ident):
    nc = kb.nc
    v, s, g = kb.v, kb.s, kb.g
    te, sy = nc.tensor, nc.sync
    es = ExitStack()
    pfx = "q0w" + kb.uid() + "_"
    sb = lambda n, sh, dt: kb.sb(es, pfx + n, sh, dt)
    psal = lambda n, sh, dt: kb.psum(es, pfx + n, sh, dt)

    wnat = sb("wnat", [128, 2, I_FF0], F32)
    qb = sb("qb", [128, 2, I_FF0], BF16)
    accg = sb("accg", [128, 8, 512], BF16)
    accd = sb("accd", [128, N_IC0, D], BF16)
    qsc = [QuantScratch(kb, es, pfx + "qA", I_FF0), QuantScratch(kb, es, pfx + "qB", I_FF0)]
    ptr = [psal("ptrA", [128, 1024], BF16), psal("ptrB", [128, 1024], BF16)]

    xsem = [kb.sx, kb.sxb]
    ptr_free = {0: None, 1: None}
    nat_free = {0: None, 1: None}
    acc_free = 0
    qb_spe = {0: 0, 1: 0}

    sy.wait_ge(kb.sw2.h, kb.sw2.n)

    def do_weight(nat, scratch, O, F, is_down):
        nonlocal acc_free
        nrc = O // 128
        nfc = F // 128

        def finish(rc):
            par = rc % 2
            v.wait_ge(kb.spe.h, qb_spe[par])
            dv = emit_quant_back(kb, qsc[par], qb[:, par, 0:F].unsqueeze(1), 1, F)
            ngrp = (nfc + 3) // 4
            for grp in range(ngrp):
                pb = grp % 2
                n_in = min(4, nfc - grp * 4)
                if ptr_free[pb] is not None:
                    te.wait_ge(ptr_free[pb][0].h, ptr_free[pb][1])
                if grp == 0:
                    te.wait_ge(kb.sdv.h, dv)
                for k in range(n_in):
                    fc = grp * 4 + k
                    if k == n_in - 1:
                        te.wait_ge(kb.spe.h, kb.spe.n)
                    last = te.transpose(out=ptr[pb][:, k * 128:(k + 1) * 128],
                                        in_=qb[:, par, fc * 128:(fc + 1) * 128], identity=ident)
                np_ = kb.spe.inc(last)
                s.wait_ge(kb.spe.h, np_)
                if is_down:
                    s.copy(out=accd[:, grp * 4:grp * 4 + n_in, rc * 128:(rc + 1) * 128],
                           in_=ptr[pb][:, 0:n_in * 128])
                else:
                    og = rc // 4
                    ow = (rc % 4) * 128
                    if grp == 0:
                        s.wait_ge(kb.sw2.h, acc_free)
                    s.copy(out=accg[:, grp * 4:grp * 4 + n_in, ow:ow + 128],
                           in_=ptr[pb][:, 0:n_in * 128])
                ptr_free[pb] = (kb.sac, kb.sac.n)
            qb_spe[par] = kb.spe.n

        def flush(rc):
            nonlocal acc_free
            if not is_down:
                og = rc // 4
                if (rc % 4 == 3) or (rc == nrc - 1):
                    szo = OG_SIZES[og]
                    off = 128 * 8 * OG_OFF[og]
                    nc.sync.wait_ge(kb.sac.h, kb.sac.n)
                    kb.sw2.inc(sy.dma_start(
                        out=scratch[off:off + 128 * 8 * szo].rearrange(
                            "(p fc o) -> p fc o", p=128, fc=8),
                        in_=accg[:, :, 0:szo]), 16)
                    acc_free = kb.sw2.n

        for rc in range(nrc):
            par = rc % 2
            if nat_free[par] is not None:
                sy.wait_ge(nat_free[par][0].h, nat_free[par][1])
            xsem[par].inc(sy.dma_start(
                out=wnat[:, par, 0:F],
                in_=nat[rc * 128 * F:(rc + 1) * 128 * F].rearrange("(p f) -> p f", p=128)), 16)
            ld = (xsem[par], xsem[par].n)
            v.wait_ge(ld[0].h, ld[1])
            s.wait_ge(ld[0].h, ld[1])
            emit_quant_front(kb, qsc[par], wnat[:, par, 0:F].unsqueeze(1), 1, F)
            nat_free[par] = (kb.sdv, kb.sdv.n)
            if rc > 0:
                finish(rc - 1)
                flush(rc - 1)
        finish(nrc - 1)
        flush(nrc - 1)
        if is_down:
            nc.sync.wait_ge(kb.sac.h, kb.sac.n)
            kb.sw2.inc(sy.dma_start(
                out=scratch[:].rearrange("(p ic o) -> p ic o", p=128, ic=N_IC0),
                in_=accd[:, :, :]), 16)

    do_weight(wg_nat, s_wg, I_FF0, D, False)
    do_weight(wu_nat, s_wu, I_FF0, D, False)
    do_weight(wd_nat, s_wd, D, I_FF0, True)
    qmarker = kb.sw2.n
    kb.barrier()
    es.close()
    return qmarker


def build_program(phases=("ff0a",)):
    kb = KB()
    nc = kb.nc
    es = kb.es

    x_in = nc.dram_tensor("x_in", [T, D], F32, kind="ExternalInput").ap()
    out = nc.dram_tensor("out", [T, D], F32, kind="ExternalOutput").ap()
    c_ident = nc.dram_tensor("c_ident", [128, 128], BF16, kind="ExternalInput").ap()
    c_identf = nc.dram_tensor("c_identf", [128, 128], F32, kind="ExternalInput").ap()

    n_gu = 128 * 8 * I_FF0
    n_wd = 128 * N_IC0 * D
    wg_nat = nc.dram_tensor("wg_nat", [I_FF0 * D], F32, kind="ExternalInput").ap()
    wu_nat = nc.dram_tensor("wu_nat", [I_FF0 * D], F32, kind="ExternalInput").ap()
    wd_nat = nc.dram_tensor("wd_nat", [D * I_FF0], F32, kind="ExternalInput").ap()
    s_wg = nc.dram_tensor("s_wg", [n_gu], BF16, kind="Internal").ap()
    s_wu = nc.dram_tensor("s_wu", [n_gu], BF16, kind="Internal").ap()
    s_wd = nc.dram_tensor("s_wd", [n_wd], BF16, kind="Internal").ap()

    xA = nc.dram_tensor("xA", [T, D], F32, kind="Internal").ap()
    xB = nc.dram_tensor("xB", [T, D], F32, kind="Internal").ap()

    wqf = nc.dram_tensor("wqf", [64 * 128 * 128], F32, kind="ExternalInput").ap()
    wkf = nc.dram_tensor("wkf", [64 * 128 * 128], F32, kind="ExternalInput").ap()
    wvf = nc.dram_tensor("wvf", [8 * 128 * D], F32, kind="ExternalInput").ap()
    wof = nc.dram_tensor("wof", [8 * 128 * D], F32, kind="ExternalInput").ap()
    gnwf = nc.dram_tensor("gnwf", [128, 8], F32, kind="ExternalInput").ap()
    gnbf = nc.dram_tensor("gnbf", [128, 8], F32, kind="ExternalInput").ap()
    c_dtc = nc.dram_tensor("c_dtc", [128, NH, 128], F32, kind="ExternalInput").ap()
    c_lamc = nc.dram_tensor("c_lamc", [128, 8, 128], F32, kind="ExternalInput").ap()
    c_muc = nc.dram_tensor("c_muc", [128, 8, 128], F32, kind="ExternalInput").ap()
    c_glcc = nc.dram_tensor("c_glcc", [128, 8], F32, kind="ExternalInput").ap()
    c_b2 = nc.dram_tensor("c_b2", [2, 128], F32, kind="ExternalInput").ap()
    c_ones2 = nc.dram_tensor("c_ones2", [128, 2], F32, kind="ExternalInput").ap()
    s_wq = nc.dram_tensor("s_wq", [64 * 128 * 128], BF16, kind="Internal").ap()
    s_wk = nc.dram_tensor("s_wk", [64 * 128 * 128], BF16, kind="Internal").ap()
    s_wv = nc.dram_tensor("s_wv", [8 * 128 * D], BF16, kind="Internal").ap()
    s_wo = nc.dram_tensor("s_wo", [8 * 128 * D], BF16, kind="Internal").ap()

    n_w12 = 32 * 8 * 128 * 128
    n_h12 = 32 * 32 * 128 * 128
    n_w3 = 8 * 32 * 128 * 128
    w1f = nc.dram_tensor("w1f", [n_w12], F32, kind="ExternalInput").ap()
    w2f = nc.dram_tensor("w2f", [n_w12], F32, kind="ExternalInput").ap()
    h1f = nc.dram_tensor("h1f", [n_h12], F32, kind="ExternalInput").ap()
    h2f = nc.dram_tensor("h2f", [n_h12], F32, kind="ExternalInput").ap()
    w3f = nc.dram_tensor("w3f", [n_w3], F32, kind="ExternalInput").ap()
    s1f = nc.dram_tensor("s1f", [128, 32], F32, kind="ExternalInput").ap()
    s2f = nc.dram_tensor("s2f", [128, 32], F32, kind="ExternalInput").ap()
    sh1f = nc.dram_tensor("sh1f", [128, 32], F32, kind="ExternalInput").ap()
    sh2f = nc.dram_tensor("sh2f", [128, 32], F32, kind="ExternalInput").ap()
    s3f = nc.dram_tensor("s3f", [128, 8], F32, kind="ExternalInput").ap()
    n_qk = 16 * 8 * 128 * 128
    n_qv = 8 * 128 * D
    qkf = nc.dram_tensor("qkf", [n_qk], F32, kind="ExternalInput").ap()
    qvf = nc.dram_tensor("qvf", [n_qv], F32, kind="ExternalInput").ap()
    wouf = nc.dram_tensor("wouf", [n_qv], F32, kind="ExternalInput").ap()
    bof = nc.dram_tensor("bof", [1, D], F32, kind="ExternalInput").ap()
    c_wmask = nc.dram_tensor("c_wmask", [128, 640], F32, kind="ExternalInput").ap()
    c_mdiag = nc.dram_tensor("c_mdiag", [128, 128], F32, kind="ExternalInput").ap()
    c_mold = nc.dram_tensor("c_mold", [128, 128], F32, kind="ExternalInput").ap()
    c_ones1 = nc.dram_tensor("c_ones1", [1, 128], F32, kind="ExternalInput").ap()
    s_qk = nc.dram_tensor("s_qk", [n_qk], BF16, kind="Internal").ap()
    s_qv = nc.dram_tensor("s_qv", [n_qv], BF16, kind="Internal").ap()
    s_wou = nc.dram_tensor("s_wou", [n_qv], BF16, kind="Internal").ap()

    s_w1 = nc.dram_tensor("s_w1", [n_w12], BF16, kind="Internal").ap()
    s_w2 = nc.dram_tensor("s_w2", [n_w12], BF16, kind="Internal").ap()
    s_h1 = nc.dram_tensor("s_h1", [n_h12], BF16, kind="Internal").ap()
    s_h2 = nc.dram_tensor("s_h2", [n_h12], BF16, kind="Internal").ap()
    s_w3 = nc.dram_tensor("s_w3", [n_w3], BF16, kind="Internal").ap()

    ident = kb.sb(es, "ident", [128, 128], BF16)
    identf = kb.sb(es, "identf", [128, 128], F32)
    kb._barr = kb.sb(es, "barr", [128, 4], F32)

    c_scal = nc.dram_tensor("c_scal", [128, len(ACT_CONSTS)], F32, kind="ExternalInput").ap()
    cpool = kb.sb(es, "cpool", [128, len(ACT_CONSTS)], F32)
    for ci, val in enumerate(ACT_CONSTS):
        nc.const_aps.register(F32, val, cpool[:, ci:ci + 1])

    g_ = nc.gpsimd
    kb.sw2.inc(nc.sync.dma_start(out=ident[:], in_=c_ident[:, :]), 16)
    kb.sw2.inc(nc.sync.dma_start(out=identf[:], in_=c_identf[:, :]), 16)
    kb.sw2.inc(nc.sync.dma_start(out=cpool[:], in_=c_scal[:, :]), 16)
    for eng in [nc.tensor, nc.vector, nc.scalar, nc.gpsimd]:
        eng.wait_ge(kb.sw2.h, kb.sw2.n)

    CONV = 1 << 21
    conv_groups = [
        ("at0", [(s_wq, wqf, 64 * 128 * 128), (s_wk, wkf, 64 * 128 * 128),
                 (s_wv, wvf, 8 * 128 * D), (s_wo, wof, 8 * 128 * D)]),
        ("at1", [(s_qk, qkf, n_qk), (s_qv, qvf, n_qv), (s_wou, wouf, n_qv)]),
        ("ff1", [(s_w1, w1f, n_w12), (s_w2, w2f, n_w12), (s_h1, h1f, n_h12),
                 (s_h2, h2f, n_h12), (s_w3, w3f, n_w3)]),
    ]
    conv_marker = {}
    for gname, items in conv_groups:
        g_.wait_ge(kb.swg.h, kb.swg.n)
        for dst_f, src_f, n in items:
            for off in range(0, n, CONV):
                sz = min(CONV, n - off)
                kb.swg.inc(g_.dma_start(out=dst_f[off:off + sz], in_=src_f[off:off + sz]), 16)
        conv_marker[gname] = kb.swg.n

    emit_quant_ffn0_weights(kb, wg_nat, wu_nat, wd_nat, s_wg, s_wu, s_wd, ident[:])

    chain = list(phases)
    n_ph = len(chain)
    for i, kind in enumerate(chain):
        src = x_in if i == 0 else (xA if i % 2 == 1 else xB)
        dst = out if i == n_ph - 1 else (xA if i % 2 == 0 else xB)
        if kind == "ff0":
            emit_l0ffn(kb, src, dst, s_wg, s_wu, s_wd, ident[:])
        elif kind == "at0":
            at0w = {"wq": s_wq, "wk": s_wk, "wv": s_wv, "wo": s_wo,
                    "marker": conv_marker["at0"]}
            emit_l0attn(kb, src, dst, at0w, gnwf, gnbf, c_dtc, c_lamc,
                        c_muc, c_glcc, c_b2, c_ones2, ident[:], identf[:])
        elif kind == "ff1":
            ff1w = {"w1": s_w1, "w2": s_w2, "h1": s_h1, "h2": s_h2, "w3": s_w3,
                    "marker": conv_marker["ff1"]}
            emit_l1ffn(kb, src, dst, ff1w, s1f, s2f, sh1f, sh2f, s3f, ident[:], identf[:])
        elif kind == "at1":
            at1w = {"qk": s_qk, "v": s_qv, "wo": s_wou, "marker": conv_marker["at1"]}
            emit_l1attn(kb, src, dst, at1w, bof, c_mdiag, c_mold, c_ones1, ident[:], identf[:])

    return kb


_PREP_CACHE = {}


def prep_shared(inputs):
    c = host_consts()
    d = {
        "c_ident": c["c_ident"],
        "c_identf": np.eye(128, dtype=np.float32),
        "c_scal": np.tile(np.array(ACT_CONSTS, dtype=np.float32)[None, :], (128, 1)),
        "wg_nat": np.ascontiguousarray(inputs["L0_ffn_w_gate"]).ravel(),
        "wu_nat": np.ascontiguousarray(inputs["L0_ffn_w_up"]).ravel(),
        "wd_nat": np.ascontiguousarray(inputs["L0_ffn_w_down"]).ravel(),
        "wqf": tiles_oc_fc(inputs["L0_attn_wq"]),
        "wkf": tiles_oc_fc(inputs["L0_attn_wk"]),
        "wvf": flat_pmajor_fcwide(inputs["L0_attn_wv"]),
        "wof": flat_oc_pmajor_wide(inputs["L0_attn_wo"]),
        "gnwf": np.ascontiguousarray(inputs["L0_attn_gn_w"].reshape(8, 128).T),
        "gnbf": np.ascontiguousarray(inputs["L0_attn_gn_b"].reshape(8, 128).T),
        "c_dtc": np.ascontiguousarray(c["c_dt"].transpose(1, 0, 2)),
        "c_lamc": np.ascontiguousarray(c["c_lam"].transpose(1, 0, 2)),
        "c_muc": np.ascontiguousarray(c["c_mu"].transpose(1, 0, 2)),
        "c_glcc": c["c_glc"],
        "c_b2": c["c_b2"],
        "c_ones2": c["c_ones2"],
        "w1f": tiles_lhsT_slabbed(inputs["L1_ffn_w1_w"], 4),
        "w2f": tiles_lhsT_slabbed(inputs["L1_ffn_w2_w"], 4),
        "h1f": tiles_lhsT_slabbed(inputs["L1_ffn_h1_w"], 2),
        "h2f": tiles_lhsT_slabbed(inputs["L1_ffn_h2_w"], 2),
        "w3f": tiles_lhsT_slabbed(inputs["L1_ffn_w3_w"], 2),
        "s1f": np.ascontiguousarray(inputs["L1_ffn_w1_s"].reshape(32, 128).T),
        "s2f": np.ascontiguousarray(inputs["L1_ffn_w2_s"].reshape(32, 128).T),
        "sh1f": np.ascontiguousarray(inputs["L1_ffn_h1_s"].reshape(32, 128).T),
        "sh2f": np.ascontiguousarray(inputs["L1_ffn_h2_s"].reshape(32, 128).T),
        "s3f": np.ascontiguousarray(inputs["L1_ffn_w3_s"].reshape(8, 128).T),
        "qkf": tiles_oc_fc(inputs["L1_attn_w_qkv"][0:2048]),
        "qvf": flat_pmajor_fcwide(inputs["L1_attn_w_qkv"][2048:3072]),
        "wouf": flat_pmajor_fcwide(inputs["L1_attn_w_out"]),
        "bof": inputs["L1_attn_b_out"].reshape(1, D),
        "c_wmask": c["c_wmask"],
        "c_mdiag": c["c_mdiag"],
        "c_mold": c["c_mold"],
        "c_ones1": c["c_ones1"],
    }
    return d


def prep_inputs_core(inputs, core, shared=None):
    if shared is None:
        shared = prep_shared(inputs)
    d = dict(shared)
    d["x_in"] = np.ascontiguousarray(inputs["L0_attn_x"][core])
    return d


def kernel(**inputs):
    kb = build_program(phases=("ff0", "at0", "ff0", "ff1", "at1", "ff1"))
    shared = prep_shared(inputs)
    in_maps = [prep_inputs_core(inputs, c, shared) for c in range(NCORE)]
    res = run_bass_kernel_spmd(kb.nc, in_maps, core_ids=list(range(NCORE)))
    return np.stack([res.results[c]["out"] for c in range(NCORE)], axis=0)


# version 11
# speedup vs baseline: 1.0713x; 1.0713x over previous
import math
from contextlib import ExitStack

import numpy as np
import ml_dtypes

import concourse.bass as bass
import concourse.mybir as mybir
from concourse.bass_utils import run_bass_kernel_spmd

F32 = mybir.dt.float32
BF16 = mybir.dt.bfloat16
AF = mybir.ActivationFunctionType
ALU = mybir.AluOpType
AX = mybir.AxisListType

T = 4096
D = 1024
NCORE = 8
I_FF0 = 2816
I_FF1 = 4096
NH = 16
HD = 64
WIN = 512
CH = 128
NCH = T // CH
LN_EPS = 1e-5
GN_EPS = 1e-5
NEG = -1.0e30
BF = ml_dtypes.bfloat16

ACT_CONSTS = [0.0, 1e-5, -4.0, -6.0, 1.0, -1.0, 0.5]

OG_SIZES = [512, 512, 512, 512, 512, 256]
OG_OFF = [0, 512, 1024, 1536, 2048, 2560]
N_IC0 = I_FF0 // 128


def gammas_np():
    lo = math.log(1.0 - 0.999)
    hi = math.log(1.0 - 0.9)
    return (1.0 - np.exp(np.linspace(lo, hi, NH, dtype=np.float32))).astype(np.float64)


def host_consts():
    g = gammas_np()
    c = {}
    c["c_ident"] = np.eye(128, dtype=BF)
    idx = np.arange(CH)
    dist = idx[None, :] - idx[:, None]
    dt = np.where(dist >= 0, np.power(g[:, None, None], -(idx[:, None] + 1.0)[None]), 0.0)
    c["c_dt"] = dt.astype(np.float32)
    lam = np.zeros((8, 128, CH))
    mu = np.zeros((8, CH, 128))
    glc = np.zeros((128, 8))
    for oc in range(8):
        for p in range(128):
            lam[oc, p, :] = np.power(g[2 * oc + p // 64], idx + 1)
            glc[p, oc] = np.power(g[2 * oc + p // 64], CH)
        for fcol in range(128):
            mu[oc, :, fcol] = np.power(g[2 * oc + fcol // 64], CH - 1 - idx)
    c["c_lam"] = lam.astype(np.float32)
    c["c_mu"] = mu.astype(np.float32)
    c["c_glc"] = glc.astype(np.float32)
    qi = np.arange(128)[:, None]
    kj = np.arange(640)[None, :]
    c["c_wmask"] = np.where((kj > qi) & (kj <= qi + WIN), 0.0, NEG).astype(np.float32)
    jj = np.arange(128)[None, :]
    c["c_mdiag"] = np.where(jj <= qi, 0.0, NEG).astype(np.float32)
    c["c_mold"] = np.where(jj > qi, 0.0, NEG).astype(np.float32)
    c["c_ones1"] = np.ones((1, 128), dtype=np.float32)
    c["c_b2"] = np.stack([(np.arange(128) < 64), (np.arange(128) >= 64)]).astype(np.float32)
    c["c_ones2"] = np.ascontiguousarray(c["c_b2"].T)
    return c


def flat_og_pmajor(w):
    O, F = w.shape
    wt = w.T.reshape(F // 128, 128, O)
    blocks = []
    for off, sz in zip(OG_OFF, OG_SIZES):
        blocks.append(np.ascontiguousarray(wt[:, :, off:off + sz].transpose(1, 0, 2)).ravel())
    return np.concatenate(blocks)


def flat_pmajor_icwide(w):
    O, F = w.shape
    wt = w.T.reshape(F // 128, 128, O).transpose(1, 0, 2)
    return np.ascontiguousarray(wt).ravel()


def tiles_oc_fc(w):
    O, F = w.shape
    wt = w.T.reshape(F // 128, 128, O // 128, 128)
    return np.ascontiguousarray(wt.transpose(2, 0, 1, 3)).ravel()


def flat_pmajor_fcwide(w):
    O, F = w.shape
    return np.ascontiguousarray(w.T.reshape(F // 128, 128, O)).ravel()


def flat_oc_pmajor_wide(w):
    return flat_pmajor_fcwide(w)


def tiles_lhsT_slabbed(w, ocps):
    O, F = w.shape
    arr = w.T.reshape(F // 128, 128, O // 128, 128).transpose(2, 0, 1, 3)
    arr = arr.reshape(O // 128 // ocps, ocps, F // 128, 128, 128).transpose(0, 3, 1, 2, 4)
    return np.ascontiguousarray(arr).ravel()


class Sem:
    def __init__(self, nc, es, name):
        self.h = es.enter_context(nc.semaphore(name))
        self.n = 0

    def inc(self, instr, k=1):
        instr.then_inc(self.h, k)
        self.n += k
        return self.n


class Eng:

    def __init__(self, raw, sem):
        self._raw = raw
        self._sem = sem

    def __getattr__(self, name):
        f = getattr(self._raw, name)

        def call(*a, **k):
            self._raw.wait_ge(self._sem.h, self._sem.n)
            ins = f(*a, **k)
            self._sem.inc(ins)
            return ins

        return call

    def wait_ge(self, h, n):
        self._raw.wait_ge(h, n)


class KB:
    def __init__(self):
        self.nc = bass.Bass("TRN2", target_bir_lowering=False)
        self.es = ExitStack()
        for name in ["sx", "sxb", "sw", "swb", "sw2", "sst", "sstb", "spe", "sdv", "sac", "spo", "scv", "swg"]:
            setattr(self, name, Sem(self.nc, self.es, name))
        self.phase_no = 0
        self._uid = 0
        self.v = Eng(self.nc.vector, self.sdv)
        self.s = Eng(self.nc.scalar, self.sac)
        self.g = Eng(self.nc.gpsimd, self.spo)

    def sb(self, es, name, shape, dtype):
        return es.enter_context(self.nc.sbuf_tensor(name, shape, dtype))

    def psum(self, es, name, shape, dtype):
        return es.enter_context(self.nc.psum_tensor(name, shape, dtype))

    def uid(self):
        self._uid += 1
        return str(self._uid)

    def barrier(self):
        nc = self.nc
        self.sdv.inc(nc.vector.memset(self._barr[:, 0:1], 0.0))
        self.sac.inc(nc.scalar.memzero(self._barr[:, 1:2]))
        nc.tensor.wait_ge(self.spe.h, self.spe.n)
        self.spe.inc(nc.tensor.nop())
        self.spo.inc(nc.gpsimd.memset(self._barr[:, 2:3], 0.0))
        nc.sync.wait_ge(self.sdv.h, self.sdv.n)
        nc.sync.wait_ge(self.sac.h, self.sac.n)
        nc.sync.wait_ge(self.spe.h, self.spe.n)
        nc.sync.wait_ge(self.spo.h, self.spo.n)
        nc.sync.wait_ge(self.sst.h, self.sst.n)
        nc.sync.wait_ge(self.sstb.h, self.sstb.n)
        nc.sync.wait_ge(self.sx.h, self.sx.n)
        nc.sync.wait_ge(self.sw.h, self.sw.n)
        nc.sync.wait_ge(self.sw2.h, self.sw2.n)
        nc.sync.wait_ge(self.swg.h, self.swg.n)
        self.phase_no += 1
        self.scv.inc(self.nc.sync.nop())
        for eng in [nc.vector, nc.scalar, nc.tensor, nc.gpsimd]:
            eng.wait_ge(self.scv.h, self.scv.n)


class QuantScratch:
    def __init__(self, kb, es, name, F):
        self.F = F
        self.bm = kb.sb(es, name + "_bm", [128, F // 16], F32)
        self.r = kb.sb(es, name + "_r", [128, F // 16], F32)
        self.xn = kb.sb(es, name + "_xn", [128, F], F32)
        self.absn = kb.sb(es, name + "_absn", [128, F], F32)
        self.sg = kb.sb(es, name + "_sg", [128, F], BF16)
        self.u = kb.sb(es, name + "_u", [128, F], BF16)
        self.r1 = kb.sb(es, name + "_r1", [128, F], BF16)
        self.r2 = kb.sb(es, name + "_r2", [128, F], BF16)


def emit_quant(kb, q, in3, out3, nj, sz, eng="v"):
    v, s = kb.v, kb.s
    e = kb.v if eng == "v" else kb.g
    esem = kb.sdv if eng == "v" else kb.spo
    F = nj * sz
    nb = F // 16
    r4 = lambda ap: ap.rearrange("p j (g k) -> p j g k", k=16)
    bm3 = q.bm[:, 0:nb].rearrange("p (j g) -> p j g", j=nj)
    fl = lambda buf: buf[:, 0:F]
    v.tensor_reduce(out=q.bm[:, 0:nb], in_=r4(in3), axis=AX.X, op=ALU.max,
                    apply_absolute_value=True)
    v.tensor_scalar(out=q.bm[:, 0:nb], in0=q.bm[:, 0:nb], scalar1=1.0 / 6.0, scalar2=1e-12,
                    op0=ALU.mult, op1=ALU.max)
    v.reciprocal(out=q.r[:, 0:nb], in_=q.bm[:, 0:nb])
    if eng != "v":
        e.wait_ge(kb.sdv.h, kb.sdv.n)
    e.tensor_tensor(out=r4(q.xn[:, 0:F].rearrange("p (j z) -> p j z", j=nj)), in0=r4(in3),
                    in1=q.r[:, 0:nb].rearrange("p (j g) -> p j g", j=nj).unsqueeze(3)
                    .broadcast_to([128, nj, sz // 16, 16]),
                    op=ALU.mult)
    s.wait_ge(esem.h, esem.n)
    s.activation(out=fl(q.absn), in_=fl(q.xn), func=AF.Abs)
    s.activation(out=fl(q.sg), in_=fl(q.xn), func=AF.Sign)
    e.wait_ge(kb.sac.h, kb.sac.n)
    if eng == "v":
        e.tensor_scalar(out=fl(q.u), in0=fl(q.absn), scalar1=0.25, scalar2=None, op0=ALU.is_gt)
        for th in (0.75, 1.25, 1.75, 2.5, 3.5, 5.0):
            e.scalar_tensor_tensor(out=fl(q.u), in0=fl(q.absn), scalar=th, in1=fl(q.u),
                                   op0=ALU.is_gt, op1=ALU.add)
    else:
        e.tensor_scalar(out=fl(q.u), in0=fl(q.absn), scalar1=0.25, scalar2=None, op0=ALU.is_gt)
        for th in (0.75, 1.25, 1.75, 2.5, 3.5, 5.0):
            e.tensor_scalar(out=fl(q.r1), in0=fl(q.absn), scalar1=th, scalar2=None, op0=ALU.is_gt)
            e.tensor_tensor(out=fl(q.u), in0=fl(q.u), in1=fl(q.r1), op=ALU.add)
    s.wait_ge(esem.h, esem.n)
    s.activation(out=fl(q.r1), in_=fl(q.u), func=AF.Relu, bias=-4.0, scale=1.0)
    s.activation(out=fl(q.r2), in_=fl(q.u), func=AF.Relu, bias=-6.0, scale=1.0)
    e.wait_ge(kb.sac.h, kb.sac.n)
    if eng == "v":
        e.scalar_tensor_tensor(out=fl(q.u), in0=fl(q.u), scalar=0.5, in1=fl(q.r2),
                               op0=ALU.mult, op1=ALU.add)
        e.scalar_tensor_tensor(out=fl(q.u), in0=fl(q.r1), scalar=0.5, in1=fl(q.u),
                               op0=ALU.mult, op1=ALU.add)
    else:
        e.tensor_scalar(out=fl(q.u), in0=fl(q.u), scalar1=0.5, scalar2=None, op0=ALU.mult)
        e.tensor_scalar(out=fl(q.r1), in0=fl(q.r1), scalar1=0.5, scalar2=None, op0=ALU.mult)
        e.tensor_tensor(out=fl(q.u), in0=fl(q.u), in1=fl(q.r2), op=ALU.add)
        e.tensor_tensor(out=fl(q.u), in0=fl(q.u), in1=fl(q.r1), op=ALU.add)
    e.tensor_tensor(out=fl(q.u), in0=fl(q.u), in1=fl(q.sg), op=ALU.mult)
    e.tensor_tensor(out=r4(out3), in0=r4(q.u[:, 0:F].rearrange("p (j z) -> p j z", j=nj)),
                    in1=q.bm[:, 0:nb].rearrange("p (j g) -> p j g", j=nj).unsqueeze(3)
                    .broadcast_to([128, nj, sz // 16, 16]),
                    op=ALU.mult)
    return esem.n


def emit_quant_front(kb, q, in3, nj, sz):
    v, s = kb.v, kb.s
    F = nj * sz
    nb = F // 16
    r4 = lambda ap: ap.rearrange("p j (g k) -> p j g k", k=16)
    v.tensor_reduce(out=q.bm[:, 0:nb], in_=r4(in3), axis=AX.X, op=ALU.max,
                    apply_absolute_value=True)
    v.tensor_scalar(out=q.bm[:, 0:nb], in0=q.bm[:, 0:nb], scalar1=1.0 / 6.0, scalar2=1e-12,
                    op0=ALU.mult, op1=ALU.max)
    v.reciprocal(out=q.r[:, 0:nb], in_=q.bm[:, 0:nb])
    v.tensor_tensor(out=r4(q.xn[:, 0:F].rearrange("p (j z) -> p j z", j=nj)), in0=r4(in3),
                    in1=q.r[:, 0:nb].rearrange("p (j g) -> p j g", j=nj).unsqueeze(3)
                    .broadcast_to([128, nj, sz // 16, 16]),
                    op=ALU.mult)
    s.wait_ge(kb.sdv.h, kb.sdv.n)
    s.activation(out=q.absn[:, 0:F], in_=q.xn[:, 0:F], func=AF.Abs)
    s.activation(out=q.sg[:, 0:F], in_=q.xn[:, 0:F], func=AF.Sign)
    q.front_sac = kb.sac.n


def emit_quant_back(kb, q, out3, nj, sz):
    v = kb.v
    F = nj * sz
    nb = F // 16
    r4 = lambda ap: ap.rearrange("p j (g k) -> p j g k", k=16)
    fl = lambda buf: buf[:, 0:F]
    v.wait_ge(kb.sac.h, q.front_sac)
    v.tensor_scalar(out=fl(q.u), in0=fl(q.absn), scalar1=0.25, scalar2=None, op0=ALU.is_gt)
    for th in (0.75, 1.25, 1.75, 2.5, 3.5, 5.0):
        v.scalar_tensor_tensor(out=fl(q.u), in0=fl(q.absn), scalar=th, in1=fl(q.u),
                               op0=ALU.is_gt, op1=ALU.add)
    v.tensor_scalar(out=fl(q.r1), in0=fl(q.u), scalar1=-4.0, scalar2=0.0,
                    op0=ALU.add, op1=ALU.max)
    v.tensor_scalar(out=fl(q.r2), in0=fl(q.u), scalar1=-6.0, scalar2=0.0,
                    op0=ALU.add, op1=ALU.max)
    v.scalar_tensor_tensor(out=fl(q.u), in0=fl(q.u), scalar=0.5, in1=fl(q.r2),
                           op0=ALU.mult, op1=ALU.add)
    v.scalar_tensor_tensor(out=fl(q.u), in0=fl(q.r1), scalar=0.5, in1=fl(q.u),
                           op0=ALU.mult, op1=ALU.add)
    v.tensor_tensor(out=fl(q.u), in0=fl(q.u), in1=fl(q.sg), op=ALU.mult)
    v.tensor_tensor(out=r4(out3), in0=r4(q.u[:, 0:F].rearrange("p (j z) -> p j z", j=nj)),
                    in1=q.bm[:, 0:nb].rearrange("p (j g) -> p j g", j=nj).unsqueeze(3)
                    .broadcast_to([128, nj, sz // 16, 16]),
                    op=ALU.mult)
    return kb.sdv.n


class LNScratch:
    def __init__(self, kb, es, name, nj):
        self.st6 = kb.sb(es, name + "_st6", [128, nj, 12], F32)
        self.mv = kb.sb(es, name + "_mv", [128, nj, 2], F32)
        self.rstd = kb.sb(es, name + "_rstd", [128, nj], F32)
        self.bneg = kb.sb(es, name + "_bneg", [128, nj], F32)
        self.sq = kb.sb(es, name + "_sq", [128, nj], F32)


def emit_ln_stats(kb, ln, xres_ap, nj):
    v, s = kb.v, kb.s
    for j in range(nj):
        v.bn_stats(out=ln.st6[:, j, 0:6], in_=xres_ap[:, j, 0:512])
        v.bn_stats(out=ln.st6[:, j, 6:12], in_=xres_ap[:, j, 512:1024])
    for j in range(nj):
        v.bn_aggr(out=ln.mv[:, j, 0:2], in_=ln.st6[:, j, 0:12])
    s.wait_ge(kb.sdv.h, kb.sdv.n)
    s.activation(out=ln.sq[:, 0:nj], in_=ln.mv[:, :, 1], func=AF.Sqrt, bias=LN_EPS, scale=1.0)
    v.wait_ge(kb.sac.h, kb.sac.n)
    v.reciprocal(out=ln.rstd[:, 0:nj], in_=ln.sq[:, 0:nj])
    v.scalar_tensor_tensor(out=ln.bneg[:, 0:nj], in0=ln.mv[:, :, 0], scalar=-1.0,
                           in1=ln.rstd[:, 0:nj], op0=ALU.mult, op1=ALU.mult)
    return kb.sdv.n


def emit_l0ffn(kb, src, dst, s_wg, s_wu, s_wd, ident):
    nc = kb.nc
    v, s, g = kb.v, kb.s, kb.g
    te, sy = nc.tensor, nc.sync
    es = ExitStack()
    pfx = "ff0" + kb.uid() + "_"
    sb = lambda n, sh, dt: kb.sb(es, pfx + n, sh, dt)
    psal = lambda n, sh, dt: kb.psum(es, pfx + n, sh, dt)

    NJ, TT = 4, 512
    NTILE = T // TT

    xres = sb("xres", [128, 2, NJ, D], F32)
    xln = sb("xln", [128, 2, D], F32)
    xq = sb("xq", [128, NJ, D], BF16)
    xqT = sb("xqT", [128, 8, TT], BF16)
    hslab = sb("hslab", [128, 2, 512], F32)
    sa = sb("sa", [128, 2, 512], F32)
    hqp = sb("hqp", [128, 2, I_FF0], BF16)
    hqT = sb("hqT", [128, N_IC0, 256], BF16)
    wgbuf = sb("wgbuf", [128, 2, 8, 512], BF16)
    wubuf = sb("wubuf", [128, 2, 8, 512], BF16)
    wd = sb("wd", [128, N_IC0, D], BF16)
    qA = QuantScratch(kb, es, pfx + "qA", 1024)
    qB = QuantScratch(kb, es, pfx + "qB", 1024)
    qs = [qA, qB]
    ln = LNScratch(kb, es, pfx + "ln", NJ)

    pgu = psal("pgu", [128, 4, 512], F32)
    pd = psal("pd", [128, 2, 512], F32)
    ptr = [psal("ptrA", [128, 1024], BF16), psal("ptrB", [128, 1024], BF16)]

    sy.wait_ge(kb.sw2.h, kb.sw2.n)
    kb.sw2.inc(sy.dma_start(out=wd[:], in_=s_wd.rearrange("(p ic o) -> p ic o", p=128, ic=N_IC0)), 16)
    wd_ready = kb.sw2.n

    xsem = [kb.sx, kb.sxb]
    wsem = [kb.sw, kb.swb]
    stsem = [kb.sst, kb.sstb]
    slab_free = {0: 0, 1: 0}
    slab_ready = {}
    slab_count = [0]
    ptr_free = {0: None, 1: None}
    gu_free_g = {0: 0, 2: 0}
    gu_free_u = {1: 0, 3: 0}
    pd_free = {0: 0, 1: 0}
    store_done = []
    x_ready = {}
    pending = []

    def flush_pending():
        for (bb_, j_, oh_, np2) in pending:
            v.wait_ge(kb.spe.h, np2)
            v.scalar_tensor_tensor(out=xres[:, bb_, j_, oh_ * 512:(oh_ + 1) * 512],
                                   in0=pd[:, oh_, :], scalar=0.5,
                                   in1=xres[:, bb_, j_, oh_ * 512:(oh_ + 1) * 512],
                                   op0=ALU.mult, op1=ALU.add)
            pd_free[oh_] = kb.sdv.n
        pending.clear()

    def slab_ap(flat, og):
        off = 128 * 8 * OG_OFF[og]
        szz = OG_SIZES[og]
        return flat[off: off + 128 * 8 * szz].rearrange("(p fc o) -> p fc o", p=128, fc=8)

    def emit_slab_loads(key):
        ti_, pair_, og_ = key
        pr = slab_count[0] % 2
        slab_count[0] += 1
        sy.wait_ge(kb.spe.h, slab_free[pr])
        szz = OG_SIZES[og_]
        wsem[pr].inc(sy.dma_start(out=wgbuf[:, pr, :, 0:szz], in_=slab_ap(s_wg, og_)), 16)
        wsem[pr].inc(sy.dma_start(out=wubuf[:, pr, :, 0:szz], in_=slab_ap(s_wu, og_)), 16)
        slab_ready[key] = (wsem[pr], wsem[pr].n, pr)

    def slab_key_seq():
        for ti_ in range(NTILE):
            for pair_ in range(2):
                for og_ in range(6):
                    yield (ti_, pair_, og_)

    slab_seq = list(slab_key_seq())
    slab_pos = [0]

    def pump_slabs(upto):
        while slab_pos[0] < min(upto, len(slab_seq)):
            emit_slab_loads(slab_seq[slab_pos[0]])
            slab_pos[0] += 1

    def emit_x_load(ti):
        b = ti % 2
        if ti >= 2:
            sy.wait_ge(store_done[ti - 2][0].h, store_done[ti - 2][1])
        ap = src[ti * TT:(ti + 1) * TT, :].rearrange("(j p) f -> p j f", p=128)
        xsem[b].inc(sy.dma_start(out=xres[:, b, :, :], in_=ap), 16)
        x_ready[ti] = (xsem[b], xsem[b].n)

    emit_x_load(0)
    pump_slabs(2)

    def emit_store(tprev):
        flush_pending()
        nc.scalar.wait_ge(kb.sdv.h, kb.sdv.n)
        out_ap = dst[tprev * TT:(tprev + 1) * TT, :].rearrange("(j p) f -> p j f", p=128)
        stsem[tprev % 2].inc(nc.scalar.dma_start(out=out_ap, in_=xres[:, tprev % 2, :, :]), 16)
        store_done.append((stsem[tprev % 2], stsem[tprev % 2].n))

    for ti in range(NTILE):
        b = ti % 2
        v.wait_ge(x_ready[ti][0].h, x_ready[ti][1])
        s.wait_ge(x_ready[ti][0].h, x_ready[ti][1])
        emit_ln_stats(kb, ln, xres[:, b], NJ)
        def xq_transposes(j, xq_dv):
            for grp in range(2):
                pb = grp
                if ptr_free[pb] is not None:
                    te.wait_ge(ptr_free[pb][0].h, ptr_free[pb][1])
                te.wait_ge(kb.sdv.h, xq_dv)
                for k in range(4):
                    fc = grp * 4 + k
                    if k == 3:
                        te.wait_ge(kb.spe.h, kb.spe.n)
                    last = te.transpose(out=ptr[pb][:, k * 128:(k + 1) * 128],
                                        in_=xq[:, j, fc * 128:(fc + 1) * 128], identity=ident)
                np_ = kb.spe.inc(last)
                s.wait_ge(kb.spe.h, np_)
                s.copy(out=xqT[:, grp * 4:(grp + 1) * 4, j * 128:(j + 1) * 128],
                       in_=ptr[pb][:, 0:512])
                ptr_free[pb] = (kb.sac, kb.sac.n)

        for j in range(NJ):
            s.wait_ge(kb.sdv.h, kb.sdv.n)
            s.activation(out=xln[:, j % 2, :], in_=xres[:, b, j, :], func=AF.Identity,
                         scale=ln.rstd[:, j:j + 1], bias=ln.bneg[:, j:j + 1])
            v.wait_ge(kb.sac.h, kb.sac.n)
            emit_quant_front(kb, qs[j % 2], xln[:, j % 2, :].unsqueeze(1), 1, D)
            if j > 0:
                dv = emit_quant_back(kb, qs[(j - 1) % 2], xq[:, j - 1, :].unsqueeze(1), 1, D)
                xq_transposes(j - 1, dv)
        dv = emit_quant_back(kb, qs[(NJ - 1) % 2], xq[:, NJ - 1, :].unsqueeze(1), 1, D)
        xq_transposes(NJ - 1, dv)
        xqT_ready = kb.sac.n
        te.wait_ge(kb.sac.h, xqT_ready)
        if ti > 0:
            emit_store(ti - 1)
        if ti + 1 < NTILE:
            emit_x_load(ti + 1)

        for pair in range(2):
            back_pend = None
            for og in range(6):
                sz = OG_SIZES[og]
                key = (ti, pair, og)
                sr = slab_ready[key]
                for jj in range(2):
                    j = 2 * pair + jj
                    gslot = jj * 2
                    uslot = gslot + 1
                    te.wait_ge(sr[0].h, sr[1])
                    te.wait_ge(kb.sac.h, gu_free_g[gslot])
                    te.wait_ge(kb.sdv.h, gu_free_u[uslot])
                    for fc in range(8):
                        te.matmul(pgu[:, gslot, 0:sz], lhsT=xqT[:, fc, j * 128:(j + 1) * 128],
                                  rhs=wgbuf[:, sr[2], fc, 0:sz], start=(fc == 0), stop=(fc == 7))
                    for fc in range(8):
                        if fc == 7:
                            te.wait_ge(kb.spe.h, kb.spe.n)
                        last = te.matmul(pgu[:, uslot, 0:sz],
                                         lhsT=xqT[:, fc, j * 128:(j + 1) * 128],
                                         rhs=wubuf[:, sr[2], fc, 0:sz],
                                         start=(fc == 0), stop=(fc == 7))
                    np_ = kb.spe.inc(last)
                    if jj == 1:
                        slab_free[sr[2]] = np_
                        pump_slabs(slab_pos[0] + 1)
                    s.wait_ge(kb.spe.h, np_)
                    s.activation(out=sa[:, jj, 0:sz], in_=pgu[:, gslot, 0:sz], func=AF.Silu)
                    gu_free_g[gslot] = kb.sac.n
                    v.wait_ge(kb.sac.h, kb.sac.n)
                    v.tensor_tensor(out=hslab[:, jj, 0:sz], in0=sa[:, jj, 0:sz],
                                    in1=pgu[:, uslot, 0:sz], op=ALU.mult)
                    gu_free_u[uslot] = kb.sdv.n
                emit_quant_front(kb, qs[og % 2], hslab[:, 0:2, 0:sz], 2, sz)
                if back_pend is not None:
                    emit_quant_back(kb, qs[back_pend[0] % 2],
                                    hqp[:, 0:2, OG_OFF[back_pend[0]]:OG_OFF[back_pend[0]] + back_pend[1]],
                                    2, back_pend[1])
                back_pend = (og, sz)
            dv = emit_quant_back(kb, qs[back_pend[0] % 2],
                                 hqp[:, 0:2, OG_OFF[back_pend[0]]:OG_OFF[back_pend[0]] + back_pend[1]],
                                 2, back_pend[1])
            hq_done = (kb.sdv, dv)
            for jj in range(2):
                j = 2 * pair + jj
                for grp in range(6):
                    pb = grp % 2
                    n_in = min(4, N_IC0 - grp * 4)
                    if ptr_free[pb] is not None:
                        te.wait_ge(ptr_free[pb][0].h, ptr_free[pb][1])
                    if grp == 0 and jj == 0:
                        te.wait_ge(hq_done[0].h, hq_done[1])
                    for k in range(n_in):
                        ic = grp * 4 + k
                        if k == n_in - 1:
                            te.wait_ge(kb.spe.h, kb.spe.n)
                        last = te.transpose(out=ptr[pb][:, k * 128:(k + 1) * 128],
                                            in_=hqp[:, jj, ic * 128:(ic + 1) * 128],
                                            identity=ident)
                    np_ = kb.spe.inc(last)
                    s.wait_ge(kb.spe.h, np_)
                    s.copy(out=hqT[:, grp * 4:grp * 4 + n_in, jj * 128:(jj + 1) * 128],
                           in_=ptr[pb][:, 0:n_in * 128])
                    ptr_free[pb] = (kb.sac, kb.sac.n)
                hqT_ready = kb.sac.n
                te.wait_ge(kb.sac.h, hqT_ready)
                if ti == 0 and pair == 0 and jj == 0:
                    te.wait_ge(kb.sw2.h, wd_ready)
                for oh in range(2):
                    if any(p_[2] == oh for p_ in pending):
                        flush_pending()
                    te.wait_ge(kb.sdv.h, pd_free[oh])
                    for ic in range(N_IC0):
                        if ic == N_IC0 - 1:
                            te.wait_ge(kb.spe.h, kb.spe.n)
                        last = te.matmul(pd[:, oh, :], lhsT=hqT[:, ic, jj * 128:(jj + 1) * 128],
                                         rhs=wd[:, ic, oh * 512:(oh + 1) * 512],
                                         start=(ic == 0), stop=(ic == N_IC0 - 1))
                    np_ = kb.spe.inc(last)
                    pending.append((b, j, oh, np_))

    emit_store(NTILE - 1)

    kb.barrier()
    es.close()



def emit_l0attn(kb, src, dst, wqf, gnwf, gnbf, c_dtc, c_lamc, c_muc,
                c_glcc, c_b2, c_ones2, ident, identf):
    nc = kb.nc
    v, s = kb.v, kb.s
    te, sy, g_ = nc.tensor, nc.sync, nc.gpsimd
    es = ExitStack()
    pfx = "at0" + kb.uid() + "_"
    sb = lambda n, sh, dt: kb.sb(es, pfx + n, sh, dt)
    psal = lambda n, sh, dt: kb.psum(es, pfx + n, sh, dt)

    SC = 512
    NSC = T // SC
    NJ = 4

    xres = sb("xres", [128, NJ, D], F32)
    xhat = sb("xhat", [128, D], BF16)
    xhT = sb("xhT", [128, 8, SC], BF16)
    qf = sb("qf", [128, 8, SC], BF16)
    kf = sb("kf", [128, 8, SC], BF16)
    vt = sb("vt", [128, NJ, D], BF16)
    ktl = sb("ktl", [128, NJ, 8, 128], BF16)
    sdtAll = sb("sdtAll", [128, NH, 128], BF16)
    yf = sb("yf", [128, 8, T], BF16)
    S_ = sb("S", [128, 8, HD], F32)
    S_bf = sb("Sbf", [128, 8, HD], BF16)
    wq_sb = sb("wq", [128, 64, 128], BF16)
    wk_sb = sb("wk", [128, 64, 128], BF16)
    wv_sb = sb("wv", [128, 8, D], BF16)
    dtc = sb("dtc", [128, NH, 128], F32)
    lamc = sb("lamc", [128, 8, 128], F32)
    muc = sb("muc", [128, 8, 128], F32)
    glc = sb("glc", [128, 8], F32)
    gnw = sb("gnw", [128, 8], F32)
    gnb = sb("gnb", [128, 8], F32)
    b2c = sb("b2c", [2, 128], F32)
    ones2 = sb("ones2", [128, 2], F32)
    st6c = sb("st6c", [128, 8, 48], F32)
    mvv = sb("mvv", [128, 8, 2], F32)
    ey2 = sb("ey2", [128, 8], F32)
    m2 = sb("m2", [128, 8], F32)
    sr8 = sb("sr8", [8, 4], F32)
    rtT = sb("rtT", [2, 8], F32)
    mnT = sb("mnT", [2, 8], F32)
    aG = sb("aG", [128, 8], F32)
    bG = sb("bG", [128, 8], F32)
    gmn = sb("gmn", [128, 8], F32)
    ln = LNScratch(kb, es, pfx + "ln", NJ)

    pbig = psal("pbig", [128, 6, 512], F32)
    ptr = [psal("ptrA", [128, 1024], BF16), psal("ptrB", [128, 1024], BF16)]

    sy.wait_ge(kb.sw2.h, kb.sw2.n)
    sy.wait_ge(kb.swg.h, wqf["marker"])
    kb.sw2.inc(sy.dma_start(out=wq_sb[:], in_=wqf["wq"].rearrange("(a p o) -> p a o", p=128, a=64)), 16)
    kb.sw2.inc(sy.dma_start(out=wk_sb[:], in_=wqf["wk"].rearrange("(a p o) -> p a o", p=128, a=64)), 16)
    kb.sw2.inc(sy.dma_start(out=wv_sb[:], in_=wqf["wv"].rearrange("(fc p o) -> p fc o", p=128, fc=8)), 16)
    kb.sw2.inc(sy.dma_start(out=dtc[:], in_=c_dtc[:, :, :]), 16)
    kb.sw2.inc(sy.dma_start(out=lamc[:], in_=c_lamc[:, :, :]), 16)
    kb.sw2.inc(sy.dma_start(out=muc[:], in_=c_muc[:, :, :]), 16)
    kb.sw2.inc(sy.dma_start(out=glc[:], in_=c_glcc[:, :]), 16)
    kb.sw2.inc(sy.dma_start(out=gnw[:], in_=gnwf[:, :]), 16)
    kb.sw2.inc(sy.dma_start(out=gnb[:], in_=gnbf[:, :]), 16)
    kb.sw2.inc(sy.dma_start(out=b2c[:], in_=c_b2[:, :]), 16)
    kb.sw2.inc(sy.dma_start(out=ones2[:], in_=c_ones2[:, :]), 16)
    w_ready = (0, kb.sw2.n)
    xsem = [kb.sx, kb.sxb]

    v.memset(S_[:], 0.0)
    v.memset(S_bf[:], 0.0)
    sbf_done = kb.sdv.n

    ptr_free = {0: None, 1: None}
    psS_free = {0: 0, 1: 0}
    psO_free = {0: 0, 1: 0, 2: 0, 3: 0}
    psDS_free = {4: 0, 5: 0}
    qkv_free = 0
    xres_free = 0
    store_done = 0

    te.wait_ge(kb.sw2.h, w_ready[1])

    def emit_x_load(idx):
        sy.wait_ge(kb.sac.h, xres_free)
        sy.wait_ge(kb.sst.h, store_done)
        ap = src[idx * SC:(idx + 1) * SC, :].rearrange("(j p) f -> p j f", p=128)
        xsem[idx % 2].inc(sy.dma_start(out=xres[:, :, :], in_=ap), 16)
        return (xsem[idx % 2], xsem[idx % 2].n)

    x_ready = emit_x_load(0)

    for sc in range(NSC):
        v.wait_ge(x_ready[0].h, x_ready[1])
        s.wait_ge(x_ready[0].h, x_ready[1])
        emit_ln_stats(kb, ln, xres[:, :, :], NJ)
        for j in range(NJ):
            s.wait_ge(kb.sdv.h, kb.sdv.n)
            s.wait_ge(kb.spe.h, kb.spe.n)
            s.activation(out=xhat[:], in_=xres[:, j, :], func=AF.Identity,
                         scale=ln.rstd[:, j:j + 1], bias=ln.bneg[:, j:j + 1])
            if j == NJ - 1:
                xres_free = kb.sac.n
            for grp in range(2):
                pb = grp
                if ptr_free[pb] is not None:
                    te.wait_ge(ptr_free[pb][0].h, ptr_free[pb][1])
                te.wait_ge(kb.sac.h, kb.sac.n)
                for k in range(4):
                    fc = grp * 4 + k
                    if k == 3:
                        te.wait_ge(kb.spe.h, kb.spe.n)
                    last = te.transpose(out=ptr[pb][:, k * 128:(k + 1) * 128],
                                        in_=xhat[:, fc * 128:(fc + 1) * 128], identity=ident)
                np_ = kb.spe.inc(last)
                s.wait_ge(kb.spe.h, np_)
                s.copy(out=xhT[:, grp * 4:(grp + 1) * 4, j * 128:(j + 1) * 128],
                       in_=ptr[pb][:, 0:512])
                ptr_free[pb] = (kb.sac, kb.sac.n)
        xhT_ready = kb.sac.n
        te.wait_ge(kb.sac.h, xhT_ready)
        te.wait_ge(kb.sdv.h, kb.sdv.n)
        te.wait_ge(kb.sac.h, kb.sac.n)
        qk_slot_free = {0: 0, 1: 0}
        for wsel, (wtile, out_sb) in enumerate([(wq_sb, qf), (wk_sb, kf)]):
            for oc in range(8):
                slot = (2 * wsel + oc) % 2
                te.wait_ge(kb.sdv.h, qk_slot_free[slot])
                for fc in range(8):
                    if fc == 7:
                        te.wait_ge(kb.spe.h, kb.spe.n)
                    last = te.matmul(pbig[:, slot, :], lhsT=wtile[:, oc * 8 + fc, :],
                                     rhs=xhT[:, fc, :], start=(fc == 0), stop=(fc == 7))
                np_ = kb.spe.inc(last)
                v.wait_ge(kb.spe.h, np_)
                if wsel == 0:
                    v.tensor_tensor(out=qf[:, oc, :].rearrange("p (c t) -> p c t", t=128),
                                    in0=pbig[:, slot, :].rearrange("p (c t) -> p c t", t=128),
                                    in1=lamc[:, oc, :].unsqueeze(1).broadcast_to([128, NJ, 128]),
                                    op=ALU.mult)
                else:
                    v.tensor_copy(out=kf[:, oc, :], in_=pbig[:, slot, :])
                qk_slot_free[slot] = kb.sdv.n
        v_slot_free = {2: 0, 3: 0}
        for j in range(NJ):
            for oh in range(2):
                slot = 2 + (2 * j + oh) % 2
                te.wait_ge(kb.sdv.h, v_slot_free[slot])
                for fc in range(8):
                    if fc == 7:
                        te.wait_ge(kb.spe.h, kb.spe.n)
                    last = te.matmul(pbig[:, slot, :], lhsT=xhT[:, fc, j * 128:(j + 1) * 128],
                                     rhs=wv_sb[:, fc, oh * 512:(oh + 1) * 512],
                                     start=(fc == 0), stop=(fc == 7))
                np_ = kb.spe.inc(last)
                v.wait_ge(kb.spe.h, np_)
                v.tensor_copy(out=vt[:, j, oh * 512:(oh + 1) * 512], in_=pbig[:, slot, :])
                v_slot_free[slot] = kb.sdv.n
        for j in range(NJ):
            for grp in range(2):
                pb = grp
                if ptr_free[pb] is not None:
                    te.wait_ge(ptr_free[pb][0].h, ptr_free[pb][1])
                te.wait_ge(kb.sdv.h, kb.sdv.n)
                for k in range(4):
                    oc = grp * 4 + k
                    if k == 3:
                        te.wait_ge(kb.spe.h, kb.spe.n)
                    last = te.transpose(out=ptr[pb][:, k * 128:(k + 1) * 128],
                                        in_=kf[:, oc, j * 128:(j + 1) * 128], identity=ident)
                np_ = kb.spe.inc(last)
                v.wait_ge(kb.spe.h, np_)
                v.tensor_tensor(out=ktl[:, j, grp * 4:(grp + 1) * 4, :],
                                in0=ptr[pb][:, 0:512].rearrange("p (a q) -> p a q", q=128),
                                in1=muc[:, grp * 4:(grp + 1) * 4, :], op=ALU.mult)
                ptr_free[pb] = (kb.sdv, kb.sdv.n)

        if sc + 1 < NSC:
            x_ready = emit_x_load(sc + 1)

        for cl in range(NJ):
            te.wait_ge(kb.sdv.h, kb.sdv.n)
            te.wait_ge(kb.sac.h, kb.sac.n)
            for h in range(NH):
                if h == NH - 1:
                    te.wait_ge(kb.spe.h, kb.spe.n)
                hc, par = h // 2, h % 2
                hr = 64 * par
                last = te.matmul(pbig[0:128, h % 4, (h // 4) * 128:(h // 4) * 128 + 128],
                                 lhsT=kf[hr:hr + 64, hc, cl * 128:(cl + 1) * 128],
                                 rhs=qf[hr:hr + 64, hc, cl * 128:(cl + 1) * 128],
                                 start=True, stop=True)
            nsc_ = kb.spe.inc(last)
            v.wait_ge(kb.spe.h, nsc_)
            v.tensor_tensor(out=sdtAll[:, :, :].rearrange("p (q b) i -> p q b i", b=4),
                            in0=pbig[:, 0:4, :].rearrange("p b (q i) -> p q b i", i=128),
                            in1=dtc[:, :, :].rearrange("p (q b) i -> p q b i", b=4),
                            op=ALU.mult)
            sdt_dv = kb.sdv.n
            te.wait_ge(kb.sdv.h, sdt_dv)
            for h in range(NH):
                hc, par = h // 2, h % 2
                hr = 64 * par
                m = h // 2
                ob = (0 if m < 4 else 1) + 2 * par
                ocol = (m % 4) * 128
                if h in (0, 1, 8, 9):
                    te.wait_ge(kb.sac.h, psO_free[ob])
                if h < 2:
                    te.wait_ge(kb.sdv.h, psDS_free[4 + par])
                te.matmul(pbig[0:64, ob, ocol:ocol + 128], lhsT=vt[:, cl, 64 * h:64 * h + 64],
                          rhs=sdtAll[:, h, :], start=True, stop=False)
                te.wait_ge(kb.spe.h, kb.spe.n)
                last = te.matmul(pbig[0:64, ob, ocol:ocol + 128], lhsT=S_bf[hr:hr + 64, hc, :],
                                 rhs=qf[hr:hr + 64, hc, cl * 128:(cl + 1) * 128],
                                 start=False, stop=True)
                no_ = kb.spe.inc(last)
                te.wait_ge(kb.spe.h, kb.spe.n)
                last = te.matmul(pbig[0:64, 4 + par, m * 64:(m + 1) * 64],
                                 lhsT=ktl[:, cl, hc, hr:hr + 64],
                                 rhs=vt[:, cl, 64 * h:64 * h + 64], start=True, stop=True)
                nds_ = kb.spe.inc(last)
                if h in (6, 7, 14, 15):
                    s.wait_ge(kb.spe.h, no_)
                    hcb = 0 if h in (6, 7) else 4
                    s.copy(out=yf[hr:hr + 64, hcb:hcb + 4,
                                  sc * SC + cl * 128: sc * SC + (cl + 1) * 128],
                           in_=pbig[0:64, ob, 0:512].rearrange("p (a i) -> p a i", i=128))
                    psO_free[ob] = kb.sac.n
                if h >= 14:
                    v.wait_ge(kb.spe.h, nds_)
                    v.tensor_tensor(out=S_[hr:hr + 64, :, :], in0=S_[hr:hr + 64, :, :],
                                    in1=glc[hr:hr + 64, :].unsqueeze(2)
                                    .broadcast_to([64, 8, HD]), op=ALU.mult)
                    v.tensor_tensor(out=S_[hr:hr + 64, :, :], in0=S_[hr:hr + 64, :, :],
                                    in1=pbig[0:64, 4 + par, :].rearrange("p (a d) -> p a d", d=64),
                                    op=ALU.add)
                    v.tensor_copy(out=S_bf[hr:hr + 64, :, :], in_=S_[hr:hr + 64, :, :])
                    psDS_free[4 + par] = kb.sdv.n
        qkv_free = kb.spe.n

    v.wait_ge(kb.sac.h, kb.sac.n)
    for oc in range(8):
        for k in range(8):
            v.bn_stats(out=st6c[:, oc, k * 6:(k + 1) * 6], in_=yf[:, oc, k * 512:(k + 1) * 512])
    for oc in range(8):
        v.bn_aggr(out=mvv[:, oc, 0:2], in_=st6c[:, oc, 0:48])
    v.tensor_tensor(out=m2[:], in0=mvv[:, :, 0], in1=mvv[:, :, 0], op=ALU.mult)
    v.tensor_tensor(out=ey2[:], in0=mvv[:, :, 1], in1=m2[:], op=ALU.add)
    te.wait_ge(kb.sdv.h, kb.sdv.n)
    te.wait_ge(kb.spe.h, kb.spe.n)
    last = te.matmul(pbig[0:8, 0, 0:2], lhsT=mvv[:, :, 0], rhs=ones2[:], start=True, stop=True)
    te.wait_ge(kb.spe.h, kb.spe.n)
    last2 = te.matmul(pbig[0:8, 1, 0:2], lhsT=ey2[:], rhs=ones2[:], start=True, stop=True)
    nst = kb.spe.inc(last2)
    v.wait_ge(kb.spe.h, nst)
    v.tensor_scalar(out=sr8[0:8, 0:2], in0=pbig[0:8, 0, 0:2], scalar1=1.0 / 64, scalar2=None,
                    op0=ALU.mult)
    v.tensor_scalar(out=sr8[0:8, 2:4], in0=pbig[0:8, 1, 0:2], scalar1=1.0 / 64, scalar2=None,
                    op0=ALU.mult)
    v.tensor_tensor(out=ey2[0:8, 0:2], in0=sr8[0:8, 0:2], in1=sr8[0:8, 0:2], op=ALU.mult)
    v.tensor_tensor(out=ey2[0:8, 2:4], in0=sr8[0:8, 2:4], in1=ey2[0:8, 0:2], op=ALU.subtract)
    s.wait_ge(kb.sdv.h, kb.sdv.n)
    s.activation(out=m2[0:8, 0:2], in_=ey2[0:8, 2:4], func=AF.Sqrt, bias=GN_EPS, scale=1.0)
    v.wait_ge(kb.sac.h, kb.sac.n)
    v.reciprocal(out=m2[0:8, 2:4], in_=m2[0:8, 0:2])
    v.tensor_copy(out=m2[0:8, 4:6], in_=sr8[0:8, 0:2])
    te.wait_ge(kb.sdv.h, kb.sdv.n)
    te.wait_ge(kb.spe.h, kb.spe.n)
    last = te.transpose(out=pbig[0:2, 2, 0:8], in_=m2[0:8, 2:4], identity=identf[0:8, 0:8])
    te.wait_ge(kb.spe.h, kb.spe.n)
    last = te.transpose(out=pbig[0:2, 3, 0:8], in_=m2[0:8, 4:6], identity=identf[0:8, 0:8])
    ntr = kb.spe.inc(last)
    v.wait_ge(kb.spe.h, ntr)
    v.tensor_copy(out=rtT[0:2, 0:8], in_=pbig[0:2, 2, 0:8])
    v.tensor_copy(out=mnT[0:2, 0:8], in_=pbig[0:2, 3, 0:8])
    te.wait_ge(kb.sdv.h, kb.sdv.n)
    te.wait_ge(kb.spe.h, kb.spe.n)
    last = te.matmul(pbig[0:128, 4, 0:8], lhsT=b2c[:, :], rhs=rtT[0:2, 0:8], start=True, stop=True)
    te.wait_ge(kb.spe.h, kb.spe.n)
    last = te.matmul(pbig[0:128, 5, 0:8], lhsT=b2c[:, :], rhs=mnT[0:2, 0:8], start=True, stop=True)
    ngb = kb.spe.inc(last)
    v.wait_ge(kb.spe.h, ngb)
    v.tensor_tensor(out=aG[:], in0=pbig[:, 4, 0:8], in1=gnw[:], op=ALU.mult)
    v.tensor_copy(out=gmn[:], in_=pbig[:, 5, 0:8])
    v.scalar_tensor_tensor(out=bG[:], in0=gmn[:], scalar=-1.0, in1=aG[:], op0=ALU.mult, op1=ALU.mult)
    v.tensor_tensor(out=bG[:], in0=bG[:], in1=gnb[:], op=ALU.add)
    gn_ready_dv = kb.sdv.n

    sy.wait_ge(kb.spe.h, kb.spe.n)
    sy.wait_ge(kb.sw2.h, kb.sw2.n)
    kb.sw2.inc(sy.dma_start(out=wv_sb[:], in_=wqf["wo"].rearrange("(oc p o) -> p oc o", p=128, oc=8)), 16)
    wo_ready = kb.sw2.n

    s.wait_ge(kb.sdv.h, gn_ready_dv)
    te.wait_ge(kb.sw2.h, wo_ready)
    for sc in range(NSC):
        x_ready = emit_x_load(sc)
        for oc in range(8):
            s.activation(out=xhT[:, oc, :], in_=yf[:, oc, sc * SC:(sc + 1) * SC],
                         func=AF.Identity, scale=aG[:, oc:oc + 1], bias=bG[:, oc:oc + 1])
        yh_ready = kb.sac.n
        v.wait_ge(x_ready[0].h, x_ready[1])
        te.wait_ge(kb.sac.h, yh_ready)
        for j in range(NJ):
            for oh in range(2):
                slot = (2 * j + oh) % 4
                te.wait_ge(kb.sdv.h, psS_free.get(('c', slot), 0))
                for oc in range(8):
                    if oc == 7:
                        te.wait_ge(kb.spe.h, kb.spe.n)
                    last = te.matmul(pbig[:, slot, :], lhsT=xhT[:, oc, j * 128:(j + 1) * 128],
                                     rhs=wv_sb[:, oc, oh * 512:(oh + 1) * 512],
                                     start=(oc == 0), stop=(oc == 7))
                np_ = kb.spe.inc(last)
                v.wait_ge(kb.spe.h, np_)
                v.tensor_tensor(out=xres[:, j, oh * 512:(oh + 1) * 512],
                                in0=pbig[:, slot, :], in1=xres[:, j, oh * 512:(oh + 1) * 512],
                                op=ALU.add)
                psS_free[('c', slot)] = kb.sdv.n
        nc.scalar.wait_ge(kb.sdv.h, kb.sdv.n)
        out_ap = dst[sc * SC:(sc + 1) * SC, :].rearrange("(j p) f -> p j f", p=128)
        kb.sst.inc(nc.scalar.dma_start(out=out_ap, in_=xres[:, :, :]), 16)
        store_done = kb.sst.n
        xres_free = kb.sac.n if False else xres_free
    kb.barrier()
    es.close()




def emit_l1ffn(kb, src, dst, w, s1f, s2f, sh1f, sh2f, s3f, ident, identf):
    nc = kb.nc
    v, s = kb.v, kb.s
    te, sy = nc.tensor, nc.sync
    es = ExitStack()
    pfx = "ff1" + kb.uid() + "_"
    sb = lambda n, sh, dt: kb.sb(es, pfx + n, sh, dt)
    psal = lambda n, sh, dt: kb.psum(es, pfx + n, sh, dt)

    NJ, TT = 4, 512
    NTILE = T // TT
    NIC = 32
    SLABS_PER_TILE = 44
    S1O, S2O, S3O, S4O = 0, 8, 24, 40

    xres = sb("xres", [128, 2, NJ, D], F32)
    xhat = sb("xhat", [128, D], BF16)
    xhT = sb("xhT", [128, 8, TT], BF16)
    hbuf = sb("h", [128, NIC, TT], BF16)
    gbuf = sb("g", [128, NIC, TT], BF16)
    warena = sb("wa", [128, 2, 8192], BF16)
    of = sb("of", [128, 2, TT], F32)
    s1 = sb("s1", [128, 32], F32)
    s2 = sb("s2", [128, 32], F32)
    sh1 = sb("sh1", [128, 32], F32)
    sh2 = sb("sh2", [128, 32], F32)
    s3h = sb("s3h", [128, 8], F32)
    ln = LNScratch(kb, es, pfx + "ln", NJ)

    pmm = psal("pmm", [128, 4, 512], F32)
    pot = psal("pot", [128, 2, 512], F32)
    ptr = [psal("ptrA", [128, 1024], BF16), psal("ptrB", [128, 1024], BF16)]

    sy.wait_ge(kb.sw2.h, kb.sw2.n)
    for dstt, srcf in [(s1, s1f), (s2, s2f), (sh1, sh1f), (sh2, sh2f)]:
        kb.sw2.inc(sy.dma_start(out=dstt[:], in_=srcf[:, :]), 16)
    kb.sw2.inc(sy.dma_start(out=s3h[:], in_=s3f[:, :]), 16)
    sc_ready = kb.sw2.n
    v.wait_ge(kb.sw2.h, sc_ready)
    v.tensor_scalar(out=s3h[:], in0=s3h[:], scalar1=0.5, scalar2=None, op0=ALU.mult)
    s3_dv = kb.sdv.n

    xsem = [kb.sx, kb.sxb]
    stsem = [kb.sst, kb.sstb]
    wsem = [kb.sw, kb.swb]
    store_done = []
    ptr_free = {0: None, 1: None}
    mm_free = {0: None, 1: None, 2: None, 3: None}
    pot_free = {0: 0, 1: 0}
    of_free = {0: None, 1: None}
    slab_free = {0: 0, 1: 0}
    slab_ready = {}

    def emit_slab(pos):
        ti, rel = divmod(pos, SLABS_PER_TILE)
        pr = pos % 2
        sy.wait_ge(kb.spe.h, slab_free[pr])
        if rel < S2O:
            sidx = rel
            npp = 4096
            off = 128 * npp * sidx
            wsem[pr].inc(sy.dma_start(
                out=warena[:, pr, 0:npp],
                in_=w["w1"][off:off + 128 * npp].rearrange("(p a) -> p a", p=128)), 16)
            wsem[pr].inc(sy.dma_start(
                out=warena[:, pr, npp:2 * npp],
                in_=w["w2"][off:off + 128 * npp].rearrange("(p a) -> p a", p=128)), 16)
        else:
            if rel < S3O:
                key, sidx = "h1", rel - S2O
            elif rel < S4O:
                key, sidx = "h2", rel - S3O
            else:
                key, sidx = "w3", rel - S4O
            npp = 8192
            off = 128 * npp * sidx
            wsem[pr].inc(sy.dma_start(
                out=warena[:, pr, 0:npp],
                in_=w[key][off:off + 128 * npp].rearrange("(p a) -> p a", p=128)), 16)
        slab_ready[pos] = (wsem[pr], wsem[pr].n)

    slab_pos = 0

    def pump_slabs(upto):
        nonlocal slab_pos
        while slab_pos < min(upto, NTILE * SLABS_PER_TILE):
            emit_slab(slab_pos)
            slab_pos += 1

    x_ready = {}

    def emit_x_load(ti):
        b = ti % 2
        if ti >= 2:
            sy.wait_ge(store_done[ti - 2][0].h, store_done[ti - 2][1])
        ap = src[ti * TT:(ti + 1) * TT, :].rearrange("(j p) f -> p j f", p=128)
        xsem[b].inc(sy.dma_start(out=xres[:, b, :, :], in_=ap), 16)
        x_ready[ti] = (xsem[b], xsem[b].n)

    sy.wait_ge(kb.swg.h, w["marker"])
    emit_x_load(0)
    pump_slabs(2)

    for ti in range(NTILE):
        b = ti % 2
        base = ti * SLABS_PER_TILE
        if ti + 1 < NTILE:
            emit_x_load(ti + 1)
        v.wait_ge(x_ready[ti][0].h, x_ready[ti][1])
        s.wait_ge(x_ready[ti][0].h, x_ready[ti][1])
        emit_ln_stats(kb, ln, xres[:, b], NJ)
        for j in range(NJ):
            s.wait_ge(kb.sdv.h, kb.sdv.n)
            s.wait_ge(kb.spe.h, kb.spe.n)
            s.activation(out=xhat[:], in_=xres[:, b, j, :], func=AF.Identity,
                         scale=ln.rstd[:, j:j + 1], bias=ln.bneg[:, j:j + 1])
            for grp in range(2):
                pb = grp
                if ptr_free[pb] is not None:
                    te.wait_ge(ptr_free[pb][0].h, ptr_free[pb][1])
                te.wait_ge(kb.sac.h, kb.sac.n)
                for k in range(4):
                    fc = grp * 4 + k
                    if k == 3:
                        te.wait_ge(kb.spe.h, kb.spe.n)
                    last = te.transpose(out=ptr[pb][:, k * 128:(k + 1) * 128],
                                        in_=xhat[:, fc * 128:(fc + 1) * 128], identity=ident)
                np_ = kb.spe.inc(last)
                s.wait_ge(kb.spe.h, np_)
                s.copy(out=xhT[:, grp * 4:(grp + 1) * 4, j * 128:(j + 1) * 128],
                       in_=ptr[pb][:, 0:512])
                ptr_free[pb] = (kb.sac, kb.sac.n)
        xhT_ready = kb.sac.n
        te.wait_ge(kb.sac.h, xhT_ready)
        if ti == 0:
            v.wait_ge(kb.sw2.h, sc_ready)
            s.wait_ge(kb.sw2.h, sc_ready)

        for oc in range(NIC):
            sidx = oc // 4
            pos = base + S1O + sidx
            pr = pos % 2
            sr = slab_ready[pos]
            oin = oc % 4
            for half in range(2):
                slot = (2 * oc + half) % 4
                te.wait_ge(sr[0].h, sr[1])
                if mm_free[slot] is not None:
                    te.wait_ge(mm_free[slot][0].h, mm_free[slot][1])
                for fc in range(8):
                    if fc == 7:
                        te.wait_ge(kb.spe.h, kb.spe.n)
                    last = te.matmul(
                        pmm[:, slot, :],
                        lhsT=warena[:, pr, half * 4096 + (oin * 8 + fc) * 128:
                                    half * 4096 + (oin * 8 + fc) * 128 + 128],
                        rhs=xhT[:, fc, :], start=(fc == 0), stop=(fc == 7))
                np_ = kb.spe.inc(last)
                par = oc % 2
                if half == 0:
                    s.wait_ge(kb.spe.h, np_)
                    if of_free[par] is not None:
                        s.wait_ge(of_free[par][0].h, of_free[par][1])
                    s.activation(out=of[:, par, :], in_=pmm[:, slot, :], func=AF.Silu,
                                 scale=s1[:, oc:oc + 1])
                    mm_free[slot] = (kb.sac, kb.sac.n)
                else:
                    v.wait_ge(kb.spe.h, np_)
                    v.wait_ge(kb.sac.h, kb.sac.n)
                    v.scalar_tensor_tensor(out=hbuf[:, oc, :], in0=pmm[:, slot, :],
                                           scalar=s2[:, oc:oc + 1], in1=of[:, par, :],
                                           op0=ALU.mult, op1=ALU.mult)
                    mm_free[slot] = (kb.sdv, kb.sdv.n)
                    of_free[par] = (kb.sdv, kb.sdv.n)
            if oin == 3:
                slab_free[pr] = kb.spe.n
                pump_slabs(pos + 3)
        h_ready_dv = kb.sdv.n

        stage_out_sac = 0
        for stage, soff, inbuf, outbuf, ssc in [(2, S2O, hbuf, gbuf, sh1),
                                                (3, S3O, gbuf, hbuf, sh2)]:
            for oc in range(NIC):
                sidx = oc // 2
                pos = base + soff + sidx
                pr = pos % 2
                sr = slab_ready[pos]
                oin = oc % 2
                slot = oc % 4
                te.wait_ge(sr[0].h, sr[1])
                if oc == 0:
                    if stage == 2:
                        te.wait_ge(kb.sdv.h, h_ready_dv)
                    else:
                        te.wait_ge(kb.sac.h, stage_out_sac)
                if mm_free[slot] is not None:
                    te.wait_ge(mm_free[slot][0].h, mm_free[slot][1])
                for ic in range(NIC):
                    if ic == NIC - 1:
                        te.wait_ge(kb.spe.h, kb.spe.n)
                    last = te.matmul(pmm[:, slot, :],
                                     lhsT=warena[:, pr, (oin * 32 + ic) * 128:
                                                 (oin * 32 + ic) * 128 + 128],
                                     rhs=inbuf[:, ic, :], start=(ic == 0), stop=(ic == NIC - 1))
                np_ = kb.spe.inc(last)
                s.wait_ge(kb.spe.h, np_)
                s.activation(out=outbuf[:, oc, :], in_=pmm[:, slot, :], func=AF.Gelu,
                             scale=ssc[:, oc:oc + 1])
                mm_free[slot] = (kb.sac, kb.sac.n)
                if oin == 1:
                    slab_free[pr] = kb.spe.n
                    pump_slabs(pos + 3)
            stage_out_sac = kb.sac.n

        for oc3 in range(8):
            sidx = oc3 // 2
            pos = base + S4O + sidx
            pr = pos % 2
            sr = slab_ready[pos]
            oin = oc3 % 2
            slot = oc3 % 4
            te.wait_ge(sr[0].h, sr[1])
            if oc3 == 0:
                te.wait_ge(kb.sac.h, stage_out_sac)
            if mm_free[slot] is not None:
                te.wait_ge(mm_free[slot][0].h, mm_free[slot][1])
            for ic in range(NIC):
                if ic == NIC - 1:
                    te.wait_ge(kb.spe.h, kb.spe.n)
                last = te.matmul(pmm[:, slot, :],
                                 lhsT=warena[:, pr, (oin * 32 + ic) * 128:
                                             (oin * 32 + ic) * 128 + 128],
                                 rhs=hbuf[:, ic, :], start=(ic == 0), stop=(ic == NIC - 1))
            np_ = kb.spe.inc(last)
            par = oc3 % 2
            s.wait_ge(kb.spe.h, np_)
            if ti == 0 and oc3 == 0:
                s.wait_ge(kb.sdv.h, s3_dv)
            if of_free[par] is not None:
                s.wait_ge(of_free[par][0].h, of_free[par][1])
            s.activation(out=of[:, par, :], in_=pmm[:, slot, :], func=AF.Identity,
                         scale=s3h[:, oc3:oc3 + 1])
            mm_free[slot] = (kb.sac, kb.sac.n)
            ofr = kb.sac.n
            for tj in range(NJ):
                pb2 = tj % 2
                te.wait_ge(kb.sdv.h, pot_free[pb2])
                te.wait_ge(kb.sac.h, ofr)
                te.wait_ge(kb.spe.h, kb.spe.n)
                last = te.transpose(out=pot[:, pb2, 0:128],
                                    in_=of[:, par, tj * 128:(tj + 1) * 128], identity=identf)
                np2 = kb.spe.inc(last)
                v.wait_ge(kb.spe.h, np2)
                v.tensor_tensor(out=xres[:, b, tj, oc3 * 128:(oc3 + 1) * 128],
                                in0=pot[:, pb2, 0:128],
                                in1=xres[:, b, tj, oc3 * 128:(oc3 + 1) * 128], op=ALU.add)
                pot_free[pb2] = kb.sdv.n
            of_free[par] = (kb.spe, kb.spe.n)
            if oin == 1:
                slab_free[pr] = kb.spe.n
                pump_slabs(pos + 3)

        nc.scalar.wait_ge(kb.sdv.h, kb.sdv.n)
        out_ap = dst[ti * TT:(ti + 1) * TT, :].rearrange("(j p) f -> p j f", p=128)
        stsem[b].inc(nc.scalar.dma_start(out=out_ap, in_=xres[:, b, :, :]), 16)
        store_done.append((stsem[b], stsem[b].n))

    kb.barrier()
    es.close()




def emit_l1attn(kb, src, dst, w, bof, c_mdiag, c_mold, c_ones1, ident, identf):
    nc = kb.nc
    v, s, g = kb.v, kb.s, kb.g
    te, sy = nc.tensor, nc.sync
    es = ExitStack()
    pfx = "at1" + kb.uid() + "_"
    sb = lambda n, sh, dt: kb.sb(es, pfx + n, sh, dt)
    psal = lambda n, sh, dt: kb.psum(es, pfx + n, sh, dt)

    NR = 6
    xres = sb("xres", [128, 2, D], F32)
    xhat = sb("xhat", [128, D], BF16)
    xhT = sb("xhT", [128, 2, 8, 128], BF16)
    qf = sb("qf", [128, 2, 8, 128], BF16)
    kfr = sb("kfr", [128, 8, NR, 128], BF16)
    vtr = sb("vtr", [128, NR, D], BF16)
    pbf = sb("pbf", [128, NH, 640], BF16)
    pnT = sb("pnT", [128, 2, 5, 128], BF16)
    yhT = sb("yhT", [128, 8, 128], BF16)
    mxh = sb("mxh", [128, NH], F32)
    smh = sb("smh", [128, NH], F32)
    rsh = sb("rsh", [128, NH], F32)
    wqk = sb("wqk", [128, 128, 128], BF16)
    wv_sb = sb("wv", [128, 8, D], BF16)
    wou = sb("wou", [128, 8, D], BF16)
    md = sb("md", [128, 128], F32)
    mo = sb("mo", [128, 128], F32)
    bor = sb("bor", [1, D], F32)
    onesr = sb("onesr", [1, 128], F32)
    bb = sb("bb", [128, D], F32)
    ln = LNScratch(kb, es, pfx + "ln", 1)

    pA = psal("pA", [128, 2, 1024], F32)
    pB = psal("pB", [128, 2, 512], F32)
    ptr = [psal("ptrA", [128, 1024], BF16), psal("ptrB", [128, 1024], BF16)]

    sy.wait_ge(kb.sw2.h, kb.sw2.n)
    sy.wait_ge(kb.swg.h, w["marker"])
    kb.sw2.inc(sy.dma_start(out=wqk[:], in_=w["qk"].rearrange("(a p o) -> p a o", p=128, a=128)), 16)
    kb.sw2.inc(sy.dma_start(out=wv_sb[:], in_=w["v"].rearrange("(fc p o) -> p fc o", p=128, fc=8)), 16)
    kb.sw2.inc(sy.dma_start(out=wou[:], in_=w["wo"].rearrange("(oc p o) -> p oc o", p=128, oc=8)), 16)
    kb.sw2.inc(sy.dma_start(out=md[:], in_=c_mdiag[:, :]), 16)
    kb.sw2.inc(sy.dma_start(out=mo[:], in_=c_mold[:, :]), 16)
    kb.sw2.inc(sy.dma_start(out=bor[:], in_=bof[:, :]), 16)
    kb.sw2.inc(sy.dma_start(out=onesr[:], in_=c_ones1[:, :]), 16)
    w_ready = kb.sw2.n
    te.wait_ge(kb.sw2.h, w_ready)
    for oh in range(2):
        te.wait_ge(kb.spe.h, kb.spe.n)
        last = te.matmul(pB[:, oh, :], lhsT=onesr[0:1, :], rhs=bor[0:1, oh * 512:(oh + 1) * 512],
                         start=True, stop=True)
    nbb = kb.spe.inc(last)
    v.wait_ge(kb.spe.h, nbb)
    v.tensor_copy(out=bb[:, 0:512], in_=pB[:, 0, :])
    v.tensor_copy(out=bb[:, 512:1024], in_=pB[:, 1, :])
    bb_dv = kb.sdv.n

    xsem = [kb.sx, kb.sxb]
    stsem = [kb.sst, kb.sstb]
    store_done = []
    x_ready = {}
    ptr_free = {0: None, 1: None}
    pa_free = {0: 0, 1: 0}
    pb_free = {0: (kb.sdv, bb_dv), 1: (kb.sdv, bb_dv)}
    kv_free = {}
    pnT_free = {0: None, 1: None}
    a_done = {}
    pbf_war = 0

    def emit_x_load(c):
        b = c % 2
        if c >= 2:
            sy.wait_ge(store_done[c - 2][0].h, store_done[c - 2][1])
        xsem[b].inc(sy.dma_start(out=xres[:, b, :],
                                 in_=src[c * 128:(c + 1) * 128, :]), 16)
        x_ready[c] = (xsem[b], xsem[b].n)

    def emit_proj(c):
        b = c % 2
        sl = c % NR
        v.wait_ge(x_ready[c][0].h, x_ready[c][1])
        s.wait_ge(x_ready[c][0].h, x_ready[c][1])
        emit_ln_stats(kb, ln, xres[:, b, :].unsqueeze(1), 1)
        s.wait_ge(kb.sdv.h, kb.sdv.n)
        s.wait_ge(kb.spe.h, kb.spe.n)
        s.activation(out=xhat[:], in_=xres[:, b, :], func=AF.Identity,
                     scale=ln.rstd[:, 0:1], bias=ln.bneg[:, 0:1])
        for grp in range(2):
            pb_ = grp
            if ptr_free[pb_] is not None:
                te.wait_ge(ptr_free[pb_][0].h, ptr_free[pb_][1])
            te.wait_ge(kb.sac.h, kb.sac.n)
            for k in range(4):
                fc = grp * 4 + k
                if k == 3:
                    te.wait_ge(kb.spe.h, kb.spe.n)
                last = te.transpose(out=ptr[pb_][:, k * 128:(k + 1) * 128],
                                    in_=xhat[:, fc * 128:(fc + 1) * 128], identity=ident)
            np_ = kb.spe.inc(last)
            s.wait_ge(kb.spe.h, np_)
            s.copy(out=xhT[:, b, grp * 4:(grp + 1) * 4, :], in_=ptr[pb_][:, 0:512])
            ptr_free[pb_] = (kb.sac, kb.sac.n)
        te.wait_ge(kb.sac.h, kb.sac.n)
        for oc in range(16):
            slot = oc % 2
            if pb_free[slot] is not None:
                te.wait_ge(pb_free[slot][0].h, pb_free[slot][1])
            if oc >= 8 and kv_free.get(sl) is not None:
                te.wait_ge(kv_free[sl][0].h, kv_free[sl][1])
            for fc in range(8):
                if fc == 7:
                    te.wait_ge(kb.spe.h, kb.spe.n)
                last = te.matmul(pB[:, slot, 0:128], lhsT=wqk[:, oc * 8 + fc, :],
                                 rhs=xhT[:, b, fc, :], start=(fc == 0), stop=(fc == 7))
            np_ = kb.spe.inc(last)
            v.wait_ge(kb.spe.h, np_)
            if oc < 8:
                v.tensor_scalar(out=qf[:, b, oc, :], in0=pB[:, slot, 0:128], scalar1=0.125,
                                scalar2=None, op0=ALU.mult)
            else:
                v.tensor_copy(out=kfr[:, oc - 8, sl, :], in_=pB[:, slot, 0:128])
            pb_free[slot] = (kb.sdv, kb.sdv.n)
        for oh in range(2):
            slot = oh
            if pb_free[slot] is not None:
                te.wait_ge(pb_free[slot][0].h, pb_free[slot][1])
            if kv_free.get(sl) is not None:
                te.wait_ge(kv_free[sl][0].h, kv_free[sl][1])
            for fc in range(8):
                if fc == 7:
                    te.wait_ge(kb.spe.h, kb.spe.n)
                last = te.matmul(pB[:, slot, :], lhsT=xhT[:, b, fc, :],
                                 rhs=wv_sb[:, fc, oh * 512:(oh + 1) * 512],
                                 start=(fc == 0), stop=(fc == 7))
            np_ = kb.spe.inc(last)
            v.wait_ge(kb.spe.h, np_)
            v.tensor_copy(out=vtr[:, sl, oh * 512:(oh + 1) * 512], in_=pB[:, slot, :])
            pb_free[slot] = (kb.sdv, kb.sdv.n)
        a_done[c] = kb.sdv.n

    emit_x_load(0)
    emit_x_load(1)
    emit_proj(0)

    for c in range(NCH):
        b = c % 2
        nsl = min(c, 4) + 1
        kcs = list(range(c - nsl + 1, c + 1))
        used = nsl * 128
        moff = (4 - c) * 128 if c < 4 else 0
        merged = []
        gi = 0
        i = 0
        while i < len(kcs):
            sl_ = kcs[i] % NR
            cnt = 1
            while (i + cnt < len(kcs) and (kcs[i + cnt] % NR) == sl_ + cnt and
                   (gi + cnt) % 4 != 0):
                cnt += 1
            merged.append((sl_, gi, cnt))
            gi += cnt
            i += cnt
        for h in range(NH):
            hc, par = h // 2, h % 2
            hr = 64 * par
            te.wait_ge(kb.sac.h, pa_free[par] if h >= 2 else max(pa_free[par], 0))
            if h < 2:
                te.wait_ge(kb.sdv.h, a_done[c])
            for ri, (sl_, gi0, cnt) in enumerate(merged):
                if ri == len(merged) - 1:
                    te.wait_ge(kb.spe.h, kb.spe.n)
                last = te.matmul(pA[:, par, gi0 * 128:(gi0 + cnt) * 128],
                                 lhsT=qf[hr:hr + 64, b, hc, :],
                                 rhs=kfr[hr:hr + 64, hc, sl_:sl_ + cnt, :],
                                 start=True, stop=True)
            nsc_ = kb.spe.inc(last)
            v.wait_ge(kb.spe.h, nsc_)
            v.tensor_tensor(out=pA[:, par, (nsl - 1) * 128:nsl * 128],
                            in0=pA[:, par, (nsl - 1) * 128:nsl * 128], in1=md[:, :],
                            op=ALU.add)
            if c >= 4:
                v.tensor_tensor(out=pA[:, par, 0:128], in0=pA[:, par, 0:128], in1=mo[:, :],
                                op=ALU.add)
            v.tensor_reduce(out=mxh[:, h:h + 1], in_=pA[:, par, 0:used], axis=AX.X, op=ALU.max)
            v.tensor_scalar(out=mxh[:, h:h + 1], in0=mxh[:, h:h + 1], scalar1=-1.0,
                            scalar2=None, op0=ALU.mult)
            s.wait_ge(kb.sdv.h, kb.sdv.n)
            if h < 2:
                s.wait_ge(kb.spe.h, pbf_war)
            s.activation(out=pbf[:, h, 0:used], in_=pA[:, par, 0:used], func=AF.Exp,
                         bias=mxh[:, h:h + 1], scale=1.0, accum_out=smh[:, h:h + 1])
            pa_free[par] = kb.sac.n
        exp_done = kb.sac.n
        v.wait_ge(kb.sac.h, exp_done)
        v.reciprocal(out=rsh[:, 0:NH], in_=smh[:, 0:NH])
        s.wait_ge(kb.sdv.h, kb.sdv.n)
        for h in range(NH):
            s.activation(out=pbf[:, h, 0:used], in_=pbf[:, h, 0:used], func=AF.Identity,
                         scale=rsh[:, h:h + 1])
        pn_done = ("ac", kb.sac.n)
        if c + 1 < NCH:
            emit_proj(c + 1)
        te.wait_ge(kb.sac.h, pn_done[1])
        pnt_dv = {}

        def emit_trans(h):
            pb_t = h % 2
            if ptr_free[pb_t] is not None:
                te.wait_ge(ptr_free[pb_t][0].h, ptr_free[pb_t][1])
            for gi in range(nsl):
                if gi == nsl - 1:
                    te.wait_ge(kb.spe.h, kb.spe.n)
                last = te.transpose(out=ptr[pb_t][:, gi * 128:(gi + 1) * 128],
                                    in_=pbf[:, h, gi * 128:(gi + 1) * 128], identity=ident)
            ntp = kb.spe.inc(last)
            v.wait_ge(kb.spe.h, ntp)
            if pnT_free[h % 2] is not None:
                v.wait_ge(pnT_free[h % 2][0].h, pnT_free[h % 2][1])
            v.tensor_copy(out=pnT[:, h % 2, 0:nsl, :], in_=ptr[pb_t][:, 0:used])
            ptr_free[pb_t] = (kb.sdv, kb.sdv.n)
            pnt_dv[h] = kb.sdv.n

        def emit_ctx(h):
            hc, par = h // 2, h % 2
            hr = 64 * par
            slot = par
            if pb_free[slot] is not None:
                te.wait_ge(pb_free[slot][0].h, pb_free[slot][1])
            te.wait_ge(kb.sdv.h, pnt_dv[h])
            for gi, kc in enumerate(kcs):
                if gi == nsl - 1:
                    te.wait_ge(kb.spe.h, kb.spe.n)
                last = te.matmul(pB[0:64, slot, 0:128], lhsT=vtr[:, kc % NR, 64 * h:64 * h + 64],
                                 rhs=pnT[:, h % 2, gi, :], start=(gi == 0), stop=(gi == nsl - 1))
            nctx = kb.spe.inc(last)
            pnT_free[h % 2] = (kb.spe, nctx)
            v.wait_ge(kb.spe.h, nctx)
            v.tensor_copy(out=yhT[hr:hr + 64, hc, :], in_=pB[0:64, slot, 0:128])
            pb_free[slot] = (kb.sdv, kb.sdv.n)

        for h in range(NH):
            emit_trans(h)
            if h > 0:
                emit_ctx(h - 1)
        emit_ctx(NH - 1)
        pbf_war = kb.spe.n
        kv_free[(c - 4) % NR if c >= 4 else (c + NR - 4) % NR] = (kb.spe, kb.spe.n)
        yh_dv = kb.sdv.n
        te.wait_ge(kb.sdv.h, yh_dv)
        if c == 0:
            v.wait_ge(kb.sdv.h, bb_dv)
        for oh in range(2):
            slot = oh
            if pb_free[slot] is not None:
                te.wait_ge(pb_free[slot][0].h, pb_free[slot][1])
            for oc in range(8):
                if oc == 7:
                    te.wait_ge(kb.spe.h, kb.spe.n)
                last = te.matmul(pB[:, slot, :], lhsT=yhT[:, oc, :],
                                 rhs=wou[:, oc, oh * 512:(oh + 1) * 512],
                                 start=(oc == 0), stop=(oc == 7))
            np_ = kb.spe.inc(last)
            v.wait_ge(kb.spe.h, np_)
            v.tensor_tensor(out=xres[:, b, oh * 512:(oh + 1) * 512], in0=pB[:, slot, :],
                            in1=xres[:, b, oh * 512:(oh + 1) * 512], op=ALU.add)
            v.tensor_tensor(out=xres[:, b, oh * 512:(oh + 1) * 512],
                            in0=xres[:, b, oh * 512:(oh + 1) * 512],
                            in1=bb[:, oh * 512:(oh + 1) * 512], op=ALU.add)
            pb_free[slot] = (kb.sdv, kb.sdv.n)
        nc.scalar.wait_ge(kb.sdv.h, kb.sdv.n)
        stsem[b].inc(nc.scalar.dma_start(out=dst[c * 128:(c + 1) * 128, :],
                                         in_=xres[:, b, :]), 16)
        store_done.append((stsem[b], stsem[b].n))
        if c + 2 < NCH:
            emit_x_load(c + 2)

    kb.barrier()
    es.close()



def emit_quant_ffn0_weights(kb, wg_nat, wu_nat, wd_nat, s_wg, s_wu, s_wd, ident):
    nc = kb.nc
    v, s, g = kb.v, kb.s, kb.g
    te, sy = nc.tensor, nc.sync
    es = ExitStack()
    pfx = "q0w" + kb.uid() + "_"
    sb = lambda n, sh, dt: kb.sb(es, pfx + n, sh, dt)
    psal = lambda n, sh, dt: kb.psum(es, pfx + n, sh, dt)

    wnat = sb("wnat", [128, 2, I_FF0], F32)
    qb = sb("qb", [128, 2, I_FF0], BF16)
    accg = sb("accg", [128, 8, 512], BF16)
    accd = sb("accd", [128, N_IC0, D], BF16)
    qsc = [QuantScratch(kb, es, pfx + "qA", I_FF0), QuantScratch(kb, es, pfx + "qB", I_FF0)]
    ptr = [psal("ptrA", [128, 1024], BF16), psal("ptrB", [128, 1024], BF16)]

    xsem = [kb.sx, kb.sxb]
    ptr_free = {0: None, 1: None}
    nat_free = {0: None, 1: None}
    acc_free = 0
    qb_spe = {0: 0, 1: 0}

    sy.wait_ge(kb.sw2.h, kb.sw2.n)

    def do_weight(nat, scratch, O, F, is_down):
        nonlocal acc_free
        nrc = O // 128
        nfc = F // 128

        def finish(rc):
            par = rc % 2
            v.wait_ge(kb.spe.h, qb_spe[par])
            dv = emit_quant_back(kb, qsc[par], qb[:, par, 0:F].unsqueeze(1), 1, F)
            ngrp = (nfc + 3) // 4
            for grp in range(ngrp):
                pb = grp % 2
                n_in = min(4, nfc - grp * 4)
                if ptr_free[pb] is not None:
                    te.wait_ge(ptr_free[pb][0].h, ptr_free[pb][1])
                if grp == 0:
                    te.wait_ge(kb.sdv.h, dv)
                for k in range(n_in):
                    fc = grp * 4 + k
                    if k == n_in - 1:
                        te.wait_ge(kb.spe.h, kb.spe.n)
                    last = te.transpose(out=ptr[pb][:, k * 128:(k + 1) * 128],
                                        in_=qb[:, par, fc * 128:(fc + 1) * 128], identity=ident)
                np_ = kb.spe.inc(last)
                s.wait_ge(kb.spe.h, np_)
                if is_down:
                    s.copy(out=accd[:, grp * 4:grp * 4 + n_in, rc * 128:(rc + 1) * 128],
                           in_=ptr[pb][:, 0:n_in * 128])
                else:
                    og = rc // 4
                    ow = (rc % 4) * 128
                    if grp == 0:
                        s.wait_ge(kb.sw2.h, acc_free)
                    s.copy(out=accg[:, grp * 4:grp * 4 + n_in, ow:ow + 128],
                           in_=ptr[pb][:, 0:n_in * 128])
                ptr_free[pb] = (kb.sac, kb.sac.n)
            qb_spe[par] = kb.spe.n

        def flush(rc):
            nonlocal acc_free
            if not is_down:
                og = rc // 4
                if (rc % 4 == 3) or (rc == nrc - 1):
                    szo = OG_SIZES[og]
                    off = 128 * 8 * OG_OFF[og]
                    nc.sync.wait_ge(kb.sac.h, kb.sac.n)
                    kb.sw2.inc(sy.dma_start(
                        out=scratch[off:off + 128 * 8 * szo].rearrange(
                            "(p fc o) -> p fc o", p=128, fc=8),
                        in_=accg[:, :, 0:szo]), 16)
                    acc_free = kb.sw2.n

        for rc in range(nrc):
            par = rc % 2
            if nat_free[par] is not None:
                sy.wait_ge(nat_free[par][0].h, nat_free[par][1])
            xsem[par].inc(sy.dma_start(
                out=wnat[:, par, 0:F],
                in_=nat[rc * 128 * F:(rc + 1) * 128 * F].rearrange("(p f) -> p f", p=128)), 16)
            ld = (xsem[par], xsem[par].n)
            v.wait_ge(ld[0].h, ld[1])
            s.wait_ge(ld[0].h, ld[1])
            emit_quant_front(kb, qsc[par], wnat[:, par, 0:F].unsqueeze(1), 1, F)
            nat_free[par] = (kb.sdv, kb.sdv.n)
            if rc > 0:
                finish(rc - 1)
                flush(rc - 1)
        finish(nrc - 1)
        flush(nrc - 1)
        if is_down:
            nc.sync.wait_ge(kb.sac.h, kb.sac.n)
            kb.sw2.inc(sy.dma_start(
                out=scratch[:].rearrange("(p ic o) -> p ic o", p=128, ic=N_IC0),
                in_=accd[:, :, :]), 16)

    do_weight(wg_nat, s_wg, I_FF0, D, False)
    do_weight(wu_nat, s_wu, I_FF0, D, False)
    do_weight(wd_nat, s_wd, D, I_FF0, True)
    qmarker = kb.sw2.n
    kb.barrier()
    es.close()
    return qmarker


def build_program(phases=("ff0a",)):
    kb = KB()
    nc = kb.nc
    es = kb.es

    x_in = nc.dram_tensor("x_in", [T, D], F32, kind="ExternalInput").ap()
    out = nc.dram_tensor("out", [T, D], F32, kind="ExternalOutput").ap()
    c_ident = nc.dram_tensor("c_ident", [128, 128], BF16, kind="ExternalInput").ap()
    c_identf = nc.dram_tensor("c_identf", [128, 128], F32, kind="ExternalInput").ap()

    n_gu = 128 * 8 * I_FF0
    n_wd = 128 * N_IC0 * D
    wg_nat = nc.dram_tensor("wg_nat", [I_FF0 * D], F32, kind="ExternalInput").ap()
    wu_nat = nc.dram_tensor("wu_nat", [I_FF0 * D], F32, kind="ExternalInput").ap()
    wd_nat = nc.dram_tensor("wd_nat", [D * I_FF0], F32, kind="ExternalInput").ap()
    s_wg = nc.dram_tensor("s_wg", [n_gu], BF16, kind="Internal").ap()
    s_wu = nc.dram_tensor("s_wu", [n_gu], BF16, kind="Internal").ap()
    s_wd = nc.dram_tensor("s_wd", [n_wd], BF16, kind="Internal").ap()

    xA = nc.dram_tensor("xA", [T, D], F32, kind="Internal").ap()
    xB = nc.dram_tensor("xB", [T, D], F32, kind="Internal").ap()

    wqf = nc.dram_tensor("wqf", [64 * 128 * 128], F32, kind="ExternalInput").ap()
    wkf = nc.dram_tensor("wkf", [64 * 128 * 128], F32, kind="ExternalInput").ap()
    wvf = nc.dram_tensor("wvf", [8 * 128 * D], F32, kind="ExternalInput").ap()
    wof = nc.dram_tensor("wof", [8 * 128 * D], F32, kind="ExternalInput").ap()
    gnwf = nc.dram_tensor("gnwf", [128, 8], F32, kind="ExternalInput").ap()
    gnbf = nc.dram_tensor("gnbf", [128, 8], F32, kind="ExternalInput").ap()
    c_dtc = nc.dram_tensor("c_dtc", [128, NH, 128], F32, kind="ExternalInput").ap()
    c_lamc = nc.dram_tensor("c_lamc", [128, 8, 128], F32, kind="ExternalInput").ap()
    c_muc = nc.dram_tensor("c_muc", [128, 8, 128], F32, kind="ExternalInput").ap()
    c_glcc = nc.dram_tensor("c_glcc", [128, 8], F32, kind="ExternalInput").ap()
    c_b2 = nc.dram_tensor("c_b2", [2, 128], F32, kind="ExternalInput").ap()
    c_ones2 = nc.dram_tensor("c_ones2", [128, 2], F32, kind="ExternalInput").ap()
    s_wq = nc.dram_tensor("s_wq", [64 * 128 * 128], BF16, kind="Internal").ap()
    s_wk = nc.dram_tensor("s_wk", [64 * 128 * 128], BF16, kind="Internal").ap()
    s_wv = nc.dram_tensor("s_wv", [8 * 128 * D], BF16, kind="Internal").ap()
    s_wo = nc.dram_tensor("s_wo", [8 * 128 * D], BF16, kind="Internal").ap()

    n_w12 = 32 * 8 * 128 * 128
    n_h12 = 32 * 32 * 128 * 128
    n_w3 = 8 * 32 * 128 * 128
    w1f = nc.dram_tensor("w1f", [n_w12], F32, kind="ExternalInput").ap()
    w2f = nc.dram_tensor("w2f", [n_w12], F32, kind="ExternalInput").ap()
    h1f = nc.dram_tensor("h1f", [n_h12], F32, kind="ExternalInput").ap()
    h2f = nc.dram_tensor("h2f", [n_h12], F32, kind="ExternalInput").ap()
    w3f = nc.dram_tensor("w3f", [n_w3], F32, kind="ExternalInput").ap()
    s1f = nc.dram_tensor("s1f", [128, 32], F32, kind="ExternalInput").ap()
    s2f = nc.dram_tensor("s2f", [128, 32], F32, kind="ExternalInput").ap()
    sh1f = nc.dram_tensor("sh1f", [128, 32], F32, kind="ExternalInput").ap()
    sh2f = nc.dram_tensor("sh2f", [128, 32], F32, kind="ExternalInput").ap()
    s3f = nc.dram_tensor("s3f", [128, 8], F32, kind="ExternalInput").ap()
    n_qk = 16 * 8 * 128 * 128
    n_qv = 8 * 128 * D
    qkf = nc.dram_tensor("qkf", [n_qk], F32, kind="ExternalInput").ap()
    qvf = nc.dram_tensor("qvf", [n_qv], F32, kind="ExternalInput").ap()
    wouf = nc.dram_tensor("wouf", [n_qv], F32, kind="ExternalInput").ap()
    bof = nc.dram_tensor("bof", [1, D], F32, kind="ExternalInput").ap()
    c_wmask = nc.dram_tensor("c_wmask", [128, 640], F32, kind="ExternalInput").ap()
    c_mdiag = nc.dram_tensor("c_mdiag", [128, 128], F32, kind="ExternalInput").ap()
    c_mold = nc.dram_tensor("c_mold", [128, 128], F32, kind="ExternalInput").ap()
    c_ones1 = nc.dram_tensor("c_ones1", [1, 128], F32, kind="ExternalInput").ap()
    s_qk = nc.dram_tensor("s_qk", [n_qk], BF16, kind="Internal").ap()
    s_qv = nc.dram_tensor("s_qv", [n_qv], BF16, kind="Internal").ap()
    s_wou = nc.dram_tensor("s_wou", [n_qv], BF16, kind="Internal").ap()

    s_w1 = nc.dram_tensor("s_w1", [n_w12], BF16, kind="Internal").ap()
    s_w2 = nc.dram_tensor("s_w2", [n_w12], BF16, kind="Internal").ap()
    s_h1 = nc.dram_tensor("s_h1", [n_h12], BF16, kind="Internal").ap()
    s_h2 = nc.dram_tensor("s_h2", [n_h12], BF16, kind="Internal").ap()
    s_w3 = nc.dram_tensor("s_w3", [n_w3], BF16, kind="Internal").ap()

    ident = kb.sb(es, "ident", [128, 128], BF16)
    identf = kb.sb(es, "identf", [128, 128], F32)
    kb._barr = kb.sb(es, "barr", [128, 4], F32)

    c_scal = nc.dram_tensor("c_scal", [128, len(ACT_CONSTS)], F32, kind="ExternalInput").ap()
    cpool = kb.sb(es, "cpool", [128, len(ACT_CONSTS)], F32)
    for ci, val in enumerate(ACT_CONSTS):
        nc.const_aps.register(F32, val, cpool[:, ci:ci + 1])

    g_ = nc.gpsimd
    kb.sw2.inc(nc.sync.dma_start(out=ident[:], in_=c_ident[:, :]), 16)
    kb.sw2.inc(nc.sync.dma_start(out=identf[:], in_=c_identf[:, :]), 16)
    kb.sw2.inc(nc.sync.dma_start(out=cpool[:], in_=c_scal[:, :]), 16)
    for eng in [nc.tensor, nc.vector, nc.scalar, nc.gpsimd]:
        eng.wait_ge(kb.sw2.h, kb.sw2.n)

    CONV = 1 << 21
    conv_groups = [
        ("at0", [(s_wq, wqf, 64 * 128 * 128), (s_wk, wkf, 64 * 128 * 128),
                 (s_wv, wvf, 8 * 128 * D), (s_wo, wof, 8 * 128 * D)]),
        ("at1", [(s_qk, qkf, n_qk), (s_qv, qvf, n_qv), (s_wou, wouf, n_qv)]),
        ("ff1", [(s_w1, w1f, n_w12), (s_w2, w2f, n_w12), (s_h1, h1f, n_h12),
                 (s_h2, h2f, n_h12), (s_w3, w3f, n_w3)]),
    ]
    conv_marker = {}
    for gname, items in conv_groups:
        g_.wait_ge(kb.swg.h, kb.swg.n)
        for dst_f, src_f, n in items:
            for off in range(0, n, CONV):
                sz = min(CONV, n - off)
                kb.swg.inc(g_.dma_start(out=dst_f[off:off + sz], in_=src_f[off:off + sz]), 16)
        conv_marker[gname] = kb.swg.n

    emit_quant_ffn0_weights(kb, wg_nat, wu_nat, wd_nat, s_wg, s_wu, s_wd, ident[:])

    chain = list(phases)
    n_ph = len(chain)
    for i, kind in enumerate(chain):
        src = x_in if i == 0 else (xA if i % 2 == 1 else xB)
        dst = out if i == n_ph - 1 else (xA if i % 2 == 0 else xB)
        if kind == "ff0":
            emit_l0ffn(kb, src, dst, s_wg, s_wu, s_wd, ident[:])
        elif kind == "at0":
            at0w = {"wq": s_wq, "wk": s_wk, "wv": s_wv, "wo": s_wo,
                    "marker": conv_marker["at0"]}
            emit_l0attn(kb, src, dst, at0w, gnwf, gnbf, c_dtc, c_lamc,
                        c_muc, c_glcc, c_b2, c_ones2, ident[:], identf[:])
        elif kind == "ff1":
            ff1w = {"w1": s_w1, "w2": s_w2, "h1": s_h1, "h2": s_h2, "w3": s_w3,
                    "marker": conv_marker["ff1"]}
            emit_l1ffn(kb, src, dst, ff1w, s1f, s2f, sh1f, sh2f, s3f, ident[:], identf[:])
        elif kind == "at1":
            at1w = {"qk": s_qk, "v": s_qv, "wo": s_wou, "marker": conv_marker["at1"]}
            emit_l1attn(kb, src, dst, at1w, bof, c_mdiag, c_mold, c_ones1, ident[:], identf[:])

    return kb


_PREP_CACHE = {}


def prep_shared(inputs):
    c = host_consts()
    d = {
        "c_ident": c["c_ident"],
        "c_identf": np.eye(128, dtype=np.float32),
        "c_scal": np.tile(np.array(ACT_CONSTS, dtype=np.float32)[None, :], (128, 1)),
        "wg_nat": np.ascontiguousarray(inputs["L0_ffn_w_gate"]).ravel(),
        "wu_nat": np.ascontiguousarray(inputs["L0_ffn_w_up"]).ravel(),
        "wd_nat": np.ascontiguousarray(inputs["L0_ffn_w_down"]).ravel(),
        "wqf": tiles_oc_fc(inputs["L0_attn_wq"]),
        "wkf": tiles_oc_fc(inputs["L0_attn_wk"]),
        "wvf": flat_pmajor_fcwide(inputs["L0_attn_wv"]),
        "wof": flat_oc_pmajor_wide(inputs["L0_attn_wo"]),
        "gnwf": np.ascontiguousarray(inputs["L0_attn_gn_w"].reshape(8, 128).T),
        "gnbf": np.ascontiguousarray(inputs["L0_attn_gn_b"].reshape(8, 128).T),
        "c_dtc": np.ascontiguousarray(c["c_dt"].transpose(1, 0, 2)),
        "c_lamc": np.ascontiguousarray(c["c_lam"].transpose(1, 0, 2)),
        "c_muc": np.ascontiguousarray(c["c_mu"].transpose(1, 0, 2)),
        "c_glcc": c["c_glc"],
        "c_b2": c["c_b2"],
        "c_ones2": c["c_ones2"],
        "w1f": tiles_lhsT_slabbed(inputs["L1_ffn_w1_w"], 4),
        "w2f": tiles_lhsT_slabbed(inputs["L1_ffn_w2_w"], 4),
        "h1f": tiles_lhsT_slabbed(inputs["L1_ffn_h1_w"], 2),
        "h2f": tiles_lhsT_slabbed(inputs["L1_ffn_h2_w"], 2),
        "w3f": tiles_lhsT_slabbed(inputs["L1_ffn_w3_w"], 2),
        "s1f": np.ascontiguousarray(inputs["L1_ffn_w1_s"].reshape(32, 128).T),
        "s2f": np.ascontiguousarray(inputs["L1_ffn_w2_s"].reshape(32, 128).T),
        "sh1f": np.ascontiguousarray(inputs["L1_ffn_h1_s"].reshape(32, 128).T),
        "sh2f": np.ascontiguousarray(inputs["L1_ffn_h2_s"].reshape(32, 128).T),
        "s3f": np.ascontiguousarray(inputs["L1_ffn_w3_s"].reshape(8, 128).T),
        "qkf": tiles_oc_fc(inputs["L1_attn_w_qkv"][0:2048]),
        "qvf": flat_pmajor_fcwide(inputs["L1_attn_w_qkv"][2048:3072]),
        "wouf": flat_pmajor_fcwide(inputs["L1_attn_w_out"]),
        "bof": inputs["L1_attn_b_out"].reshape(1, D),
        "c_wmask": c["c_wmask"],
        "c_mdiag": c["c_mdiag"],
        "c_mold": c["c_mold"],
        "c_ones1": c["c_ones1"],
    }
    return d


def prep_inputs_core(inputs, core, shared=None):
    if shared is None:
        shared = prep_shared(inputs)
    d = dict(shared)
    d["x_in"] = np.ascontiguousarray(inputs["L0_attn_x"][core])
    return d


def kernel(**inputs):
    kb = build_program(phases=("ff0", "at0", "ff0", "ff1", "at1", "ff1"))
    shared = prep_shared(inputs)
    in_maps = [prep_inputs_core(inputs, c, shared) for c in range(NCORE)]
    res = run_bass_kernel_spmd(kb.nc, in_maps, core_ids=list(range(NCORE)))
    return np.stack([res.results[c]["out"] for c in range(NCORE)], axis=0)


# version 13
# speedup vs baseline: 1.0921x; 1.0194x over previous
import math
from contextlib import ExitStack

import numpy as np
import ml_dtypes

import concourse.bass as bass
import concourse.mybir as mybir
from concourse.bass_utils import run_bass_kernel_spmd

F32 = mybir.dt.float32
BF16 = mybir.dt.bfloat16
AF = mybir.ActivationFunctionType
ALU = mybir.AluOpType
AX = mybir.AxisListType

T = 4096
D = 1024
NCORE = 8
I_FF0 = 2816
I_FF1 = 4096
NH = 16
HD = 64
WIN = 512
CH = 128
NCH = T // CH
LN_EPS = 1e-5
GN_EPS = 1e-5
NEG = -1.0e30
BF = ml_dtypes.bfloat16

ACT_CONSTS = [0.0, 1e-5, -4.0, -6.0, 1.0, -1.0, 0.5]

OG_SIZES = [512, 512, 512, 512, 512, 256]
OG_OFF = [0, 512, 1024, 1536, 2048, 2560]
N_IC0 = I_FF0 // 128


def gammas_np():
    lo = math.log(1.0 - 0.999)
    hi = math.log(1.0 - 0.9)
    return (1.0 - np.exp(np.linspace(lo, hi, NH, dtype=np.float32))).astype(np.float64)


def host_consts():
    g = gammas_np()
    c = {}
    c["c_ident"] = np.eye(128, dtype=BF)
    idx = np.arange(CH)
    dist = idx[None, :] - idx[:, None]
    dt = np.where(dist >= 0, np.power(g[:, None, None], -(idx[:, None] + 1.0)[None]), 0.0)
    c["c_dt"] = dt.astype(np.float32)
    lam = np.zeros((8, 128, CH))
    mu = np.zeros((8, CH, 128))
    glc = np.zeros((128, 8))
    for oc in range(8):
        for p in range(128):
            lam[oc, p, :] = np.power(g[2 * oc + p // 64], idx + 1)
            glc[p, oc] = np.power(g[2 * oc + p // 64], CH)
        for fcol in range(128):
            mu[oc, :, fcol] = np.power(g[2 * oc + fcol // 64], CH - 1 - idx)
    c["c_lam"] = lam.astype(np.float32)
    c["c_mu"] = mu.astype(np.float32)
    c["c_glc"] = glc.astype(np.float32)
    qi = np.arange(128)[:, None]
    kj = np.arange(640)[None, :]
    c["c_wmask"] = np.where((kj > qi) & (kj <= qi + WIN), 0.0, NEG).astype(np.float32)
    jj = np.arange(128)[None, :]
    c["c_mdiag"] = np.where(jj <= qi, 0.0, NEG).astype(np.float32)
    c["c_mold"] = np.where(jj > qi, 0.0, NEG).astype(np.float32)
    c["c_ones1"] = np.ones((1, 128), dtype=np.float32)
    c["c_b2"] = np.stack([(np.arange(128) < 64), (np.arange(128) >= 64)]).astype(np.float32)
    c["c_ones2"] = np.ascontiguousarray(c["c_b2"].T)
    return c


def flat_og_pmajor(w):
    O, F = w.shape
    wt = w.T.reshape(F // 128, 128, O)
    blocks = []
    for off, sz in zip(OG_OFF, OG_SIZES):
        blocks.append(np.ascontiguousarray(wt[:, :, off:off + sz].transpose(1, 0, 2)).ravel())
    return np.concatenate(blocks)


def flat_pmajor_icwide(w):
    O, F = w.shape
    wt = w.T.reshape(F // 128, 128, O).transpose(1, 0, 2)
    return np.ascontiguousarray(wt).ravel()


def tiles_oc_fc(w):
    O, F = w.shape
    wt = w.T.reshape(F // 128, 128, O // 128, 128)
    return np.ascontiguousarray(wt.transpose(2, 0, 1, 3)).ravel()


def flat_pmajor_fcwide(w):
    O, F = w.shape
    return np.ascontiguousarray(w.T.reshape(F // 128, 128, O)).ravel()


def flat_oc_pmajor_wide(w):
    return flat_pmajor_fcwide(w)


def tiles_lhsT_slabbed(w, ocps):
    O, F = w.shape
    arr = w.T.reshape(F // 128, 128, O // 128, 128).transpose(2, 0, 1, 3)
    arr = arr.reshape(O // 128 // ocps, ocps, F // 128, 128, 128).transpose(0, 3, 1, 2, 4)
    return np.ascontiguousarray(arr).ravel()


class Sem:
    def __init__(self, nc, es, name):
        self.h = es.enter_context(nc.semaphore(name))
        self.n = 0

    def inc(self, instr, k=1):
        instr.then_inc(self.h, k)
        self.n += k
        return self.n


class Eng:

    def __init__(self, raw, sem):
        self._raw = raw
        self._sem = sem

    def __getattr__(self, name):
        f = getattr(self._raw, name)

        def call(*a, **k):
            self._raw.wait_ge(self._sem.h, self._sem.n)
            ins = f(*a, **k)
            self._sem.inc(ins)
            return ins

        return call

    def wait_ge(self, h, n):
        self._raw.wait_ge(h, n)


class KB:
    def __init__(self):
        self.nc = bass.Bass("TRN2", target_bir_lowering=False)
        self.es = ExitStack()
        for name in ["sx", "sxb", "sw", "swb", "sw2", "sst", "sstb", "spe", "sdv", "sac", "spo", "scv", "swg"]:
            setattr(self, name, Sem(self.nc, self.es, name))
        self.phase_no = 0
        self._uid = 0
        self.v = Eng(self.nc.vector, self.sdv)
        self.s = Eng(self.nc.scalar, self.sac)
        self.g = Eng(self.nc.gpsimd, self.spo)

    def sb(self, es, name, shape, dtype):
        return es.enter_context(self.nc.sbuf_tensor(name, shape, dtype))

    def psum(self, es, name, shape, dtype):
        return es.enter_context(self.nc.psum_tensor(name, shape, dtype))

    def uid(self):
        self._uid += 1
        return str(self._uid)

    def barrier(self):
        nc = self.nc
        self.sdv.inc(nc.vector.memset(self._barr[:, 0:1], 0.0))
        self.sac.inc(nc.scalar.memzero(self._barr[:, 1:2]))
        nc.tensor.wait_ge(self.spe.h, self.spe.n)
        self.spe.inc(nc.tensor.nop())
        self.spo.inc(nc.gpsimd.memset(self._barr[:, 2:3], 0.0))
        nc.sync.wait_ge(self.sdv.h, self.sdv.n)
        nc.sync.wait_ge(self.sac.h, self.sac.n)
        nc.sync.wait_ge(self.spe.h, self.spe.n)
        nc.sync.wait_ge(self.spo.h, self.spo.n)
        nc.sync.wait_ge(self.sst.h, self.sst.n)
        nc.sync.wait_ge(self.sstb.h, self.sstb.n)
        nc.sync.wait_ge(self.sx.h, self.sx.n)
        nc.sync.wait_ge(self.sw.h, self.sw.n)
        nc.sync.wait_ge(self.sw2.h, self.sw2.n)
        nc.sync.wait_ge(self.swg.h, self.swg.n)
        self.phase_no += 1
        self.scv.inc(self.nc.sync.nop())
        for eng in [nc.vector, nc.scalar, nc.tensor, nc.gpsimd]:
            eng.wait_ge(self.scv.h, self.scv.n)


class QuantScratch:
    def __init__(self, kb, es, name, F):
        self.F = F
        self.bm = kb.sb(es, name + "_bm", [128, F // 16], F32)
        self.r = kb.sb(es, name + "_r", [128, F // 16], F32)
        self.xn = kb.sb(es, name + "_xn", [128, F], F32)
        self.absn = kb.sb(es, name + "_absn", [128, F], F32)
        self.sg = kb.sb(es, name + "_sg", [128, F], BF16)
        self.u = kb.sb(es, name + "_u", [128, F], BF16)
        self.r1 = kb.sb(es, name + "_r1", [128, F], BF16)
        self.r2 = kb.sb(es, name + "_r2", [128, F], BF16)


def emit_quant(kb, q, in3, out3, nj, sz, eng="v"):
    v, s = kb.v, kb.s
    e = kb.v if eng == "v" else kb.g
    esem = kb.sdv if eng == "v" else kb.spo
    F = nj * sz
    nb = F // 16
    r4 = lambda ap: ap.rearrange("p j (g k) -> p j g k", k=16)
    bm3 = q.bm[:, 0:nb].rearrange("p (j g) -> p j g", j=nj)
    fl = lambda buf: buf[:, 0:F]
    v.tensor_reduce(out=q.bm[:, 0:nb], in_=r4(in3), axis=AX.X, op=ALU.max,
                    apply_absolute_value=True)
    v.tensor_scalar(out=q.bm[:, 0:nb], in0=q.bm[:, 0:nb], scalar1=1.0 / 6.0, scalar2=1e-12,
                    op0=ALU.mult, op1=ALU.max)
    v.reciprocal(out=q.r[:, 0:nb], in_=q.bm[:, 0:nb])
    if eng != "v":
        e.wait_ge(kb.sdv.h, kb.sdv.n)
    e.tensor_tensor(out=r4(q.xn[:, 0:F].rearrange("p (j z) -> p j z", j=nj)), in0=r4(in3),
                    in1=q.r[:, 0:nb].rearrange("p (j g) -> p j g", j=nj).unsqueeze(3)
                    .broadcast_to([128, nj, sz // 16, 16]),
                    op=ALU.mult)
    s.wait_ge(esem.h, esem.n)
    s.activation(out=fl(q.absn), in_=fl(q.xn), func=AF.Abs)
    s.activation(out=fl(q.sg), in_=fl(q.xn), func=AF.Sign)
    e.wait_ge(kb.sac.h, kb.sac.n)
    if eng == "v":
        e.tensor_scalar(out=fl(q.u), in0=fl(q.absn), scalar1=0.25, scalar2=None, op0=ALU.is_gt)
        for th in (0.75, 1.25, 1.75, 2.5, 3.5, 5.0):
            e.scalar_tensor_tensor(out=fl(q.u), in0=fl(q.absn), scalar=th, in1=fl(q.u),
                                   op0=ALU.is_gt, op1=ALU.add)
    else:
        e.tensor_scalar(out=fl(q.u), in0=fl(q.absn), scalar1=0.25, scalar2=None, op0=ALU.is_gt)
        for th in (0.75, 1.25, 1.75, 2.5, 3.5, 5.0):
            e.tensor_scalar(out=fl(q.r1), in0=fl(q.absn), scalar1=th, scalar2=None, op0=ALU.is_gt)
            e.tensor_tensor(out=fl(q.u), in0=fl(q.u), in1=fl(q.r1), op=ALU.add)
    s.wait_ge(esem.h, esem.n)
    s.activation(out=fl(q.r1), in_=fl(q.u), func=AF.Relu, bias=-4.0, scale=1.0)
    s.activation(out=fl(q.r2), in_=fl(q.u), func=AF.Relu, bias=-6.0, scale=1.0)
    e.wait_ge(kb.sac.h, kb.sac.n)
    if eng == "v":
        e.scalar_tensor_tensor(out=fl(q.u), in0=fl(q.u), scalar=0.5, in1=fl(q.r2),
                               op0=ALU.mult, op1=ALU.add)
        e.scalar_tensor_tensor(out=fl(q.u), in0=fl(q.r1), scalar=0.5, in1=fl(q.u),
                               op0=ALU.mult, op1=ALU.add)
    else:
        e.tensor_scalar(out=fl(q.u), in0=fl(q.u), scalar1=0.5, scalar2=None, op0=ALU.mult)
        e.tensor_scalar(out=fl(q.r1), in0=fl(q.r1), scalar1=0.5, scalar2=None, op0=ALU.mult)
        e.tensor_tensor(out=fl(q.u), in0=fl(q.u), in1=fl(q.r2), op=ALU.add)
        e.tensor_tensor(out=fl(q.u), in0=fl(q.u), in1=fl(q.r1), op=ALU.add)
    e.tensor_tensor(out=fl(q.u), in0=fl(q.u), in1=fl(q.sg), op=ALU.mult)
    e.tensor_tensor(out=r4(out3), in0=r4(q.u[:, 0:F].rearrange("p (j z) -> p j z", j=nj)),
                    in1=q.bm[:, 0:nb].rearrange("p (j g) -> p j g", j=nj).unsqueeze(3)
                    .broadcast_to([128, nj, sz // 16, 16]),
                    op=ALU.mult)
    return esem.n


def emit_quant_front(kb, q, in3, nj, sz):
    v, s = kb.v, kb.s
    F = nj * sz
    nb = F // 16
    r4 = lambda ap: ap.rearrange("p j (g k) -> p j g k", k=16)
    v.tensor_reduce(out=q.bm[:, 0:nb], in_=r4(in3), axis=AX.X, op=ALU.max,
                    apply_absolute_value=True)
    v.tensor_scalar(out=q.bm[:, 0:nb], in0=q.bm[:, 0:nb], scalar1=1.0 / 6.0, scalar2=1e-12,
                    op0=ALU.mult, op1=ALU.max)
    v.reciprocal(out=q.r[:, 0:nb], in_=q.bm[:, 0:nb])
    v.tensor_tensor(out=r4(q.xn[:, 0:F].rearrange("p (j z) -> p j z", j=nj)), in0=r4(in3),
                    in1=q.r[:, 0:nb].rearrange("p (j g) -> p j g", j=nj).unsqueeze(3)
                    .broadcast_to([128, nj, sz // 16, 16]),
                    op=ALU.mult)
    s.wait_ge(kb.sdv.h, kb.sdv.n)
    s.activation(out=q.absn[:, 0:F], in_=q.xn[:, 0:F], func=AF.Abs)
    s.activation(out=q.sg[:, 0:F], in_=q.xn[:, 0:F], func=AF.Sign)
    q.front_sac = kb.sac.n


def emit_quant_back(kb, q, out3, nj, sz):
    v = kb.v
    F = nj * sz
    nb = F // 16
    r4 = lambda ap: ap.rearrange("p j (g k) -> p j g k", k=16)
    fl = lambda buf: buf[:, 0:F]
    v.wait_ge(kb.sac.h, q.front_sac)
    v.tensor_scalar(out=fl(q.u), in0=fl(q.absn), scalar1=0.25, scalar2=None, op0=ALU.is_gt)
    for th in (0.75, 1.25, 1.75, 2.5, 3.5, 5.0):
        v.scalar_tensor_tensor(out=fl(q.u), in0=fl(q.absn), scalar=th, in1=fl(q.u),
                               op0=ALU.is_gt, op1=ALU.add)
    v.tensor_scalar(out=fl(q.r1), in0=fl(q.u), scalar1=-4.0, scalar2=0.0,
                    op0=ALU.add, op1=ALU.max)
    v.tensor_scalar(out=fl(q.r2), in0=fl(q.u), scalar1=-6.0, scalar2=0.0,
                    op0=ALU.add, op1=ALU.max)
    v.scalar_tensor_tensor(out=fl(q.u), in0=fl(q.u), scalar=0.5, in1=fl(q.r2),
                           op0=ALU.mult, op1=ALU.add)
    v.scalar_tensor_tensor(out=fl(q.u), in0=fl(q.r1), scalar=0.5, in1=fl(q.u),
                           op0=ALU.mult, op1=ALU.add)
    v.tensor_tensor(out=fl(q.u), in0=fl(q.u), in1=fl(q.sg), op=ALU.mult)
    v.tensor_tensor(out=r4(out3), in0=r4(q.u[:, 0:F].rearrange("p (j z) -> p j z", j=nj)),
                    in1=q.bm[:, 0:nb].rearrange("p (j g) -> p j g", j=nj).unsqueeze(3)
                    .broadcast_to([128, nj, sz // 16, 16]),
                    op=ALU.mult)
    return kb.sdv.n


class LNScratch:
    def __init__(self, kb, es, name, nj):
        self.st6 = kb.sb(es, name + "_st6", [128, nj, 12], F32)
        self.mv = kb.sb(es, name + "_mv", [128, nj, 2], F32)
        self.rstd = kb.sb(es, name + "_rstd", [128, nj], F32)
        self.bneg = kb.sb(es, name + "_bneg", [128, nj], F32)
        self.sq = kb.sb(es, name + "_sq", [128, nj], F32)


def emit_ln_stats(kb, ln, xres_ap, nj):
    v, s = kb.v, kb.s
    for j in range(nj):
        v.bn_stats(out=ln.st6[:, j, 0:6], in_=xres_ap[:, j, 0:512])
        v.bn_stats(out=ln.st6[:, j, 6:12], in_=xres_ap[:, j, 512:1024])
    for j in range(nj):
        v.bn_aggr(out=ln.mv[:, j, 0:2], in_=ln.st6[:, j, 0:12])
    s.wait_ge(kb.sdv.h, kb.sdv.n)
    s.activation(out=ln.sq[:, 0:nj], in_=ln.mv[:, :, 1], func=AF.Sqrt, bias=LN_EPS, scale=1.0)
    v.wait_ge(kb.sac.h, kb.sac.n)
    v.reciprocal(out=ln.rstd[:, 0:nj], in_=ln.sq[:, 0:nj])
    v.scalar_tensor_tensor(out=ln.bneg[:, 0:nj], in0=ln.mv[:, :, 0], scalar=-1.0,
                           in1=ln.rstd[:, 0:nj], op0=ALU.mult, op1=ALU.mult)
    return kb.sdv.n


def emit_l0ffn(kb, src, dst, s_wg, s_wu, s_wd, ident):
    nc = kb.nc
    v, s, g = kb.v, kb.s, kb.g
    te, sy = nc.tensor, nc.sync
    es = ExitStack()
    pfx = "ff0" + kb.uid() + "_"
    sb = lambda n, sh, dt: kb.sb(es, pfx + n, sh, dt)
    psal = lambda n, sh, dt: kb.psum(es, pfx + n, sh, dt)

    NJ, TT = 4, 512
    NTILE = T // TT

    xres = sb("xres", [128, 2, NJ, D], F32)
    xln = sb("xln", [128, 2, D], F32)
    xq = sb("xq", [128, NJ, D], BF16)
    xqT = sb("xqT", [128, 8, TT], BF16)
    hslab = sb("hslab", [128, 2, 512], F32)
    sa = sb("sa", [128, 2, 512], F32)
    hqp = sb("hqp", [128, 2, I_FF0], BF16)
    hqT = sb("hqT", [128, N_IC0, 256], BF16)
    wgbuf = sb("wgbuf", [128, 2, 8, 512], BF16)
    wubuf = sb("wubuf", [128, 2, 8, 512], BF16)
    wd = sb("wd", [128, N_IC0, D], BF16)
    qA = QuantScratch(kb, es, pfx + "qA", 1024)
    qB = QuantScratch(kb, es, pfx + "qB", 1024)
    qs = [qA, qB]
    ln = LNScratch(kb, es, pfx + "ln", NJ)

    pgu = psal("pgu", [128, 4, 512], F32)
    pd = psal("pd", [128, 2, 512], F32)
    ptr = [psal("ptrA", [128, 1024], BF16), psal("ptrB", [128, 1024], BF16)]

    sy.wait_ge(kb.sw2.h, kb.sw2.n)
    kb.sw2.inc(sy.dma_start(out=wd[:], in_=s_wd.rearrange("(p ic o) -> p ic o", p=128, ic=N_IC0)), 16)
    wd_ready = kb.sw2.n

    xsem = [kb.sx, kb.sxb]
    wsem = [kb.sw, kb.swb]
    stsem = [kb.sst, kb.sstb]
    slab_free = {0: 0, 1: 0}
    slab_ready = {}
    slab_count = [0]
    ptr_free = {0: None, 1: None}
    gu_free_g = {0: 0, 2: 0}
    gu_free_u = {1: 0, 3: 0}
    pd_free = {0: 0, 1: 0}
    store_done = []
    x_ready = {}
    pending = []

    def flush_pending():
        for (bb_, j_, oh_, np2) in pending:
            v.wait_ge(kb.spe.h, np2)
            v.scalar_tensor_tensor(out=xres[:, bb_, j_, oh_ * 512:(oh_ + 1) * 512],
                                   in0=pd[:, oh_, :], scalar=0.5,
                                   in1=xres[:, bb_, j_, oh_ * 512:(oh_ + 1) * 512],
                                   op0=ALU.mult, op1=ALU.add)
            pd_free[oh_] = kb.sdv.n
        pending.clear()

    def slab_ap(flat, og):
        off = 128 * 8 * OG_OFF[og]
        szz = OG_SIZES[og]
        return flat[off: off + 128 * 8 * szz].rearrange("(p fc o) -> p fc o", p=128, fc=8)

    def emit_slab_loads(key):
        ti_, pair_, og_ = key
        pr = slab_count[0] % 2
        slab_count[0] += 1
        sy.wait_ge(kb.spe.h, slab_free[pr])
        szz = OG_SIZES[og_]
        wsem[pr].inc(sy.dma_start(out=wgbuf[:, pr, :, 0:szz], in_=slab_ap(s_wg, og_)), 16)
        wsem[pr].inc(sy.dma_start(out=wubuf[:, pr, :, 0:szz], in_=slab_ap(s_wu, og_)), 16)
        slab_ready[key] = (wsem[pr], wsem[pr].n, pr)

    def slab_key_seq():
        for ti_ in range(NTILE):
            for pair_ in range(2):
                for og_ in range(6):
                    yield (ti_, pair_, og_)

    slab_seq = list(slab_key_seq())
    slab_pos = [0]

    def pump_slabs(upto):
        while slab_pos[0] < min(upto, len(slab_seq)):
            emit_slab_loads(slab_seq[slab_pos[0]])
            slab_pos[0] += 1

    def emit_x_load(ti):
        b = ti % 2
        if ti >= 2:
            sy.wait_ge(store_done[ti - 2][0].h, store_done[ti - 2][1])
        ap = src[ti * TT:(ti + 1) * TT, :].rearrange("(j p) f -> p j f", p=128)
        xsem[b].inc(sy.dma_start(out=xres[:, b, :, :], in_=ap), 16)
        x_ready[ti] = (xsem[b], xsem[b].n)

    emit_x_load(0)
    pump_slabs(2)

    def emit_store(tprev):
        flush_pending()
        nc.scalar.wait_ge(kb.sdv.h, kb.sdv.n)
        out_ap = dst[tprev * TT:(tprev + 1) * TT, :].rearrange("(j p) f -> p j f", p=128)
        stsem[tprev % 2].inc(nc.scalar.dma_start(out=out_ap, in_=xres[:, tprev % 2, :, :]), 16)
        store_done.append((stsem[tprev % 2], stsem[tprev % 2].n))

    for ti in range(NTILE):
        b = ti % 2
        v.wait_ge(x_ready[ti][0].h, x_ready[ti][1])
        s.wait_ge(x_ready[ti][0].h, x_ready[ti][1])
        emit_ln_stats(kb, ln, xres[:, b], NJ)
        def xq_transposes(j, xq_dv):
            for grp in range(2):
                pb = grp
                if ptr_free[pb] is not None:
                    te.wait_ge(ptr_free[pb][0].h, ptr_free[pb][1])
                te.wait_ge(kb.sdv.h, xq_dv)
                for k in range(4):
                    fc = grp * 4 + k
                    if k == 3:
                        te.wait_ge(kb.spe.h, kb.spe.n)
                    last = te.transpose(out=ptr[pb][:, k * 128:(k + 1) * 128],
                                        in_=xq[:, j, fc * 128:(fc + 1) * 128], identity=ident)
                np_ = kb.spe.inc(last)
                s.wait_ge(kb.spe.h, np_)
                s.copy(out=xqT[:, grp * 4:(grp + 1) * 4, j * 128:(j + 1) * 128],
                       in_=ptr[pb][:, 0:512])
                ptr_free[pb] = (kb.sac, kb.sac.n)

        for j in range(NJ):
            s.wait_ge(kb.sdv.h, kb.sdv.n)
            s.activation(out=xln[:, j % 2, :], in_=xres[:, b, j, :], func=AF.Identity,
                         scale=ln.rstd[:, j:j + 1], bias=ln.bneg[:, j:j + 1])
            v.wait_ge(kb.sac.h, kb.sac.n)
            emit_quant_front(kb, qs[j % 2], xln[:, j % 2, :].unsqueeze(1), 1, D)
            if j > 0:
                dv = emit_quant_back(kb, qs[(j - 1) % 2], xq[:, j - 1, :].unsqueeze(1), 1, D)
                xq_transposes(j - 1, dv)
        dv = emit_quant_back(kb, qs[(NJ - 1) % 2], xq[:, NJ - 1, :].unsqueeze(1), 1, D)
        xq_transposes(NJ - 1, dv)
        xqT_ready = kb.sac.n
        te.wait_ge(kb.sac.h, xqT_ready)
        if ti > 0:
            emit_store(ti - 1)
        if ti + 1 < NTILE:
            emit_x_load(ti + 1)

        for pair in range(2):
            back_pend = None
            for og in range(6):
                sz = OG_SIZES[og]
                key = (ti, pair, og)
                sr = slab_ready[key]
                for jj in range(2):
                    j = 2 * pair + jj
                    gslot = jj * 2
                    uslot = gslot + 1
                    te.wait_ge(sr[0].h, sr[1])
                    te.wait_ge(kb.sac.h, gu_free_g[gslot])
                    te.wait_ge(kb.sdv.h, gu_free_u[uslot])
                    for fc in range(8):
                        te.matmul(pgu[:, gslot, 0:sz], lhsT=xqT[:, fc, j * 128:(j + 1) * 128],
                                  rhs=wgbuf[:, sr[2], fc, 0:sz], start=(fc == 0), stop=(fc == 7))
                    for fc in range(8):
                        if fc == 7:
                            te.wait_ge(kb.spe.h, kb.spe.n)
                        last = te.matmul(pgu[:, uslot, 0:sz],
                                         lhsT=xqT[:, fc, j * 128:(j + 1) * 128],
                                         rhs=wubuf[:, sr[2], fc, 0:sz],
                                         start=(fc == 0), stop=(fc == 7))
                    np_ = kb.spe.inc(last)
                    if jj == 1:
                        slab_free[sr[2]] = np_
                        pump_slabs(slab_pos[0] + 1)
                    s.wait_ge(kb.spe.h, np_)
                    s.activation(out=sa[:, jj, 0:sz], in_=pgu[:, gslot, 0:sz], func=AF.Silu)
                    gu_free_g[gslot] = kb.sac.n
                    v.wait_ge(kb.sac.h, kb.sac.n)
                    v.tensor_tensor(out=hslab[:, jj, 0:sz], in0=sa[:, jj, 0:sz],
                                    in1=pgu[:, uslot, 0:sz], op=ALU.mult)
                    gu_free_u[uslot] = kb.sdv.n
                emit_quant_front(kb, qs[og % 2], hslab[:, 0:2, 0:sz], 2, sz)
                if back_pend is not None:
                    emit_quant_back(kb, qs[back_pend[0] % 2],
                                    hqp[:, 0:2, OG_OFF[back_pend[0]]:OG_OFF[back_pend[0]] + back_pend[1]],
                                    2, back_pend[1])
                back_pend = (og, sz)
            dv = emit_quant_back(kb, qs[back_pend[0] % 2],
                                 hqp[:, 0:2, OG_OFF[back_pend[0]]:OG_OFF[back_pend[0]] + back_pend[1]],
                                 2, back_pend[1])
            hq_done = (kb.sdv, dv)
            for jj in range(2):
                j = 2 * pair + jj
                for grp in range(6):
                    pb = grp % 2
                    n_in = min(4, N_IC0 - grp * 4)
                    if ptr_free[pb] is not None:
                        te.wait_ge(ptr_free[pb][0].h, ptr_free[pb][1])
                    if grp == 0 and jj == 0:
                        te.wait_ge(hq_done[0].h, hq_done[1])
                    for k in range(n_in):
                        ic = grp * 4 + k
                        if k == n_in - 1:
                            te.wait_ge(kb.spe.h, kb.spe.n)
                        last = te.transpose(out=ptr[pb][:, k * 128:(k + 1) * 128],
                                            in_=hqp[:, jj, ic * 128:(ic + 1) * 128],
                                            identity=ident)
                    np_ = kb.spe.inc(last)
                    s.wait_ge(kb.spe.h, np_)
                    s.copy(out=hqT[:, grp * 4:grp * 4 + n_in, jj * 128:(jj + 1) * 128],
                           in_=ptr[pb][:, 0:n_in * 128])
                    ptr_free[pb] = (kb.sac, kb.sac.n)
                hqT_ready = kb.sac.n
                te.wait_ge(kb.sac.h, hqT_ready)
                if ti == 0 and pair == 0 and jj == 0:
                    te.wait_ge(kb.sw2.h, wd_ready)
                for oh in range(2):
                    if any(p_[2] == oh for p_ in pending):
                        flush_pending()
                    te.wait_ge(kb.sdv.h, pd_free[oh])
                    for ic in range(N_IC0):
                        if ic == N_IC0 - 1:
                            te.wait_ge(kb.spe.h, kb.spe.n)
                        last = te.matmul(pd[:, oh, :], lhsT=hqT[:, ic, jj * 128:(jj + 1) * 128],
                                         rhs=wd[:, ic, oh * 512:(oh + 1) * 512],
                                         start=(ic == 0), stop=(ic == N_IC0 - 1))
                    np_ = kb.spe.inc(last)
                    pending.append((b, j, oh, np_))

    emit_store(NTILE - 1)

    kb.barrier()
    es.close()



def emit_l0attn(kb, src, dst, wqf, gnwf, gnbf, c_dtc, c_lamc, c_muc,
                c_glcc, c_b2, c_ones2, ident, identf):
    nc = kb.nc
    v, s = kb.v, kb.s
    te, sy, g_ = nc.tensor, nc.sync, nc.gpsimd
    es = ExitStack()
    pfx = "at0" + kb.uid() + "_"
    sb = lambda n, sh, dt: kb.sb(es, pfx + n, sh, dt)
    psal = lambda n, sh, dt: kb.psum(es, pfx + n, sh, dt)

    SC = 512
    NSC = T // SC
    NJ = 4

    xres = sb("xres", [128, NJ, D], F32)
    xhat = sb("xhat", [128, D], BF16)
    xhT = sb("xhT", [128, 8, SC], BF16)
    qf = sb("qf", [128, 8, SC], BF16)
    kf = sb("kf", [128, 8, SC], BF16)
    vt = sb("vt", [128, NJ, D], BF16)
    ktl = sb("ktl", [128, NJ, 8, 128], BF16)
    sdtAll = sb("sdtAll", [128, NH, 128], BF16)
    yf = sb("yf", [128, 8, T], BF16)
    S_ = sb("S", [128, 8, HD], F32)
    S_bf = sb("Sbf", [128, 8, HD], BF16)
    wq_sb = sb("wq", [128, 64, 128], BF16)
    wk_sb = sb("wk", [128, 64, 128], BF16)
    wv_sb = sb("wv", [128, 8, D], BF16)
    dtc = sb("dtc", [128, NH, 128], F32)
    lamc = sb("lamc", [128, 8, 128], F32)
    muc = sb("muc", [128, 8, 128], F32)
    glc = sb("glc", [128, 8], F32)
    gnw = sb("gnw", [128, 8], F32)
    gnb = sb("gnb", [128, 8], F32)
    b2c = sb("b2c", [2, 128], F32)
    ones2 = sb("ones2", [128, 2], F32)
    st6c = sb("st6c", [128, 8, 48], F32)
    mvv = sb("mvv", [128, 8, 2], F32)
    ey2 = sb("ey2", [128, 8], F32)
    m2 = sb("m2", [128, 8], F32)
    sr8 = sb("sr8", [8, 4], F32)
    rtT = sb("rtT", [2, 8], F32)
    mnT = sb("mnT", [2, 8], F32)
    aG = sb("aG", [128, 8], F32)
    bG = sb("bG", [128, 8], F32)
    gmn = sb("gmn", [128, 8], F32)
    ln = LNScratch(kb, es, pfx + "ln", NJ)

    pbig = psal("pbig", [128, 6, 512], F32)
    ptr = [psal("ptrA", [128, 1024], BF16), psal("ptrB", [128, 1024], BF16)]

    sy.wait_ge(kb.sw2.h, kb.sw2.n)
    sy.wait_ge(kb.swg.h, wqf["marker"])
    kb.sw2.inc(sy.dma_start(out=wq_sb[:], in_=wqf["wq"].rearrange("(a p o) -> p a o", p=128, a=64)), 16)
    kb.sw2.inc(sy.dma_start(out=wk_sb[:], in_=wqf["wk"].rearrange("(a p o) -> p a o", p=128, a=64)), 16)
    kb.sw2.inc(sy.dma_start(out=wv_sb[:], in_=wqf["wv"].rearrange("(fc p o) -> p fc o", p=128, fc=8)), 16)
    kb.sw2.inc(sy.dma_start(out=dtc[:], in_=c_dtc[:, :, :]), 16)
    kb.sw2.inc(sy.dma_start(out=lamc[:], in_=c_lamc[:, :, :]), 16)
    kb.sw2.inc(sy.dma_start(out=muc[:], in_=c_muc[:, :, :]), 16)
    kb.sw2.inc(sy.dma_start(out=glc[:], in_=c_glcc[:, :]), 16)
    kb.sw2.inc(sy.dma_start(out=gnw[:], in_=gnwf[:, :]), 16)
    kb.sw2.inc(sy.dma_start(out=gnb[:], in_=gnbf[:, :]), 16)
    kb.sw2.inc(sy.dma_start(out=b2c[:], in_=c_b2[:, :]), 16)
    kb.sw2.inc(sy.dma_start(out=ones2[:], in_=c_ones2[:, :]), 16)
    w_ready = (0, kb.sw2.n)
    xsem = [kb.sx, kb.sxb]

    v.memset(S_[:], 0.0)
    v.memset(S_bf[:], 0.0)
    sbf_done = kb.sdv.n

    ptr_free = {0: None, 1: None}
    psS_free = {0: 0, 1: 0}
    psO_free = {0: 0, 1: 0, 2: 0, 3: 0}
    psDS_free = {4: 0, 5: 0}
    qkv_free = 0
    xres_free = 0
    store_done = 0

    te.wait_ge(kb.sw2.h, w_ready[1])

    def emit_x_load(idx):
        sy.wait_ge(kb.sac.h, xres_free)
        sy.wait_ge(kb.sst.h, store_done)
        ap = src[idx * SC:(idx + 1) * SC, :].rearrange("(j p) f -> p j f", p=128)
        xsem[idx % 2].inc(sy.dma_start(out=xres[:, :, :], in_=ap), 16)
        return (xsem[idx % 2], xsem[idx % 2].n)

    x_ready = emit_x_load(0)

    for sc in range(NSC):
        v.wait_ge(x_ready[0].h, x_ready[1])
        s.wait_ge(x_ready[0].h, x_ready[1])
        emit_ln_stats(kb, ln, xres[:, :, :], NJ)
        for j in range(NJ):
            s.wait_ge(kb.sdv.h, kb.sdv.n)
            s.wait_ge(kb.spe.h, kb.spe.n)
            s.activation(out=xhat[:], in_=xres[:, j, :], func=AF.Identity,
                         scale=ln.rstd[:, j:j + 1], bias=ln.bneg[:, j:j + 1])
            if j == NJ - 1:
                xres_free = kb.sac.n
            for grp in range(2):
                pb = grp
                if ptr_free[pb] is not None:
                    te.wait_ge(ptr_free[pb][0].h, ptr_free[pb][1])
                te.wait_ge(kb.sac.h, kb.sac.n)
                for k in range(4):
                    fc = grp * 4 + k
                    if k == 3:
                        te.wait_ge(kb.spe.h, kb.spe.n)
                    last = te.transpose(out=ptr[pb][:, k * 128:(k + 1) * 128],
                                        in_=xhat[:, fc * 128:(fc + 1) * 128], identity=ident)
                np_ = kb.spe.inc(last)
                s.wait_ge(kb.spe.h, np_)
                s.copy(out=xhT[:, grp * 4:(grp + 1) * 4, j * 128:(j + 1) * 128],
                       in_=ptr[pb][:, 0:512])
                ptr_free[pb] = (kb.sac, kb.sac.n)
        xhT_ready = kb.sac.n
        te.wait_ge(kb.sac.h, xhT_ready)
        te.wait_ge(kb.sdv.h, kb.sdv.n)
        te.wait_ge(kb.sac.h, kb.sac.n)
        qk_slot_free = {0: 0, 1: 0}
        for wsel, (wtile, out_sb) in enumerate([(wq_sb, qf), (wk_sb, kf)]):
            for oc in range(8):
                slot = (2 * wsel + oc) % 2
                te.wait_ge(kb.sdv.h, qk_slot_free[slot])
                for fc in range(8):
                    if fc == 7:
                        te.wait_ge(kb.spe.h, kb.spe.n)
                    last = te.matmul(pbig[:, slot, :], lhsT=wtile[:, oc * 8 + fc, :],
                                     rhs=xhT[:, fc, :], start=(fc == 0), stop=(fc == 7))
                np_ = kb.spe.inc(last)
                v.wait_ge(kb.spe.h, np_)
                if wsel == 0:
                    v.tensor_tensor(out=qf[:, oc, :].rearrange("p (c t) -> p c t", t=128),
                                    in0=pbig[:, slot, :].rearrange("p (c t) -> p c t", t=128),
                                    in1=lamc[:, oc, :].unsqueeze(1).broadcast_to([128, NJ, 128]),
                                    op=ALU.mult)
                else:
                    v.tensor_copy(out=kf[:, oc, :], in_=pbig[:, slot, :])
                qk_slot_free[slot] = kb.sdv.n
        v_slot_free = {2: 0, 3: 0}
        for j in range(NJ):
            for oh in range(2):
                slot = 2 + (2 * j + oh) % 2
                te.wait_ge(kb.sdv.h, v_slot_free[slot])
                for fc in range(8):
                    if fc == 7:
                        te.wait_ge(kb.spe.h, kb.spe.n)
                    last = te.matmul(pbig[:, slot, :], lhsT=xhT[:, fc, j * 128:(j + 1) * 128],
                                     rhs=wv_sb[:, fc, oh * 512:(oh + 1) * 512],
                                     start=(fc == 0), stop=(fc == 7))
                np_ = kb.spe.inc(last)
                v.wait_ge(kb.spe.h, np_)
                v.tensor_copy(out=vt[:, j, oh * 512:(oh + 1) * 512], in_=pbig[:, slot, :])
                v_slot_free[slot] = kb.sdv.n
        for j in range(NJ):
            for grp in range(2):
                pb = grp
                if ptr_free[pb] is not None:
                    te.wait_ge(ptr_free[pb][0].h, ptr_free[pb][1])
                te.wait_ge(kb.sdv.h, kb.sdv.n)
                for k in range(4):
                    oc = grp * 4 + k
                    if k == 3:
                        te.wait_ge(kb.spe.h, kb.spe.n)
                    last = te.transpose(out=ptr[pb][:, k * 128:(k + 1) * 128],
                                        in_=kf[:, oc, j * 128:(j + 1) * 128], identity=ident)
                np_ = kb.spe.inc(last)
                v.wait_ge(kb.spe.h, np_)
                v.tensor_tensor(out=ktl[:, j, grp * 4:(grp + 1) * 4, :],
                                in0=ptr[pb][:, 0:512].rearrange("p (a q) -> p a q", q=128),
                                in1=muc[:, grp * 4:(grp + 1) * 4, :], op=ALU.mult)
                ptr_free[pb] = (kb.sdv, kb.sdv.n)

        if sc + 1 < NSC:
            x_ready = emit_x_load(sc + 1)

        for cl in range(NJ):
            te.wait_ge(kb.sdv.h, kb.sdv.n)
            te.wait_ge(kb.sac.h, kb.sac.n)
            for h in range(NH):
                if h == NH - 1:
                    te.wait_ge(kb.spe.h, kb.spe.n)
                hc, par = h // 2, h % 2
                hr = 64 * par
                last = te.matmul(pbig[0:128, h % 4, (h // 4) * 128:(h // 4) * 128 + 128],
                                 lhsT=kf[hr:hr + 64, hc, cl * 128:(cl + 1) * 128],
                                 rhs=qf[hr:hr + 64, hc, cl * 128:(cl + 1) * 128],
                                 start=True, stop=True)
            nsc_ = kb.spe.inc(last)
            v.wait_ge(kb.spe.h, nsc_)
            v.tensor_tensor(out=sdtAll[:, :, :].rearrange("p (q b) i -> p q b i", b=4),
                            in0=pbig[:, 0:4, :].rearrange("p b (q i) -> p q b i", i=128),
                            in1=dtc[:, :, :].rearrange("p (q b) i -> p q b i", b=4),
                            op=ALU.mult)
            sdt_dv = kb.sdv.n
            te.wait_ge(kb.sdv.h, sdt_dv)
            for h in range(NH):
                hc, par = h // 2, h % 2
                hr = 64 * par
                m = h // 2
                ob = (0 if m < 4 else 1) + 2 * par
                ocol = (m % 4) * 128
                if h in (0, 1, 8, 9):
                    te.wait_ge(kb.sac.h, psO_free[ob])
                if h < 2:
                    te.wait_ge(kb.sdv.h, psDS_free[4 + par])
                te.matmul(pbig[0:64, ob, ocol:ocol + 128], lhsT=vt[:, cl, 64 * h:64 * h + 64],
                          rhs=sdtAll[:, h, :], start=True, stop=False)
                te.wait_ge(kb.spe.h, kb.spe.n)
                last = te.matmul(pbig[0:64, ob, ocol:ocol + 128], lhsT=S_bf[hr:hr + 64, hc, :],
                                 rhs=qf[hr:hr + 64, hc, cl * 128:(cl + 1) * 128],
                                 start=False, stop=True)
                no_ = kb.spe.inc(last)
                te.wait_ge(kb.spe.h, kb.spe.n)
                last = te.matmul(pbig[0:64, 4 + par, m * 64:(m + 1) * 64],
                                 lhsT=ktl[:, cl, hc, hr:hr + 64],
                                 rhs=vt[:, cl, 64 * h:64 * h + 64], start=True, stop=True)
                nds_ = kb.spe.inc(last)
                if h in (6, 7, 14, 15):
                    s.wait_ge(kb.spe.h, no_)
                    hcb = 0 if h in (6, 7) else 4
                    s.copy(out=yf[hr:hr + 64, hcb:hcb + 4,
                                  sc * SC + cl * 128: sc * SC + (cl + 1) * 128],
                           in_=pbig[0:64, ob, 0:512].rearrange("p (a i) -> p a i", i=128))
                    psO_free[ob] = kb.sac.n
                if h >= 14:
                    v.wait_ge(kb.spe.h, nds_)
                    v.tensor_tensor(out=S_[hr:hr + 64, :, :], in0=S_[hr:hr + 64, :, :],
                                    in1=glc[hr:hr + 64, :].unsqueeze(2)
                                    .broadcast_to([64, 8, HD]), op=ALU.mult)
                    v.tensor_tensor(out=S_[hr:hr + 64, :, :], in0=S_[hr:hr + 64, :, :],
                                    in1=pbig[0:64, 4 + par, :].rearrange("p (a d) -> p a d", d=64),
                                    op=ALU.add)
                    v.tensor_copy(out=S_bf[hr:hr + 64, :, :], in_=S_[hr:hr + 64, :, :])
                    psDS_free[4 + par] = kb.sdv.n
        qkv_free = kb.spe.n

    v.wait_ge(kb.sac.h, kb.sac.n)
    for oc in range(8):
        for k in range(8):
            v.bn_stats(out=st6c[:, oc, k * 6:(k + 1) * 6], in_=yf[:, oc, k * 512:(k + 1) * 512])
    for oc in range(8):
        v.bn_aggr(out=mvv[:, oc, 0:2], in_=st6c[:, oc, 0:48])
    v.tensor_tensor(out=m2[:], in0=mvv[:, :, 0], in1=mvv[:, :, 0], op=ALU.mult)
    v.tensor_tensor(out=ey2[:], in0=mvv[:, :, 1], in1=m2[:], op=ALU.add)
    te.wait_ge(kb.sdv.h, kb.sdv.n)
    te.wait_ge(kb.spe.h, kb.spe.n)
    last = te.matmul(pbig[0:8, 0, 0:2], lhsT=mvv[:, :, 0], rhs=ones2[:], start=True, stop=True)
    te.wait_ge(kb.spe.h, kb.spe.n)
    last2 = te.matmul(pbig[0:8, 1, 0:2], lhsT=ey2[:], rhs=ones2[:], start=True, stop=True)
    nst = kb.spe.inc(last2)
    v.wait_ge(kb.spe.h, nst)
    v.tensor_scalar(out=sr8[0:8, 0:2], in0=pbig[0:8, 0, 0:2], scalar1=1.0 / 64, scalar2=None,
                    op0=ALU.mult)
    v.tensor_scalar(out=sr8[0:8, 2:4], in0=pbig[0:8, 1, 0:2], scalar1=1.0 / 64, scalar2=None,
                    op0=ALU.mult)
    v.tensor_tensor(out=ey2[0:8, 0:2], in0=sr8[0:8, 0:2], in1=sr8[0:8, 0:2], op=ALU.mult)
    v.tensor_tensor(out=ey2[0:8, 2:4], in0=sr8[0:8, 2:4], in1=ey2[0:8, 0:2], op=ALU.subtract)
    s.wait_ge(kb.sdv.h, kb.sdv.n)
    s.activation(out=m2[0:8, 0:2], in_=ey2[0:8, 2:4], func=AF.Sqrt, bias=GN_EPS, scale=1.0)
    v.wait_ge(kb.sac.h, kb.sac.n)
    v.reciprocal(out=m2[0:8, 2:4], in_=m2[0:8, 0:2])
    v.tensor_copy(out=m2[0:8, 4:6], in_=sr8[0:8, 0:2])
    te.wait_ge(kb.sdv.h, kb.sdv.n)
    te.wait_ge(kb.spe.h, kb.spe.n)
    last = te.transpose(out=pbig[0:2, 2, 0:8], in_=m2[0:8, 2:4], identity=identf[0:8, 0:8])
    te.wait_ge(kb.spe.h, kb.spe.n)
    last = te.transpose(out=pbig[0:2, 3, 0:8], in_=m2[0:8, 4:6], identity=identf[0:8, 0:8])
    ntr = kb.spe.inc(last)
    v.wait_ge(kb.spe.h, ntr)
    v.tensor_copy(out=rtT[0:2, 0:8], in_=pbig[0:2, 2, 0:8])
    v.tensor_copy(out=mnT[0:2, 0:8], in_=pbig[0:2, 3, 0:8])
    te.wait_ge(kb.sdv.h, kb.sdv.n)
    te.wait_ge(kb.spe.h, kb.spe.n)
    last = te.matmul(pbig[0:128, 4, 0:8], lhsT=b2c[:, :], rhs=rtT[0:2, 0:8], start=True, stop=True)
    te.wait_ge(kb.spe.h, kb.spe.n)
    last = te.matmul(pbig[0:128, 5, 0:8], lhsT=b2c[:, :], rhs=mnT[0:2, 0:8], start=True, stop=True)
    ngb = kb.spe.inc(last)
    v.wait_ge(kb.spe.h, ngb)
    v.tensor_tensor(out=aG[:], in0=pbig[:, 4, 0:8], in1=gnw[:], op=ALU.mult)
    v.tensor_copy(out=gmn[:], in_=pbig[:, 5, 0:8])
    v.scalar_tensor_tensor(out=bG[:], in0=gmn[:], scalar=-1.0, in1=aG[:], op0=ALU.mult, op1=ALU.mult)
    v.tensor_tensor(out=bG[:], in0=bG[:], in1=gnb[:], op=ALU.add)
    gn_ready_dv = kb.sdv.n

    sy.wait_ge(kb.spe.h, kb.spe.n)
    sy.wait_ge(kb.sw2.h, kb.sw2.n)
    kb.sw2.inc(sy.dma_start(out=wv_sb[:], in_=wqf["wo"].rearrange("(oc p o) -> p oc o", p=128, oc=8)), 16)
    wo_ready = kb.sw2.n

    s.wait_ge(kb.sdv.h, gn_ready_dv)
    te.wait_ge(kb.sw2.h, wo_ready)
    for sc in range(NSC):
        x_ready = emit_x_load(sc)
        for oc in range(8):
            s.activation(out=xhT[:, oc, :], in_=yf[:, oc, sc * SC:(sc + 1) * SC],
                         func=AF.Identity, scale=aG[:, oc:oc + 1], bias=bG[:, oc:oc + 1])
        yh_ready = kb.sac.n
        v.wait_ge(x_ready[0].h, x_ready[1])
        te.wait_ge(kb.sac.h, yh_ready)
        for j in range(NJ):
            for oh in range(2):
                slot = (2 * j + oh) % 4
                te.wait_ge(kb.sdv.h, psS_free.get(('c', slot), 0))
                for oc in range(8):
                    if oc == 7:
                        te.wait_ge(kb.spe.h, kb.spe.n)
                    last = te.matmul(pbig[:, slot, :], lhsT=xhT[:, oc, j * 128:(j + 1) * 128],
                                     rhs=wv_sb[:, oc, oh * 512:(oh + 1) * 512],
                                     start=(oc == 0), stop=(oc == 7))
                np_ = kb.spe.inc(last)
                v.wait_ge(kb.spe.h, np_)
                v.tensor_tensor(out=xres[:, j, oh * 512:(oh + 1) * 512],
                                in0=pbig[:, slot, :], in1=xres[:, j, oh * 512:(oh + 1) * 512],
                                op=ALU.add)
                psS_free[('c', slot)] = kb.sdv.n
        nc.scalar.wait_ge(kb.sdv.h, kb.sdv.n)
        out_ap = dst[sc * SC:(sc + 1) * SC, :].rearrange("(j p) f -> p j f", p=128)
        kb.sst.inc(nc.scalar.dma_start(out=out_ap, in_=xres[:, :, :]), 16)
        store_done = kb.sst.n
        xres_free = kb.sac.n if False else xres_free
    kb.barrier()
    es.close()




def emit_l1ffn(kb, src, dst, w, s1f, s2f, sh1f, sh2f, s3f, ident, identf):
    nc = kb.nc
    v, s = kb.v, kb.s
    te, sy = nc.tensor, nc.sync
    es = ExitStack()
    pfx = "ff1" + kb.uid() + "_"
    sb = lambda n, sh, dt: kb.sb(es, pfx + n, sh, dt)
    psal = lambda n, sh, dt: kb.psum(es, pfx + n, sh, dt)

    NJ, TT = 4, 512
    NTILE = T // TT
    NIC = 32
    SLABS_PER_TILE = 44
    S1O, S2O, S3O, S4O = 0, 8, 24, 40

    xres = sb("xres", [128, 2, NJ, D], F32)
    xhat = sb("xhat", [128, D], BF16)
    xhT = sb("xhT", [128, 2, 8, TT], BF16)
    hbuf = sb("h", [128, NIC, TT], BF16)
    gbuf = sb("g", [128, NIC, TT], BF16)
    warena = sb("wa", [128, 2, 8192], BF16)
    of = sb("of", [128, 2, TT], F32)
    s1 = sb("s1", [128, 32], F32)
    s2 = sb("s2", [128, 32], F32)
    sh1 = sb("sh1", [128, 32], F32)
    sh2 = sb("sh2", [128, 32], F32)
    s3h = sb("s3h", [128, 8], F32)
    ln = LNScratch(kb, es, pfx + "ln", NJ)

    pmm = psal("pmm", [128, 4, 512], F32)
    pot = psal("pot", [128, 2, 512], F32)
    ptr = [psal("ptrA", [128, 1024], BF16), psal("ptrB", [128, 1024], BF16)]

    sy.wait_ge(kb.sw2.h, kb.sw2.n)
    for dstt, srcf in [(s1, s1f), (s2, s2f), (sh1, sh1f), (sh2, sh2f)]:
        kb.sw2.inc(sy.dma_start(out=dstt[:], in_=srcf[:, :]), 16)
    kb.sw2.inc(sy.dma_start(out=s3h[:], in_=s3f[:, :]), 16)
    sc_ready = kb.sw2.n
    v.wait_ge(kb.sw2.h, sc_ready)
    v.tensor_scalar(out=s3h[:], in0=s3h[:], scalar1=0.5, scalar2=None, op0=ALU.mult)
    s3_dv = kb.sdv.n

    xsem = [kb.sx, kb.sxb]
    stsem = [kb.sst, kb.sstb]
    wsem = [kb.sw, kb.swb]
    store_done = []
    xhT_ready = {}
    ptr_free = {0: None, 1: None}
    mm_free = {0: None, 1: None, 2: None, 3: None}
    pot_free = {0: 0, 1: 0}
    of_free = {0: None, 1: None}
    slab_free = {0: 0, 1: 0}
    slab_ready = {}

    def emit_slab(pos):
        ti, rel = divmod(pos, SLABS_PER_TILE)
        pr = pos % 2
        sy.wait_ge(kb.spe.h, slab_free[pr])
        if rel < S2O:
            sidx = rel
            npp = 4096
            off = 128 * npp * sidx
            wsem[pr].inc(sy.dma_start(
                out=warena[:, pr, 0:npp],
                in_=w["w1"][off:off + 128 * npp].rearrange("(p a) -> p a", p=128)), 16)
            wsem[pr].inc(sy.dma_start(
                out=warena[:, pr, npp:2 * npp],
                in_=w["w2"][off:off + 128 * npp].rearrange("(p a) -> p a", p=128)), 16)
        else:
            if rel < S3O:
                key, sidx = "h1", rel - S2O
            elif rel < S4O:
                key, sidx = "h2", rel - S3O
            else:
                key, sidx = "w3", rel - S4O
            npp = 8192
            off = 128 * npp * sidx
            wsem[pr].inc(sy.dma_start(
                out=warena[:, pr, 0:npp],
                in_=w[key][off:off + 128 * npp].rearrange("(p a) -> p a", p=128)), 16)
        slab_ready[pos] = (wsem[pr], wsem[pr].n)

    slab_pos = 0

    def pump_slabs(upto):
        nonlocal slab_pos
        while slab_pos < min(upto, NTILE * SLABS_PER_TILE):
            emit_slab(slab_pos)
            slab_pos += 1

    x_ready = {}

    def emit_x_load(ti):
        b = ti % 2
        if ti >= 2:
            sy.wait_ge(store_done[ti - 2][0].h, store_done[ti - 2][1])
        ap = src[ti * TT:(ti + 1) * TT, :].rearrange("(j p) f -> p j f", p=128)
        xsem[b].inc(sy.dma_start(out=xres[:, b, :, :], in_=ap), 16)
        x_ready[ti] = (xsem[b], xsem[b].n)

    sy.wait_ge(kb.swg.h, w["marker"])
    emit_x_load(0)
    pump_slabs(2)

    def emit_xsection(ti):
        b = ti % 2
        v.wait_ge(x_ready[ti][0].h, x_ready[ti][1])
        s.wait_ge(x_ready[ti][0].h, x_ready[ti][1])
        emit_ln_stats(kb, ln, xres[:, b], NJ)
        for j in range(NJ):
            s.wait_ge(kb.sdv.h, kb.sdv.n)
            s.wait_ge(kb.spe.h, kb.spe.n)
            s.activation(out=xhat[:], in_=xres[:, b, j, :], func=AF.Identity,
                         scale=ln.rstd[:, j:j + 1], bias=ln.bneg[:, j:j + 1])
            for grp in range(2):
                pb = grp
                if ptr_free[pb] is not None:
                    te.wait_ge(ptr_free[pb][0].h, ptr_free[pb][1])
                te.wait_ge(kb.sac.h, kb.sac.n)
                for k in range(4):
                    fc = grp * 4 + k
                    if k == 3:
                        te.wait_ge(kb.spe.h, kb.spe.n)
                    last = te.transpose(out=ptr[pb][:, k * 128:(k + 1) * 128],
                                        in_=xhat[:, fc * 128:(fc + 1) * 128], identity=ident)
                np_ = kb.spe.inc(last)
                s.wait_ge(kb.spe.h, np_)
                s.copy(out=xhT[:, b, grp * 4:(grp + 1) * 4, j * 128:(j + 1) * 128],
                       in_=ptr[pb][:, 0:512])
                ptr_free[pb] = (kb.sac, kb.sac.n)
        xhT_ready[ti] = kb.sac.n

    emit_xsection(0)

    for ti in range(NTILE):
        b = ti % 2
        base = ti * SLABS_PER_TILE
        if ti + 1 < NTILE:
            emit_x_load(ti + 1)
        te.wait_ge(kb.sac.h, xhT_ready[ti])
        if ti == 0:
            v.wait_ge(kb.sw2.h, sc_ready)
            s.wait_ge(kb.sw2.h, sc_ready)

        for oc in range(NIC):
            sidx = oc // 4
            pos = base + S1O + sidx
            pr = pos % 2
            sr = slab_ready[pos]
            oin = oc % 4
            for half in range(2):
                slot = (2 * oc + half) % 4
                te.wait_ge(sr[0].h, sr[1])
                if mm_free[slot] is not None:
                    te.wait_ge(mm_free[slot][0].h, mm_free[slot][1])
                for fc in range(8):
                    if fc == 7:
                        te.wait_ge(kb.spe.h, kb.spe.n)
                    last = te.matmul(
                        pmm[:, slot, :],
                        lhsT=warena[:, pr, half * 4096 + (oin * 8 + fc) * 128:
                                    half * 4096 + (oin * 8 + fc) * 128 + 128],
                        rhs=xhT[:, b, fc, :], start=(fc == 0), stop=(fc == 7))
                np_ = kb.spe.inc(last)
                par = oc % 2
                if half == 0:
                    s.wait_ge(kb.spe.h, np_)
                    if of_free[par] is not None:
                        s.wait_ge(of_free[par][0].h, of_free[par][1])
                    s.activation(out=of[:, par, :], in_=pmm[:, slot, :], func=AF.Silu,
                                 scale=s1[:, oc:oc + 1])
                    mm_free[slot] = (kb.sac, kb.sac.n)
                else:
                    v.wait_ge(kb.spe.h, np_)
                    v.wait_ge(kb.sac.h, kb.sac.n)
                    v.scalar_tensor_tensor(out=hbuf[:, oc, :], in0=pmm[:, slot, :],
                                           scalar=s2[:, oc:oc + 1], in1=of[:, par, :],
                                           op0=ALU.mult, op1=ALU.mult)
                    mm_free[slot] = (kb.sdv, kb.sdv.n)
                    of_free[par] = (kb.sdv, kb.sdv.n)
            if oin == 3:
                slab_free[pr] = kb.spe.n
                pump_slabs(pos + 3)
        h_ready_dv = kb.sdv.n

        stage_out_sac = 0
        for stage, soff, inbuf, outbuf, ssc in [(2, S2O, hbuf, gbuf, sh1),
                                                (3, S3O, gbuf, hbuf, sh2)]:
            for oc in range(NIC):
                sidx = oc // 2
                pos = base + soff + sidx
                pr = pos % 2
                sr = slab_ready[pos]
                oin = oc % 2
                slot = oc % 4
                te.wait_ge(sr[0].h, sr[1])
                if oc == 0:
                    if stage == 2:
                        te.wait_ge(kb.sdv.h, h_ready_dv)
                    else:
                        te.wait_ge(kb.sac.h, stage_out_sac)
                if mm_free[slot] is not None:
                    te.wait_ge(mm_free[slot][0].h, mm_free[slot][1])
                for ic in range(NIC):
                    if ic == NIC - 1:
                        te.wait_ge(kb.spe.h, kb.spe.n)
                    last = te.matmul(pmm[:, slot, :],
                                     lhsT=warena[:, pr, (oin * 32 + ic) * 128:
                                                 (oin * 32 + ic) * 128 + 128],
                                     rhs=inbuf[:, ic, :], start=(ic == 0), stop=(ic == NIC - 1))
                np_ = kb.spe.inc(last)
                s.wait_ge(kb.spe.h, np_)
                s.activation(out=outbuf[:, oc, :], in_=pmm[:, slot, :], func=AF.Gelu,
                             scale=ssc[:, oc:oc + 1])
                mm_free[slot] = (kb.sac, kb.sac.n)
                if oin == 1:
                    slab_free[pr] = kb.spe.n
                    pump_slabs(pos + 3)
            stage_out_sac = kb.sac.n
            if stage == 2 and ti + 1 < NTILE:
                emit_xsection(ti + 1)

        for oc3 in range(8):
            sidx = oc3 // 2
            pos = base + S4O + sidx
            pr = pos % 2
            sr = slab_ready[pos]
            oin = oc3 % 2
            slot = oc3 % 4
            te.wait_ge(sr[0].h, sr[1])
            if oc3 == 0:
                te.wait_ge(kb.sac.h, stage_out_sac)
            if mm_free[slot] is not None:
                te.wait_ge(mm_free[slot][0].h, mm_free[slot][1])
            for ic in range(NIC):
                if ic == NIC - 1:
                    te.wait_ge(kb.spe.h, kb.spe.n)
                last = te.matmul(pmm[:, slot, :],
                                 lhsT=warena[:, pr, (oin * 32 + ic) * 128:
                                             (oin * 32 + ic) * 128 + 128],
                                 rhs=hbuf[:, ic, :], start=(ic == 0), stop=(ic == NIC - 1))
            np_ = kb.spe.inc(last)
            par = oc3 % 2
            s.wait_ge(kb.spe.h, np_)
            if ti == 0 and oc3 == 0:
                s.wait_ge(kb.sdv.h, s3_dv)
            if of_free[par] is not None:
                s.wait_ge(of_free[par][0].h, of_free[par][1])
            s.activation(out=of[:, par, :], in_=pmm[:, slot, :], func=AF.Identity,
                         scale=s3h[:, oc3:oc3 + 1])
            mm_free[slot] = (kb.sac, kb.sac.n)
            ofr = kb.sac.n
            for tj in range(NJ):
                pb2 = tj % 2
                te.wait_ge(kb.sdv.h, pot_free[pb2])
                te.wait_ge(kb.sac.h, ofr)
                te.wait_ge(kb.spe.h, kb.spe.n)
                last = te.transpose(out=pot[:, pb2, 0:128],
                                    in_=of[:, par, tj * 128:(tj + 1) * 128], identity=identf)
                np2 = kb.spe.inc(last)
                v.wait_ge(kb.spe.h, np2)
                v.tensor_tensor(out=xres[:, b, tj, oc3 * 128:(oc3 + 1) * 128],
                                in0=pot[:, pb2, 0:128],
                                in1=xres[:, b, tj, oc3 * 128:(oc3 + 1) * 128], op=ALU.add)
                pot_free[pb2] = kb.sdv.n
            of_free[par] = (kb.spe, kb.spe.n)
            if oin == 1:
                slab_free[pr] = kb.spe.n
                pump_slabs(pos + 3)

        nc.scalar.wait_ge(kb.sdv.h, kb.sdv.n)
        out_ap = dst[ti * TT:(ti + 1) * TT, :].rearrange("(j p) f -> p j f", p=128)
        stsem[b].inc(nc.scalar.dma_start(out=out_ap, in_=xres[:, b, :, :]), 16)
        store_done.append((stsem[b], stsem[b].n))

    kb.barrier()
    es.close()




def emit_l1attn(kb, src, dst, w, bof, c_mdiag, c_mold, c_ones1, ident, identf):
    nc = kb.nc
    v, s, g = kb.v, kb.s, kb.g
    te, sy = nc.tensor, nc.sync
    es = ExitStack()
    pfx = "at1" + kb.uid() + "_"
    sb = lambda n, sh, dt: kb.sb(es, pfx + n, sh, dt)
    psal = lambda n, sh, dt: kb.psum(es, pfx + n, sh, dt)

    NR = 6
    xres = sb("xres", [128, 2, D], F32)
    xhat = sb("xhat", [128, D], BF16)
    xhT = sb("xhT", [128, 2, 8, 128], BF16)
    qf = sb("qf", [128, 2, 8, 128], BF16)
    kfr = sb("kfr", [128, 8, NR, 128], BF16)
    vtr = sb("vtr", [128, NR, D], BF16)
    pbf = sb("pbf", [128, NH, 640], BF16)
    pnT = sb("pnT", [128, 2, 5, 128], BF16)
    yhT = sb("yhT", [128, 8, 128], BF16)
    mxh = sb("mxh", [128, NH], F32)
    smh = sb("smh", [128, NH], F32)
    rsh = sb("rsh", [128, NH], F32)
    wqk = sb("wqk", [128, 128, 128], BF16)
    wv_sb = sb("wv", [128, 8, D], BF16)
    wou = sb("wou", [128, 8, D], BF16)
    md = sb("md", [128, 128], F32)
    mo = sb("mo", [128, 128], F32)
    bor = sb("bor", [1, D], F32)
    onesr = sb("onesr", [1, 128], F32)
    bb = sb("bb", [128, D], F32)
    ln = LNScratch(kb, es, pfx + "ln", 1)

    pA = psal("pA", [128, 2, 1024], F32)
    pB = psal("pB", [128, 2, 512], F32)
    ptr = [psal("ptrA", [128, 1024], BF16), psal("ptrB", [128, 1024], BF16)]

    sy.wait_ge(kb.sw2.h, kb.sw2.n)
    sy.wait_ge(kb.swg.h, w["marker"])
    kb.sw2.inc(sy.dma_start(out=wqk[:], in_=w["qk"].rearrange("(a p o) -> p a o", p=128, a=128)), 16)
    kb.sw2.inc(sy.dma_start(out=wv_sb[:], in_=w["v"].rearrange("(fc p o) -> p fc o", p=128, fc=8)), 16)
    kb.sw2.inc(sy.dma_start(out=wou[:], in_=w["wo"].rearrange("(oc p o) -> p oc o", p=128, oc=8)), 16)
    kb.sw2.inc(sy.dma_start(out=md[:], in_=c_mdiag[:, :]), 16)
    kb.sw2.inc(sy.dma_start(out=mo[:], in_=c_mold[:, :]), 16)
    kb.sw2.inc(sy.dma_start(out=bor[:], in_=bof[:, :]), 16)
    kb.sw2.inc(sy.dma_start(out=onesr[:], in_=c_ones1[:, :]), 16)
    w_ready = kb.sw2.n
    te.wait_ge(kb.sw2.h, w_ready)
    for oh in range(2):
        te.wait_ge(kb.spe.h, kb.spe.n)
        last = te.matmul(pB[:, oh, :], lhsT=onesr[0:1, :], rhs=bor[0:1, oh * 512:(oh + 1) * 512],
                         start=True, stop=True)
    nbb = kb.spe.inc(last)
    v.wait_ge(kb.spe.h, nbb)
    v.tensor_copy(out=bb[:, 0:512], in_=pB[:, 0, :])
    v.tensor_copy(out=bb[:, 512:1024], in_=pB[:, 1, :])
    bb_dv = kb.sdv.n

    xsem = [kb.sx, kb.sxb]
    stsem = [kb.sst, kb.sstb]
    store_done = []
    x_ready = {}
    ptr_free = {0: None, 1: None}
    pa_free = {0: 0, 1: 0}
    pb_free = {0: (kb.sdv, bb_dv), 1: (kb.sdv, bb_dv)}
    kv_free = {}
    pnT_free = {0: None, 1: None}
    a_done = {}
    pbf_war = 0

    def emit_x_load(c):
        b = c % 2
        if c >= 2:
            sy.wait_ge(store_done[c - 2][0].h, store_done[c - 2][1])
        xsem[b].inc(sy.dma_start(out=xres[:, b, :],
                                 in_=src[c * 128:(c + 1) * 128, :]), 16)
        x_ready[c] = (xsem[b], xsem[b].n)

    def emit_proj(c):
        b = c % 2
        sl = c % NR
        v.wait_ge(x_ready[c][0].h, x_ready[c][1])
        s.wait_ge(x_ready[c][0].h, x_ready[c][1])
        emit_ln_stats(kb, ln, xres[:, b, :].unsqueeze(1), 1)
        s.wait_ge(kb.sdv.h, kb.sdv.n)
        s.wait_ge(kb.spe.h, kb.spe.n)
        s.activation(out=xhat[:], in_=xres[:, b, :], func=AF.Identity,
                     scale=ln.rstd[:, 0:1], bias=ln.bneg[:, 0:1])
        for grp in range(2):
            pb_ = grp
            if ptr_free[pb_] is not None:
                te.wait_ge(ptr_free[pb_][0].h, ptr_free[pb_][1])
            te.wait_ge(kb.sac.h, kb.sac.n)
            for k in range(4):
                fc = grp * 4 + k
                if k == 3:
                    te.wait_ge(kb.spe.h, kb.spe.n)
                last = te.transpose(out=ptr[pb_][:, k * 128:(k + 1) * 128],
                                    in_=xhat[:, fc * 128:(fc + 1) * 128], identity=ident)
            np_ = kb.spe.inc(last)
            s.wait_ge(kb.spe.h, np_)
            s.copy(out=xhT[:, b, grp * 4:(grp + 1) * 4, :], in_=ptr[pb_][:, 0:512])
            ptr_free[pb_] = (kb.sac, kb.sac.n)
        te.wait_ge(kb.sac.h, kb.sac.n)
        for oc in range(16):
            slot = oc % 2
            if pb_free[slot] is not None:
                te.wait_ge(pb_free[slot][0].h, pb_free[slot][1])
            if oc >= 8 and kv_free.get(sl) is not None:
                te.wait_ge(kv_free[sl][0].h, kv_free[sl][1])
            for fc in range(8):
                if fc == 7:
                    te.wait_ge(kb.spe.h, kb.spe.n)
                last = te.matmul(pB[:, slot, 0:128], lhsT=wqk[:, oc * 8 + fc, :],
                                 rhs=xhT[:, b, fc, :], start=(fc == 0), stop=(fc == 7))
            np_ = kb.spe.inc(last)
            v.wait_ge(kb.spe.h, np_)
            if oc < 8:
                v.tensor_scalar(out=qf[:, b, oc, :], in0=pB[:, slot, 0:128], scalar1=0.125,
                                scalar2=None, op0=ALU.mult)
            else:
                v.tensor_copy(out=kfr[:, oc - 8, sl, :], in_=pB[:, slot, 0:128])
            pb_free[slot] = (kb.sdv, kb.sdv.n)
        for oh in range(2):
            slot = oh
            if pb_free[slot] is not None:
                te.wait_ge(pb_free[slot][0].h, pb_free[slot][1])
            if kv_free.get(sl) is not None:
                te.wait_ge(kv_free[sl][0].h, kv_free[sl][1])
            for fc in range(8):
                if fc == 7:
                    te.wait_ge(kb.spe.h, kb.spe.n)
                last = te.matmul(pB[:, slot, :], lhsT=xhT[:, b, fc, :],
                                 rhs=wv_sb[:, fc, oh * 512:(oh + 1) * 512],
                                 start=(fc == 0), stop=(fc == 7))
            np_ = kb.spe.inc(last)
            v.wait_ge(kb.spe.h, np_)
            v.tensor_copy(out=vtr[:, sl, oh * 512:(oh + 1) * 512], in_=pB[:, slot, :])
            pb_free[slot] = (kb.sdv, kb.sdv.n)
        a_done[c] = kb.sdv.n

    emit_x_load(0)
    emit_x_load(1)
    emit_proj(0)

    for c in range(NCH):
        b = c % 2
        nsl = min(c, 4) + 1
        kcs = list(range(c - nsl + 1, c + 1))
        used = nsl * 128
        moff = (4 - c) * 128 if c < 4 else 0
        merged = []
        gi = 0
        i = 0
        while i < len(kcs):
            sl_ = kcs[i] % NR
            cnt = 1
            while (i + cnt < len(kcs) and (kcs[i + cnt] % NR) == sl_ + cnt and
                   (gi + cnt) % 4 != 0):
                cnt += 1
            merged.append((sl_, gi, cnt))
            gi += cnt
            i += cnt
        for h in range(NH):
            hc, par = h // 2, h % 2
            hr = 64 * par
            te.wait_ge(kb.sac.h, pa_free[par] if h >= 2 else max(pa_free[par], 0))
            if h < 2:
                te.wait_ge(kb.sdv.h, a_done[c])
            for ri, (sl_, gi0, cnt) in enumerate(merged):
                if ri == len(merged) - 1:
                    te.wait_ge(kb.spe.h, kb.spe.n)
                last = te.matmul(pA[:, par, gi0 * 128:(gi0 + cnt) * 128],
                                 lhsT=qf[hr:hr + 64, b, hc, :],
                                 rhs=kfr[hr:hr + 64, hc, sl_:sl_ + cnt, :],
                                 start=True, stop=True)
            nsc_ = kb.spe.inc(last)
            v.wait_ge(kb.spe.h, nsc_)
            v.tensor_tensor(out=pA[:, par, (nsl - 1) * 128:nsl * 128],
                            in0=pA[:, par, (nsl - 1) * 128:nsl * 128], in1=md[:, :],
                            op=ALU.add)
            if c >= 4:
                v.tensor_tensor(out=pA[:, par, 0:128], in0=pA[:, par, 0:128], in1=mo[:, :],
                                op=ALU.add)
            v.tensor_reduce(out=mxh[:, h:h + 1], in_=pA[:, par, 0:used], axis=AX.X, op=ALU.max)
            v.tensor_scalar(out=mxh[:, h:h + 1], in0=mxh[:, h:h + 1], scalar1=-1.0,
                            scalar2=None, op0=ALU.mult)
            s.wait_ge(kb.sdv.h, kb.sdv.n)
            if h < 2:
                s.wait_ge(kb.spe.h, pbf_war)
            s.activation(out=pbf[:, h, 0:used], in_=pA[:, par, 0:used], func=AF.Exp,
                         bias=mxh[:, h:h + 1], scale=1.0, accum_out=smh[:, h:h + 1])
            pa_free[par] = kb.sac.n
        exp_done = kb.sac.n
        v.wait_ge(kb.sac.h, exp_done)
        v.reciprocal(out=rsh[:, 0:NH], in_=smh[:, 0:NH])
        s.wait_ge(kb.sdv.h, kb.sdv.n)
        for h in range(NH):
            s.activation(out=pbf[:, h, 0:used], in_=pbf[:, h, 0:used], func=AF.Identity,
                         scale=rsh[:, h:h + 1])
        pn_done = ("ac", kb.sac.n)
        if c + 1 < NCH:
            emit_proj(c + 1)
        te.wait_ge(kb.sac.h, pn_done[1])
        pnt_dv = {}

        def emit_trans(h):
            pb_t = h % 2
            if ptr_free[pb_t] is not None:
                te.wait_ge(ptr_free[pb_t][0].h, ptr_free[pb_t][1])
            for gi in range(nsl):
                if gi == nsl - 1:
                    te.wait_ge(kb.spe.h, kb.spe.n)
                last = te.transpose(out=ptr[pb_t][:, gi * 128:(gi + 1) * 128],
                                    in_=pbf[:, h, gi * 128:(gi + 1) * 128], identity=ident)
            ntp = kb.spe.inc(last)
            v.wait_ge(kb.spe.h, ntp)
            if pnT_free[h % 2] is not None:
                v.wait_ge(pnT_free[h % 2][0].h, pnT_free[h % 2][1])
            v.tensor_copy(out=pnT[:, h % 2, 0:nsl, :], in_=ptr[pb_t][:, 0:used])
            ptr_free[pb_t] = (kb.sdv, kb.sdv.n)
            pnt_dv[h] = kb.sdv.n

        def emit_ctx(h):
            hc, par = h // 2, h % 2
            hr = 64 * par
            slot = par
            if pb_free[slot] is not None:
                te.wait_ge(pb_free[slot][0].h, pb_free[slot][1])
            te.wait_ge(kb.sdv.h, pnt_dv[h])
            for gi, kc in enumerate(kcs):
                if gi == nsl - 1:
                    te.wait_ge(kb.spe.h, kb.spe.n)
                last = te.matmul(pB[0:64, slot, 0:128], lhsT=vtr[:, kc % NR, 64 * h:64 * h + 64],
                                 rhs=pnT[:, h % 2, gi, :], start=(gi == 0), stop=(gi == nsl - 1))
            nctx = kb.spe.inc(last)
            pnT_free[h % 2] = (kb.spe, nctx)
            v.wait_ge(kb.spe.h, nctx)
            v.tensor_copy(out=yhT[hr:hr + 64, hc, :], in_=pB[0:64, slot, 0:128])
            pb_free[slot] = (kb.sdv, kb.sdv.n)

        for h in range(NH):
            emit_trans(h)
            if h > 0:
                emit_ctx(h - 1)
        emit_ctx(NH - 1)
        pbf_war = kb.spe.n
        kv_free[(c - 4) % NR if c >= 4 else (c + NR - 4) % NR] = (kb.spe, kb.spe.n)
        yh_dv = kb.sdv.n
        te.wait_ge(kb.sdv.h, yh_dv)
        if c == 0:
            v.wait_ge(kb.sdv.h, bb_dv)
        for oh in range(2):
            slot = oh
            if pb_free[slot] is not None:
                te.wait_ge(pb_free[slot][0].h, pb_free[slot][1])
            for oc in range(8):
                if oc == 7:
                    te.wait_ge(kb.spe.h, kb.spe.n)
                last = te.matmul(pB[:, slot, :], lhsT=yhT[:, oc, :],
                                 rhs=wou[:, oc, oh * 512:(oh + 1) * 512],
                                 start=(oc == 0), stop=(oc == 7))
            np_ = kb.spe.inc(last)
            v.wait_ge(kb.spe.h, np_)
            v.tensor_tensor(out=xres[:, b, oh * 512:(oh + 1) * 512], in0=pB[:, slot, :],
                            in1=xres[:, b, oh * 512:(oh + 1) * 512], op=ALU.add)
            v.tensor_tensor(out=xres[:, b, oh * 512:(oh + 1) * 512],
                            in0=xres[:, b, oh * 512:(oh + 1) * 512],
                            in1=bb[:, oh * 512:(oh + 1) * 512], op=ALU.add)
            pb_free[slot] = (kb.sdv, kb.sdv.n)
        nc.scalar.wait_ge(kb.sdv.h, kb.sdv.n)
        stsem[b].inc(nc.scalar.dma_start(out=dst[c * 128:(c + 1) * 128, :],
                                         in_=xres[:, b, :]), 16)
        store_done.append((stsem[b], stsem[b].n))
        if c + 2 < NCH:
            emit_x_load(c + 2)

    kb.barrier()
    es.close()



def emit_quant_ffn0_weights(kb, wg_nat, wu_nat, wd_nat, s_wg, s_wu, s_wd, ident):
    nc = kb.nc
    v, s, g = kb.v, kb.s, kb.g
    te, sy = nc.tensor, nc.sync
    es = ExitStack()
    pfx = "q0w" + kb.uid() + "_"
    sb = lambda n, sh, dt: kb.sb(es, pfx + n, sh, dt)
    psal = lambda n, sh, dt: kb.psum(es, pfx + n, sh, dt)

    wnat = sb("wnat", [128, 2, I_FF0], F32)
    qb = sb("qb", [128, 2, I_FF0], BF16)
    accg = sb("accg", [128, 8, 512], BF16)
    accd = sb("accd", [128, N_IC0, D], BF16)
    qsc = [QuantScratch(kb, es, pfx + "qA", I_FF0), QuantScratch(kb, es, pfx + "qB", I_FF0)]
    ptr = [psal("ptrA", [128, 1024], BF16), psal("ptrB", [128, 1024], BF16)]

    xsem = [kb.sx, kb.sxb]
    ptr_free = {0: None, 1: None}
    nat_free = {0: None, 1: None}
    acc_free = 0
    qb_spe = {0: 0, 1: 0}

    sy.wait_ge(kb.sw2.h, kb.sw2.n)

    def do_weight(nat, scratch, O, F, is_down):
        nonlocal acc_free
        nrc = O // 128
        nfc = F // 128

        def finish(rc):
            par = rc % 2
            v.wait_ge(kb.spe.h, qb_spe[par])
            dv = emit_quant_back(kb, qsc[par], qb[:, par, 0:F].unsqueeze(1), 1, F)
            ngrp = (nfc + 3) // 4
            for grp in range(ngrp):
                pb = grp % 2
                n_in = min(4, nfc - grp * 4)
                if ptr_free[pb] is not None:
                    te.wait_ge(ptr_free[pb][0].h, ptr_free[pb][1])
                if grp == 0:
                    te.wait_ge(kb.sdv.h, dv)
                for k in range(n_in):
                    fc = grp * 4 + k
                    if k == n_in - 1:
                        te.wait_ge(kb.spe.h, kb.spe.n)
                    last = te.transpose(out=ptr[pb][:, k * 128:(k + 1) * 128],
                                        in_=qb[:, par, fc * 128:(fc + 1) * 128], identity=ident)
                np_ = kb.spe.inc(last)
                s.wait_ge(kb.spe.h, np_)
                if is_down:
                    s.copy(out=accd[:, grp * 4:grp * 4 + n_in, rc * 128:(rc + 1) * 128],
                           in_=ptr[pb][:, 0:n_in * 128])
                else:
                    og = rc // 4
                    ow = (rc % 4) * 128
                    if grp == 0:
                        s.wait_ge(kb.sw2.h, acc_free)
                    s.copy(out=accg[:, grp * 4:grp * 4 + n_in, ow:ow + 128],
                           in_=ptr[pb][:, 0:n_in * 128])
                ptr_free[pb] = (kb.sac, kb.sac.n)
            qb_spe[par] = kb.spe.n

        def flush(rc):
            nonlocal acc_free
            if not is_down:
                og = rc // 4
                if (rc % 4 == 3) or (rc == nrc - 1):
                    szo = OG_SIZES[og]
                    off = 128 * 8 * OG_OFF[og]
                    nc.sync.wait_ge(kb.sac.h, kb.sac.n)
                    kb.sw2.inc(sy.dma_start(
                        out=scratch[off:off + 128 * 8 * szo].rearrange(
                            "(p fc o) -> p fc o", p=128, fc=8),
                        in_=accg[:, :, 0:szo]), 16)
                    acc_free = kb.sw2.n

        for rc in range(nrc):
            par = rc % 2
            if nat_free[par] is not None:
                sy.wait_ge(nat_free[par][0].h, nat_free[par][1])
            xsem[par].inc(sy.dma_start(
                out=wnat[:, par, 0:F],
                in_=nat[rc * 128 * F:(rc + 1) * 128 * F].rearrange("(p f) -> p f", p=128)), 16)
            ld = (xsem[par], xsem[par].n)
            v.wait_ge(ld[0].h, ld[1])
            s.wait_ge(ld[0].h, ld[1])
            emit_quant_front(kb, qsc[par], wnat[:, par, 0:F].unsqueeze(1), 1, F)
            nat_free[par] = (kb.sdv, kb.sdv.n)
            if rc > 0:
                finish(rc - 1)
                flush(rc - 1)
        finish(nrc - 1)
        flush(nrc - 1)
        if is_down:
            nc.sync.wait_ge(kb.sac.h, kb.sac.n)
            kb.sw2.inc(sy.dma_start(
                out=scratch[:].rearrange("(p ic o) -> p ic o", p=128, ic=N_IC0),
                in_=accd[:, :, :]), 16)

    do_weight(wg_nat, s_wg, I_FF0, D, False)
    do_weight(wu_nat, s_wu, I_FF0, D, False)
    do_weight(wd_nat, s_wd, D, I_FF0, True)
    qmarker = kb.sw2.n
    kb.barrier()
    es.close()
    return qmarker


def build_program(phases=("ff0a",)):
    kb = KB()
    nc = kb.nc
    es = kb.es

    x_in = nc.dram_tensor("x_in", [T, D], F32, kind="ExternalInput").ap()
    out = nc.dram_tensor("out", [T, D], F32, kind="ExternalOutput").ap()
    c_ident = nc.dram_tensor("c_ident", [128, 128], BF16, kind="ExternalInput").ap()
    c_identf = nc.dram_tensor("c_identf", [128, 128], F32, kind="ExternalInput").ap()

    n_gu = 128 * 8 * I_FF0
    n_wd = 128 * N_IC0 * D
    wg_nat = nc.dram_tensor("wg_nat", [I_FF0 * D], F32, kind="ExternalInput").ap()
    wu_nat = nc.dram_tensor("wu_nat", [I_FF0 * D], F32, kind="ExternalInput").ap()
    wd_nat = nc.dram_tensor("wd_nat", [D * I_FF0], F32, kind="ExternalInput").ap()
    s_wg = nc.dram_tensor("s_wg", [n_gu], BF16, kind="Internal").ap()
    s_wu = nc.dram_tensor("s_wu", [n_gu], BF16, kind="Internal").ap()
    s_wd = nc.dram_tensor("s_wd", [n_wd], BF16, kind="Internal").ap()

    xA = nc.dram_tensor("xA", [T, D], F32, kind="Internal").ap()
    xB = nc.dram_tensor("xB", [T, D], F32, kind="Internal").ap()

    wqf = nc.dram_tensor("wqf", [64 * 128 * 128], F32, kind="ExternalInput").ap()
    wkf = nc.dram_tensor("wkf", [64 * 128 * 128], F32, kind="ExternalInput").ap()
    wvf = nc.dram_tensor("wvf", [8 * 128 * D], F32, kind="ExternalInput").ap()
    wof = nc.dram_tensor("wof", [8 * 128 * D], F32, kind="ExternalInput").ap()
    gnwf = nc.dram_tensor("gnwf", [128, 8], F32, kind="ExternalInput").ap()
    gnbf = nc.dram_tensor("gnbf", [128, 8], F32, kind="ExternalInput").ap()
    c_dtc = nc.dram_tensor("c_dtc", [128, NH, 128], F32, kind="ExternalInput").ap()
    c_lamc = nc.dram_tensor("c_lamc", [128, 8, 128], F32, kind="ExternalInput").ap()
    c_muc = nc.dram_tensor("c_muc", [128, 8, 128], F32, kind="ExternalInput").ap()
    c_glcc = nc.dram_tensor("c_glcc", [128, 8], F32, kind="ExternalInput").ap()
    c_b2 = nc.dram_tensor("c_b2", [2, 128], F32, kind="ExternalInput").ap()
    c_ones2 = nc.dram_tensor("c_ones2", [128, 2], F32, kind="ExternalInput").ap()
    s_wq = nc.dram_tensor("s_wq", [64 * 128 * 128], BF16, kind="Internal").ap()
    s_wk = nc.dram_tensor("s_wk", [64 * 128 * 128], BF16, kind="Internal").ap()
    s_wv = nc.dram_tensor("s_wv", [8 * 128 * D], BF16, kind="Internal").ap()
    s_wo = nc.dram_tensor("s_wo", [8 * 128 * D], BF16, kind="Internal").ap()

    n_w12 = 32 * 8 * 128 * 128
    n_h12 = 32 * 32 * 128 * 128
    n_w3 = 8 * 32 * 128 * 128
    w1f = nc.dram_tensor("w1f", [n_w12], F32, kind="ExternalInput").ap()
    w2f = nc.dram_tensor("w2f", [n_w12], F32, kind="ExternalInput").ap()
    h1f = nc.dram_tensor("h1f", [n_h12], F32, kind="ExternalInput").ap()
    h2f = nc.dram_tensor("h2f", [n_h12], F32, kind="ExternalInput").ap()
    w3f = nc.dram_tensor("w3f", [n_w3], F32, kind="ExternalInput").ap()
    s1f = nc.dram_tensor("s1f", [128, 32], F32, kind="ExternalInput").ap()
    s2f = nc.dram_tensor("s2f", [128, 32], F32, kind="ExternalInput").ap()
    sh1f = nc.dram_tensor("sh1f", [128, 32], F32, kind="ExternalInput").ap()
    sh2f = nc.dram_tensor("sh2f", [128, 32], F32, kind="ExternalInput").ap()
    s3f = nc.dram_tensor("s3f", [128, 8], F32, kind="ExternalInput").ap()
    n_qk = 16 * 8 * 128 * 128
    n_qv = 8 * 128 * D
    qkf = nc.dram_tensor("qkf", [n_qk], F32, kind="ExternalInput").ap()
    qvf = nc.dram_tensor("qvf", [n_qv], F32, kind="ExternalInput").ap()
    wouf = nc.dram_tensor("wouf", [n_qv], F32, kind="ExternalInput").ap()
    bof = nc.dram_tensor("bof", [1, D], F32, kind="ExternalInput").ap()
    c_wmask = nc.dram_tensor("c_wmask", [128, 640], F32, kind="ExternalInput").ap()
    c_mdiag = nc.dram_tensor("c_mdiag", [128, 128], F32, kind="ExternalInput").ap()
    c_mold = nc.dram_tensor("c_mold", [128, 128], F32, kind="ExternalInput").ap()
    c_ones1 = nc.dram_tensor("c_ones1", [1, 128], F32, kind="ExternalInput").ap()
    s_qk = nc.dram_tensor("s_qk", [n_qk], BF16, kind="Internal").ap()
    s_qv = nc.dram_tensor("s_qv", [n_qv], BF16, kind="Internal").ap()
    s_wou = nc.dram_tensor("s_wou", [n_qv], BF16, kind="Internal").ap()

    s_w1 = nc.dram_tensor("s_w1", [n_w12], BF16, kind="Internal").ap()
    s_w2 = nc.dram_tensor("s_w2", [n_w12], BF16, kind="Internal").ap()
    s_h1 = nc.dram_tensor("s_h1", [n_h12], BF16, kind="Internal").ap()
    s_h2 = nc.dram_tensor("s_h2", [n_h12], BF16, kind="Internal").ap()
    s_w3 = nc.dram_tensor("s_w3", [n_w3], BF16, kind="Internal").ap()

    ident = kb.sb(es, "ident", [128, 128], BF16)
    identf = kb.sb(es, "identf", [128, 128], F32)
    kb._barr = kb.sb(es, "barr", [128, 4], F32)

    c_scal = nc.dram_tensor("c_scal", [128, len(ACT_CONSTS)], F32, kind="ExternalInput").ap()
    cpool = kb.sb(es, "cpool", [128, len(ACT_CONSTS)], F32)
    for ci, val in enumerate(ACT_CONSTS):
        nc.const_aps.register(F32, val, cpool[:, ci:ci + 1])

    g_ = nc.gpsimd
    kb.sw2.inc(nc.sync.dma_start(out=ident[:], in_=c_ident[:, :]), 16)
    kb.sw2.inc(nc.sync.dma_start(out=identf[:], in_=c_identf[:, :]), 16)
    kb.sw2.inc(nc.sync.dma_start(out=cpool[:], in_=c_scal[:, :]), 16)
    for eng in [nc.tensor, nc.vector, nc.scalar, nc.gpsimd]:
        eng.wait_ge(kb.sw2.h, kb.sw2.n)

    CONV = 1 << 21
    conv_groups = [
        ("at0", [(s_wq, wqf, 64 * 128 * 128), (s_wk, wkf, 64 * 128 * 128),
                 (s_wv, wvf, 8 * 128 * D), (s_wo, wof, 8 * 128 * D)]),
        ("at1", [(s_qk, qkf, n_qk), (s_qv, qvf, n_qv), (s_wou, wouf, n_qv)]),
        ("ff1", [(s_w1, w1f, n_w12), (s_w2, w2f, n_w12), (s_h1, h1f, n_h12),
                 (s_h2, h2f, n_h12), (s_w3, w3f, n_w3)]),
    ]
    conv_marker = {}
    for gname, items in conv_groups:
        g_.wait_ge(kb.swg.h, kb.swg.n)
        for dst_f, src_f, n in items:
            for off in range(0, n, CONV):
                sz = min(CONV, n - off)
                kb.swg.inc(g_.dma_start(out=dst_f[off:off + sz], in_=src_f[off:off + sz]), 16)
        conv_marker[gname] = kb.swg.n

    emit_quant_ffn0_weights(kb, wg_nat, wu_nat, wd_nat, s_wg, s_wu, s_wd, ident[:])

    chain = list(phases)
    n_ph = len(chain)
    for i, kind in enumerate(chain):
        src = x_in if i == 0 else (xA if i % 2 == 1 else xB)
        dst = out if i == n_ph - 1 else (xA if i % 2 == 0 else xB)
        if kind == "ff0":
            emit_l0ffn(kb, src, dst, s_wg, s_wu, s_wd, ident[:])
        elif kind == "at0":
            at0w = {"wq": s_wq, "wk": s_wk, "wv": s_wv, "wo": s_wo,
                    "marker": conv_marker["at0"]}
            emit_l0attn(kb, src, dst, at0w, gnwf, gnbf, c_dtc, c_lamc,
                        c_muc, c_glcc, c_b2, c_ones2, ident[:], identf[:])
        elif kind == "ff1":
            ff1w = {"w1": s_w1, "w2": s_w2, "h1": s_h1, "h2": s_h2, "w3": s_w3,
                    "marker": conv_marker["ff1"]}
            emit_l1ffn(kb, src, dst, ff1w, s1f, s2f, sh1f, sh2f, s3f, ident[:], identf[:])
        elif kind == "at1":
            at1w = {"qk": s_qk, "v": s_qv, "wo": s_wou, "marker": conv_marker["at1"]}
            emit_l1attn(kb, src, dst, at1w, bof, c_mdiag, c_mold, c_ones1, ident[:], identf[:])

    return kb


_PREP_CACHE = {}


def prep_shared(inputs):
    c = host_consts()
    d = {
        "c_ident": c["c_ident"],
        "c_identf": np.eye(128, dtype=np.float32),
        "c_scal": np.tile(np.array(ACT_CONSTS, dtype=np.float32)[None, :], (128, 1)),
        "wg_nat": np.ascontiguousarray(inputs["L0_ffn_w_gate"]).ravel(),
        "wu_nat": np.ascontiguousarray(inputs["L0_ffn_w_up"]).ravel(),
        "wd_nat": np.ascontiguousarray(inputs["L0_ffn_w_down"]).ravel(),
        "wqf": tiles_oc_fc(inputs["L0_attn_wq"]),
        "wkf": tiles_oc_fc(inputs["L0_attn_wk"]),
        "wvf": flat_pmajor_fcwide(inputs["L0_attn_wv"]),
        "wof": flat_oc_pmajor_wide(inputs["L0_attn_wo"]),
        "gnwf": np.ascontiguousarray(inputs["L0_attn_gn_w"].reshape(8, 128).T),
        "gnbf": np.ascontiguousarray(inputs["L0_attn_gn_b"].reshape(8, 128).T),
        "c_dtc": np.ascontiguousarray(c["c_dt"].transpose(1, 0, 2)),
        "c_lamc": np.ascontiguousarray(c["c_lam"].transpose(1, 0, 2)),
        "c_muc": np.ascontiguousarray(c["c_mu"].transpose(1, 0, 2)),
        "c_glcc": c["c_glc"],
        "c_b2": c["c_b2"],
        "c_ones2": c["c_ones2"],
        "w1f": tiles_lhsT_slabbed(inputs["L1_ffn_w1_w"], 4),
        "w2f": tiles_lhsT_slabbed(inputs["L1_ffn_w2_w"], 4),
        "h1f": tiles_lhsT_slabbed(inputs["L1_ffn_h1_w"], 2),
        "h2f": tiles_lhsT_slabbed(inputs["L1_ffn_h2_w"], 2),
        "w3f": tiles_lhsT_slabbed(inputs["L1_ffn_w3_w"], 2),
        "s1f": np.ascontiguousarray(inputs["L1_ffn_w1_s"].reshape(32, 128).T),
        "s2f": np.ascontiguousarray(inputs["L1_ffn_w2_s"].reshape(32, 128).T),
        "sh1f": np.ascontiguousarray(inputs["L1_ffn_h1_s"].reshape(32, 128).T),
        "sh2f": np.ascontiguousarray(inputs["L1_ffn_h2_s"].reshape(32, 128).T),
        "s3f": np.ascontiguousarray(inputs["L1_ffn_w3_s"].reshape(8, 128).T),
        "qkf": tiles_oc_fc(inputs["L1_attn_w_qkv"][0:2048]),
        "qvf": flat_pmajor_fcwide(inputs["L1_attn_w_qkv"][2048:3072]),
        "wouf": flat_pmajor_fcwide(inputs["L1_attn_w_out"]),
        "bof": inputs["L1_attn_b_out"].reshape(1, D),
        "c_wmask": c["c_wmask"],
        "c_mdiag": c["c_mdiag"],
        "c_mold": c["c_mold"],
        "c_ones1": c["c_ones1"],
    }
    return d


def prep_inputs_core(inputs, core, shared=None):
    if shared is None:
        shared = prep_shared(inputs)
    d = dict(shared)
    d["x_in"] = np.ascontiguousarray(inputs["L0_attn_x"][core])
    return d


def kernel(**inputs):
    kb = build_program(phases=("ff0", "at0", "ff0", "ff1", "at1", "ff1"))
    shared = prep_shared(inputs)
    in_maps = [prep_inputs_core(inputs, c, shared) for c in range(NCORE)]
    res = run_bass_kernel_spmd(kb.nc, in_maps, core_ids=list(range(NCORE)))
    return np.stack([res.results[c]["out"] for c in range(NCORE)], axis=0)
